# version 06 (final)
# baseline (speedup 1.0000x reference)
import contextlib
import math
import numpy as np
import ml_dtypes

import concourse.bass as bass
import concourse.mybir as mybir
from concourse.bass_utils import run_bass_kernel_spmd

F32 = mybir.dt.float32
BF16 = mybir.dt.bfloat16
I32 = mybir.dt.int32
AF = mybir.ActivationFunctionType
ALU = mybir.AluOpType
AX = mybir.AxisListType
bfdt = ml_dtypes.bfloat16

NCORES = 8
T = 2048
D = 4096
NT = T // 128
DC = D // 128
S = 4096
NEGBIG = -3.0e38
LN_EPS = 1e-5
SCALE0 = 1.0 / 128.0
SCALE1 = 128.0 ** -0.5
MST_W = 6016
MST_OFF = 3968

REPLICA_PAIRS = [[0, 1], [2, 3], [4, 5], [6, 7]]


def _alibi_slopes(n_heads):
    cp2 = 2 ** math.floor(math.log2(n_heads))
    base = 2 ** (-(2 ** (-(math.log2(cp2) - 3))))
    slopes = base ** np.arange(1, cp2 + 1, dtype=np.float32)
    if cp2 != n_heads:
        extra_base = 2 ** (-(2 ** (-(math.log2(2 * cp2) - 3))))
        n_rem = min(cp2, n_heads - cp2)
        extra = extra_base ** np.arange(1, 1 + 2 * n_rem, 2, dtype=np.float32)
        slopes = np.concatenate([slopes, extra])
    return slopes.astype(np.float32)


SEM_ALIAS = (("a1", "a0"), ("pk", "pq"), ("py", "pv"), ("bcl2", "bcl"),
             ("xdq", "podq"), ("sgol", "sghl"), ("s8l", "bcl"))


def _sem_alias(name):
    for a, b in SEM_ALIAS:
        if name.startswith(a):
            return b + name[len(a):]
    return name


class Prog:
    def __init__(self, nc):
        self.nc = nc
        self._sems = {}
        self.ctr = {}

    def sem(self, name):
        name = _sem_alias(name)
        if name not in self._sems:
            self._sems[name] = self.nc.alloc_semaphore(name)
            self.ctr[name] = 0
        return self._sems[name]

    def inc(self, name, n=1):
        name = _sem_alias(name)
        self.sem(name)
        self.ctr[name] += n
        return self.ctr[name]

    def val(self, name):
        name = _sem_alias(name)
        self.sem(name)
        return self.ctr[name]

    def ss(self, eng, ins, name="zss"):
        ins.then_inc(self.sem(name), 1)
        eng.wait_ge(self.sem(name), self.inc(name))
        return ins


class NcPfx:
    def __init__(self, nc, pfx):
        self._nc = nc
        self._pfx = pfx

    def sbuf_tensor(self, name, *a, **k):
        return self._nc.sbuf_tensor(self._pfx + name, *a, **k)

    def psum_tensor(self, name, *a, **k):
        return self._nc.psum_tensor(self._pfx + name, *a, **k)

    def dram_tensor(self, name, *a, **k):
        return self._nc.dram_tensor(self._pfx + name, *a, **k)

    def __getattr__(self, n):
        return getattr(self._nc, n)


def _mk_io(nc, stack):
    io = {}
    io["ident"] = stack.enter_context(nc.sbuf_tensor("ident", [128, 128], BF16))
    io["ones2"] = stack.enter_context(nc.sbuf_tensor("ones2", [2, 128], BF16))
    io["onescol"] = stack.enter_context(nc.sbuf_tensor("onescol", [128, 1], BF16))
    io["iot"] = stack.enter_context(nc.sbuf_tensor("iot", [128, 128], F32))
    io["eps"] = stack.enter_context(nc.sbuf_tensor("epsc", [128, 1], F32))
    io["eps12"] = stack.enter_context(nc.sbuf_tensor("eps12", [128, 1], F32))
    io["orow"] = stack.enter_context(nc.sbuf_tensor("orow", [1, 512], BF16))
    io["fcell"] = stack.enter_context(nc.sbuf_tensor("fcell", [1, 4], F32))
    return io


def pe_fence(p, t, io, trash, sname=None):
    ins = t.matmul(trash[:, 0:192], io["orow"][0:1, 0:128],
                   io["orow"][0:1, 0:192], start=True, stop=True)
    if sname is not None:
        ins.then_inc(p.sem(sname), 1)
        p.inc(sname)
    return ins


def dma(eng, p, sname, dst, src):
    eng.dma_start(dst, src).then_inc(p.sem(sname), 16)
    p.inc(sname, 16)


def load_w_colstrip(g, p, sname, dst3, w_hbm, col0, ncols, nrows=D, rsplit=4):
    nch = nrows // 128
    per = nch // rsplit
    src = w_hbm[:, col0:col0 + ncols].rearrange("(c p) f -> p c f", p=128)
    for r in range(rsplit):
        dma(g, p, sname, dst3[:, r * per:(r + 1) * per, :],
            src[:, r * per:(r + 1) * per, :])



def emit_init_consts(p, io):
    nc = p.nc
    with nc.Block() as blk:
        @blk.gpsimd
        def _(g):
            g.iota(io["iot"][:], [[1, 128]], base=0, channel_multiplier=-1,
                   allow_small_or_imprecise_dtypes=True)
            g.memset(io["ones2"][:], 1.0)
            g.memset(io["onescol"][:], 1.0)
            g.memset(io["eps"][:], LN_EPS)
            g.memset(io["eps12"][:], 1e-12)
            g.memset(io["orow"][:], 1.0)
    with nc.Block() as blk:
        @blk.vector
        def _(v):
            v.tensor_scalar(io["ident"][:], io["iot"][:], 0.0, 1.0,
                            ALU.is_equal, ALU.mult)


def emit_ln_xnt(p, io, x_src, xnT, tag, sig_cb=None):
    nc = p.nc
    ld, sc1, v2, sc2, v3, tr, ev = (tag + s for s in
                                    ("ld", "sc1", "v2", "sc2", "v3", "tr", "ev"))
    with contextlib.ExitStack() as st:
        xst = st.enter_context(nc.sbuf_tensor(tag + "x", [128, D], F32))
        xsq = st.enter_context(nc.sbuf_tensor(tag + "sq", [128, D], BF16))
        xnb = st.enter_context(nc.sbuf_tensor(tag + "xn", [128, D], BF16))
        s1 = st.enter_context(nc.sbuf_tensor(tag + "s1", [128, 1], F32))
        s2 = st.enter_context(nc.sbuf_tensor(tag + "s2", [128, 1], F32))
        mu = st.enter_context(nc.sbuf_tensor(tag + "mu", [128, 1], F32))
        msq = st.enter_context(nc.sbuf_tensor(tag + "msq", [128, 1], F32))
        var = st.enter_context(nc.sbuf_tensor(tag + "var", [128, 1], F32))
        sd = st.enter_context(nc.sbuf_tensor(tag + "sd", [128, 1], F32))
        rsd = st.enter_context(nc.sbuf_tensor(tag + "rsd", [128, 1], F32))
        ps0 = st.enter_context(nc.psum_tensor(tag + "p0", [128, 128], BF16))
        ps1 = st.enter_context(nc.psum_tensor(tag + "p1", [128, 128], BF16))
        pst = [ps0, ps1]
        for tt in range(NT):
            with nc.Block() as blk:
                @blk.sync
                def _(sy):
                    dma(sy, p, ld, xst[:], x_src[tt * 128:(tt + 1) * 128, :])

                @blk.scalar
                def _(sc):
                    sc.wait_ge(p.sem(ld), p.val(ld))
                    sc.activation(xsq[:], xst[:], AF.Square,
                                  accum_out=s2[:]).then_inc(p.sem(sc1), 1)
                    p.inc(sc1)

                @blk.vector
                def _(v):
                    v.wait_ge(p.sem(ld), p.val(ld))
                    p.ss(v, v.reduce_sum(s1[:], xst[:], axis=AX.X))
                    p.ss(v, v.tensor_scalar(mu[:], s1[:], 1.0 / D, None, ALU.mult))
                    p.ss(v, v.tensor_mul(msq[:], mu[:], mu[:]))
                    v.wait_ge(p.sem(sc1), p.val(sc1))
                    p.ss(v, v.tensor_scalar(var[:], s2[:], 1.0 / D, None, ALU.mult))
                    v.tensor_sub(var[:], var[:], msq[:]).then_inc(p.sem(v2), 1)
                    p.inc(v2)

                @blk.scalar
                def _(sc):
                    sc.wait_ge(p.sem(v2), p.val(v2))
                    sc.activation(sd[:], var[:], AF.Sqrt,
                                  bias=io["eps"][:]).then_inc(p.sem(sc2), 1)
                    p.inc(sc2)

                @blk.vector
                def _(v):
                    v.wait_ge(p.sem(sc2), p.val(sc2))
                    p.ss(v, v.reciprocal(rsd[:], sd[:]))
                    ins = v.tensor_scalar(xnb[:], xst[:], mu[:], rsd[:],
                                          ALU.subtract, ALU.mult)
                    if sig_cb is None:
                        ins.then_inc(p.sem(v3), 1)
                        p.inc(v3)
                    else:
                        p.ss(v, ins)
                        sig_cb(v, tt, xnb)
                        v.tensor_copy(io["fcell"][0:1, 0:1],
                                      io["eps"][0:1, :]).then_inc(p.sem(v3), 1)
                        p.inc(v3)

                base_tr = p.val(tr)
                base_ev = p.val(ev)

                @blk.tensor
                def _(t):
                    t.wait_ge(p.sem(v3), p.val(v3))
                    for c in range(DC):
                        if c >= 2:
                            t.wait_ge(p.sem(ev), base_ev + c - 1)
                        t.transpose(pst[c % 2][:], xnb[:, c * 128:(c + 1) * 128],
                                    io["ident"][:]).then_inc(p.sem(tr), 1)
                        p.inc(tr)
                    t.wait_ge(p.sem(ev), base_ev + DC - 1)
                    t.transpose(pst[0][:], xnb[:, 0:128],
                                io["ident"][:]).then_inc(p.sem(tr), 1)
                    p.inc(tr)

                @blk.vector
                def _(v):
                    for c in range(DC):
                        v.wait_ge(p.sem(tr), base_tr + c + 2)
                        col = c * T + tt * 128
                        v.tensor_copy(xnT[:, col:col + 128],
                                      pst[c % 2][:]).then_inc(p.sem(ev), 1)
                        p.inc(ev)


def emit_featmaj_proj(p, io, xnT, w_hbm, wcol0, nfc, bias_cols, bias_off,
                      dst_hbm, tag, dst_row0=0):
    nc = p.nc
    wl, mm, evq, stq = (tag + s for s in ("wl", "mm", "ev", "st"))
    with contextlib.ExitStack() as st:
        wbuf = [st.enter_context(nc.sbuf_tensor(f"{tag}w{i}", [128, DC, 128], BF16))
                for i in range(2)]
        evb = [st.enter_context(nc.sbuf_tensor(f"{tag}e{i}", [128, T], BF16))
               for i in range(2)]
        psq = [st.enter_context(nc.psum_tensor(f"{tag}p{i}", [128, 512], F32))
               for i in range(4)]
        trash = st.enter_context(nc.psum_tensor(tag + "tr", [128, 512], F32))
        for fc in range(nfc):
            par = fc % 2
            with nc.Block() as blk:
                @blk.gpsimd
                def _(g):
                    if fc == 0:
                        load_w_colstrip(g, p, wl + "0", wbuf[0], w_hbm, wcol0, 128)
                    if fc + 1 < nfc:
                        load_w_colstrip(g, p, wl + str(1 - par), wbuf[1 - par],
                                        w_hbm, wcol0 + (fc + 1) * 128, 128)

                @blk.tensor
                def _(t):
                    t.wait_ge(p.sem(wl + str(par)), p.val(wl + str(par)) -
                              (64 if (fc + 1 < nfc and (1 - par) == par) else 0))
                    for c in range(DC):
                        for ts in range(4):
                            t.matmul(psq[ts][:], wbuf[par][:, c, :],
                                     xnT[:, c * T + ts * 512: c * T + ts * 512 + 512],
                                     start=(c == 0), stop=(c == DC - 1))
                    pe_fence(p, t, io, trash, mm)

                @blk.vector
                def _(v):
                    v.wait_ge(p.sem(mm), p.val(mm))
                    for ts in range(4):
                        if bias_cols is not None:
                            bi = bias_off + fc
                            ins = v.tensor_scalar(evb[par][:, ts * 512:(ts + 1) * 512],
                                                  psq[ts][:],
                                                  bias_cols[:, bi:bi + 1], None,
                                                  ALU.add)
                        else:
                            ins = v.tensor_copy(evb[par][:, ts * 512:(ts + 1) * 512],
                                                psq[ts][:])
                    ins.then_inc(p.sem(evq), 1)
                    p.inc(evq)

                @blk.sync
                def _(sy):
                    sy.wait_ge(p.sem(evq), p.val(evq))
                    r0 = dst_row0 + fc * 128
                    dma(sy, p, stq, dst_hbm[r0:r0 + 128, :], evb[par][:])
                    sy.wait_ge(p.sem(stq), p.val(stq))


def emit_tokmaj_proj(p, io, xnT, w_hbm, wrow_total, wcol0, nfs, bias_bcast,
                     dst_hbm, tag, extra_mm=None, wcast=True):
    nc = p.nc
    wl, mm, evq, stq = (tag + s for s in ("wl", "mm", "ev", "st"))
    with contextlib.ExitStack() as st:
        wmov = st.enter_context(nc.sbuf_tensor(tag + "w", [128, DC, 512], BF16))
        evb = [st.enter_context(nc.sbuf_tensor(f"{tag}e{i}", [128, 512], BF16))
               for i in range(2)]
        psq = [st.enter_context(nc.psum_tensor(f"{tag}p{i}", [128, 512], F32))
               for i in range(2)]
        trash = st.enter_context(nc.psum_tensor(tag + "tr", [128, 512], F32))
        for fs in range(nfs):
            with nc.Block() as blk:
                @blk.gpsimd
                def _(g):
                    load_w_colstrip(g, p, wl, wmov, w_hbm, wcol0 + fs * 512, 512)

                base_mm = p.val(mm)
                base_ev = p.val(evq)
                base_st = [p.val(stq + "0"), p.val(stq + "1")]

                @blk.tensor
                def _(t):
                    t.wait_ge(p.sem(wl), p.val(wl))
                    for tt in range(NT):
                        if tt >= 2:
                            t.wait_ge(p.sem(evq), base_ev + tt - 1)
                        ps = psq[tt % 2]
                        for c in range(DC):
                            t.matmul(ps[:], xnT[:, c * T + tt * 128: c * T + tt * 128 + 128],
                                     wmov[:, c, :],
                                     start=(c == 0),
                                     stop=(c == DC - 1 and extra_mm is None))
                        if extra_mm is not None:
                            extra_mm(t, tt, fs, ps)
                        pe_fence(p, t, io, trash, mm)

                @blk.vector
                def _(v):
                    for tt in range(NT):
                        v.wait_ge(p.sem(mm), base_mm + tt + 1)
                        if tt >= 2:
                            v.wait_ge(p.sem(stq + str(tt % 2)),
                                      base_st[tt % 2] + 16 * (tt // 2))
                        ps = psq[tt % 2]
                        eb = evb[tt % 2]
                        if bias_bcast is not None:
                            ins = v.tensor_add(eb[:], ps[:],
                                               bias_bcast[:, fs * 512:(fs + 1) * 512])
                        else:
                            ins = v.tensor_copy(eb[:], ps[:])
                        ins.then_inc(p.sem(evq), 1)
                        p.inc(evq)

                @blk.sync
                def _(sy):
                    for tt in range(NT):
                        sy.wait_ge(p.sem(evq), base_ev + tt + 1)
                        dma(sy, p, stq + str(tt % 2),
                            dst_hbm[tt * 128:(tt + 1) * 128,
                                    fs * 512:(fs + 1) * 512],
                            evb[tt % 2][:])
                    sy.wait_ge(p.sem(stq + "0"), p.val(stq + "0"))
                    sy.wait_ge(p.sem(stq + "1"), p.val(stq + "1"))


def emit_lora_prep(p, io, st, xnT, a4t, w4t, bblk, rb4, prior4, expmat):
    nc = p.nc
    a4t_sb = st.enter_context(nc.sbuf_tensor("lra", [128, DC, 32], BF16))
    w4t_sb = st.enter_context(nc.sbuf_tensor("lrw4", [128, DC, 4], BF16))
    bblk_sb = st.enter_context(nc.sbuf_tensor("lrb", [32, D], BF16))
    exp_sb = st.enter_context(nc.sbuf_tensor("lrexp", [4, 32], BF16))
    pri_sb = st.enter_context(nc.sbuf_tensor("lrpri", [1, 4], F32))
    rb_sb = st.enter_context(nc.sbuf_tensor("lrrb", [1, 4], F32))
    lp_sb = st.enter_context(nc.sbuf_tensor("lrlp", [1, 4], F32))
    lp_bf = st.enter_context(nc.sbuf_tensor("lrlpb", [1, 4], BF16))
    lp2b = st.enter_context(nc.sbuf_tensor("lrlp2", [128, 4], F32))
    zt_sb = st.enter_context(nc.sbuf_tensor("lrzt", [32, T], BF16))
    lg_sb = st.enter_context(nc.sbuf_tensor("lrlg", [128, NT * 4], F32))
    la_sb = st.enter_context(nc.sbuf_tensor("lrla", [128, 4], F32))
    mx_sb = st.enter_context(nc.sbuf_tensor("lrmx", [128, 1], F32))
    ssum = st.enter_context(nc.sbuf_tensor("lrss", [128, 1], F32))
    rs_sb = st.enter_context(nc.sbuf_tensor("lrrs", [128, 1], F32))
    al_sb = st.enter_context(nc.sbuf_tensor("lral", [128, NT * 4], BF16))
    alT = st.enter_context(nc.sbuf_tensor("lralT", [4, T], BF16))
    zh_sb = st.enter_context(nc.sbuf_tensor("lrzh", [32, T], BF16))

    with nc.Block() as blk:
        @blk.gpsimd
        def _(g):
            load_w_colstrip(g, p, "lrl", a4t_sb, a4t, 0, 32)
            load_w_colstrip(g, p, "lrl", w4t_sb, w4t, 0, 4)
            dma(g, p, "lrl", bblk_sb[:], bblk)
            dma(g, p, "lrl", exp_sb[:], expmat)

        @blk.sync
        def _(sy):
            dma(sy, p, "lrl", pri_sb[:], prior4)
            dma(sy, p, "lrl", rb_sb[:], rb4)

    with contextlib.ExitStack() as psz_st:
        psz = psz_st.enter_context(nc.psum_tensor("lrpz", [32, T], F32))
        trz = psz_st.enter_context(nc.psum_tensor("lrtz", [128, 512], F32))
        with nc.Block() as blk:
            @blk.tensor
            def _(t):
                t.wait_ge(p.sem("lrl"), p.val("lrl"))
                for c in range(DC):
                    for ts in range(4):
                        t.matmul(psz[:, ts * 512:(ts + 1) * 512],
                                 a4t_sb[:, c, :],
                                 xnT[:, c * T + ts * 512: c * T + ts * 512 + 512],
                                 start=(c == 0), stop=(c == DC - 1))
                pe_fence(p, t, io, trz, "lrz")

            @blk.vector
            def _(v):
                v.wait_ge(p.sem("lrz"), p.val("lrz"))
                v.tensor_copy(zt_sb[:], psz[:])

    with contextlib.ExitStack() as ps_st:
        pslg = [ps_st.enter_context(nc.psum_tensor(f"lrpl{i}", [128, 4], F32))
                for i in range(2)]
        psb = ps_st.enter_context(nc.psum_tensor("lrpb", [128, 4], F32))
        pstr = [ps_st.enter_context(nc.psum_tensor(f"lrpt{i}", [4, 128], BF16))
                for i in range(2)]
        psax = ps_st.enter_context(nc.psum_tensor("lrpax", [32, 512], F32))
        trb = ps_st.enter_context(nc.psum_tensor("lrtb", [128, 512], F32))

        base_ev = p.val("lrgev")
        with nc.Block() as blk:
            @blk.tensor
            def _(t):
                for tt in range(NT):
                    if tt >= 2:
                        t.wait_ge(p.sem("lrgev"), base_ev + tt - 1)
                    for c in range(DC):
                        t.matmul(pslg[tt % 2][:],
                                 xnT[:, c * T + tt * 128: c * T + tt * 128 + 128],
                                 w4t_sb[:, c, :],
                                 start=(c == 0), stop=(c == DC - 1))
                    pe_fence(p, t, io, trb, "lrgmm")

            @blk.vector
            def _(v):
                for tt in range(NT):
                    v.wait_ge(p.sem("lrgmm"), base_ev + tt + 1)
                    v.tensor_copy(lg_sb[:, tt * 4:(tt + 1) * 4],
                                  pslg[tt % 2][:]).then_inc(p.sem("lrgev"), 1)
                    p.inc("lrgev")

        with nc.Block() as blk:
            @blk.scalar
            def _(sc):
                sc.activation(lp_sb[:], pri_sb[:], AF.Ln, bias=io["eps12"][0:1, :])
        with nc.Block() as blk:
            @blk.vector
            def _(v):
                v.tensor_add(lp_sb[:], lp_sb[:], rb_sb[:])
                v.tensor_copy(lp_bf[:], lp_sb[:])
        with nc.Block() as blk:
            @blk.tensor
            def _(t):
                t.matmul(psb[:], io["ones2"][0:1, :], lp_bf[:],
                         start=True, stop=True)
                pe_fence(p, t, io, trb)
        with nc.Block() as blk:
            @blk.vector
            def _(v):
                v.tensor_copy(lp2b[:], psb[:])

        for tt in range(NT):
            with nc.Block() as blk:
                @blk.vector
                def _(v):
                    p.ss(v, v.tensor_add(la_sb[:], lg_sb[:, tt * 4:(tt + 1) * 4], lp2b[:]))
                    p.ss(v, v.reduce_max(mx_sb[:], la_sb[:], axis=AX.X))
                    v.tensor_scalar(mx_sb[:], mx_sb[:], -1.0, None,
                                    ALU.mult).then_inc(p.sem("lrs1"), 1)
                    p.inc("lrs1")

                @blk.scalar
                def _(sc):
                    sc.wait_ge(p.sem("lrs1"), p.val("lrs1"))
                    sc.activation(la_sb[:], la_sb[:], AF.Exp, bias=mx_sb[:],
                                  accum_out=ssum[:]).then_inc(p.sem("lrs2"), 1)
                    p.inc("lrs2")

                @blk.vector
                def _(v):
                    v.wait_ge(p.sem("lrs2"), p.val("lrs2"))
                    p.ss(v, v.reciprocal(rs_sb[:], ssum[:]))
                    v.tensor_scalar(al_sb[:, tt * 4:(tt + 1) * 4], la_sb[:],
                                    rs_sb[:], None, ALU.mult).then_inc(p.sem("lrs3"), 1)
                    p.inc("lrs3")

                @blk.tensor
                def _(t):
                    t.wait_ge(p.sem("lrs3"), p.val("lrs3"))
                    t.transpose(pstr[tt % 2][:], al_sb[:, tt * 4:(tt + 1) * 4],
                                io["ident"][:])
                    pe_fence(p, t, io, trb, "lrs4")

                @blk.vector
                def _(v):
                    v.wait_ge(p.sem("lrs4"), p.val("lrs4"))
                    v.tensor_copy(alT[:, tt * 128:(tt + 1) * 128], pstr[tt % 2][:])

        for ts in range(4):
            with nc.Block() as blk:
                @blk.tensor
                def _(t):
                    t.matmul(psax[:], exp_sb[:], alT[:, ts * 512:(ts + 1) * 512],
                             start=True, stop=True)
                    pe_fence(p, t, io, trb, "lrx")

                @blk.vector
                def _(v):
                    v.wait_ge(p.sem("lrx"), p.val("lrx"))
                    v.scalar_tensor_tensor(zh_sb[:, ts * 512:(ts + 1) * 512],
                                           zt_sb[:, ts * 512:(ts + 1) * 512],
                                           2.0, psax[:], ALU.mult, ALU.mult)
    return zh_sb, bblk_sb


def emit_attention(p, io, cfg, qt_hbm, agk_out, agv_out, attn_dst, masterT_sb,
                   alibi_cols=None, sqrows=None, sgoT=None):
    nc = p.nc
    tg = cfg["tag"]
    n_qh, kv_per, dkv, scale = cfg["n_qh"], cfg["kv_per"], cfg["dkv"], cfg["scale"]
    with contextlib.ExitStack() as st:
        qt_h = st.enter_context(nc.sbuf_tensor(tg + "qt", [128, T], BF16))
        kt_sb = st.enter_context(nc.sbuf_tensor(tg + "kt", [128, S], BF16))
        v_sb = st.enter_context(nc.sbuf_tensor(tg + "v", [128, 32, 128], BF16))
        if cfg["alibi"]:
            qsq = st.enter_context(nc.sbuf_tensor(tg + "qsq", [128, T], BF16))
            ksq = st.enter_context(nc.sbuf_tensor(tg + "ksq", [128, S], BF16))
            qn2 = st.enter_context(nc.sbuf_tensor(tg + "qn2", [1, T], F32))
            kn2 = st.enter_context(nc.sbuf_tensor(tg + "kn2", [1, S], F32))
            qmx = st.enter_context(nc.sbuf_tensor(tg + "qmx", [1, 4], F32))
            kmx = st.enter_context(nc.sbuf_tensor(tg + "kmx", [1, 1], F32))
            mrow = st.enter_context(nc.sbuf_tensor(tg + "mrow", [1, 4], F32))
            mrow_bf = st.enter_context(nc.sbuf_tensor(tg + "mrbf", [1, 4], BF16))
            mcols = st.enter_context(nc.sbuf_tensor(tg + "mcols", [128, 4], F32))
            biasm = st.enter_context(nc.sbuf_tensor(tg + "biasm", [128, 128], F32))
            ac_h = st.enter_context(nc.sbuf_tensor(tg + "ach", [128, 32], F32))
            sq_h = st.enter_context(nc.sbuf_tensor(tg + "sqh", [2, T], BF16))
            msk = [st.enter_context(nc.sbuf_tensor(tg + f"msk{i}", [128, 512], F32))
                   for i in range(2)]
        pT = st.enter_context(nc.sbuf_tensor(tg + "pT", [128, 32 * 512], BF16))
        linv = st.enter_context(nc.sbuf_tensor(tg + "linv", [1, 512], F32))
        linv_bf = st.enter_context(nc.sbuf_tensor(tg + "linb", [1, 512], BF16))
        r_sb = st.enter_context(nc.sbuf_tensor(tg + "rsb", [128, 512], BF16))
        aev = st.enter_context(nc.sbuf_tensor(tg + "aev", [128, 512], BF16))
        psc = [st.enter_context(nc.psum_tensor(tg + f"psc{i}", [128, 512], F32))
               for i in range(2)]
        pav = st.enter_context(nc.psum_tensor(tg + "pav", [128, 512], F32))
        pl = st.enter_context(nc.psum_tensor(tg + "pl", [1, 512], F32))
        pr = st.enter_context(nc.psum_tensor(tg + "pr", [128, 512], F32))
        psn = st.enter_context(nc.psum_tensor(tg + "psn", [1, 512], F32))
        trash = st.enter_context(nc.psum_tensor(tg + "tr", [128, 512], F32))
        psg = psn if sgoT is not None else None
        sgrow = st.enter_context(nc.sbuf_tensor(tg + "sgr", [1, T], F32)) \
            if sgoT is not None else None
        if not cfg["alibi"]:
            mskbig = st.enter_context(nc.sbuf_tensor(tg + "msb", [128, 32, 512], F32))
            permax = st.enter_context(nc.sbuf_tensor(tg + "pmx", [128, 32], F32))
            pm1 = st.enter_context(nc.sbuf_tensor(tg + "pm1", [128, 1], F32))
            pm1b = st.enter_context(nc.sbuf_tensor(tg + "pm1b", [128, 1], BF16))
            prow128 = st.enter_context(nc.sbuf_tensor(tg + "pr128", [1, 128], F32))
            prow = st.enter_context(nc.sbuf_tensor(tg + "prw", [1, 1], F32))
            mneg = st.enter_context(nc.sbuf_tensor(tg + "mng", [1, 1], F32))
            mnegb = st.enter_context(nc.sbuf_tensor(tg + "mngb", [1, 1], BF16))
            mcol1 = st.enter_context(nc.sbuf_tensor(tg + "mc1", [128, 1], F32))
            pTh = st.enter_context(nc.sbuf_tensor(tg + "pth", [128, 512], BF16))
            ptm = st.enter_context(nc.psum_tensor(tg + "ptm", [1, 128], BF16))

        nqs = T // 512
        nkstr_q = T // 512
        nkstr_k = S // 512
        for h in range(n_qh):
            kvh = h // kv_per
            with nc.Block() as blk:
                @blk.sync
                def _(sy):
                    dma(sy, p, tg + "ld", qt_h[:], qt_hbm[h * 128:(h + 1) * 128, :])
                    dma(sy, p, tg + "ld", kt_sb[:, 0:T],
                        agk_out[kvh * 128:(kvh + 1) * 128, :])
                    dma(sy, p, tg + "ld", kt_sb[:, T:S],
                        agk_out[dkv + kvh * 128: dkv + (kvh + 1) * 128, :])
                    vsrc = agv_out[:, kvh * 128:(kvh + 1) * 128].rearrange(
                        "(kc p) f -> p kc f", p=128)
                    for r in range(4):
                        dma(sy, p, tg + "ld", v_sb[:, r * 8:(r + 1) * 8, :],
                            vsrc[:, r * 8:(r + 1) * 8, :])
                    if cfg["alibi"]:
                        dma(sy, p, tg + "ld", ac_h[:], alibi_cols[h])
                        dma(sy, p, tg + "ld", sq_h[:], sqrows[h])

                if cfg["alibi"]:
                    @blk.scalar
                    def _(sc):
                        sc.wait_ge(p.sem(tg + "ld"), p.val(tg + "ld"))
                        sc.activation(qsq[:], qt_h[:], AF.Square)
                        sc.activation(ksq[:], kt_sb[:],
                                      AF.Square).then_inc(p.sem(tg + "hsq"), 1)
                        p.inc(tg + "hsq")

                    nbase = p.val(tg + "hnev")
                    strips = [(qsq, qn2, i) for i in range(nkstr_q)] + \
                             [(ksq, kn2, i) for i in range(nkstr_k)]

                    @blk.tensor
                    def _(t):
                        t.wait_ge(p.sem(tg + "hsq"), p.val(tg + "hsq"))
                        for i, (srcb, dstb, si) in enumerate(strips):
                            if i >= 1:
                                t.wait_ge(p.sem(tg + "hnev"), nbase + i)
                            t.matmul(psn[:], io["onescol"][:],
                                     srcb[:, si * 512:(si + 1) * 512],
                                     start=True, stop=True)
                            pe_fence(p, t, io, trash, tg + "hnmm")

                    @blk.vector
                    def _(v):
                        for i, (srcb, dstb, si) in enumerate(strips):
                            v.wait_ge(p.sem(tg + "hnmm"), nbase + i + 1)
                            v.tensor_copy(dstb[:, si * 512:(si + 1) * 512],
                                          psn[:]).then_inc(p.sem(tg + "hnev"), 1)
                            p.inc(tg + "hnev")
                        p.ss(v, v.tensor_copy(io["fcell"][0:1, 0:1], io["eps"][0:1, :]))
                        v.reduce_max(qmx[:], qn2[:].rearrange("a (q x) -> a q x", x=512),
                                     axis=AX.X)
                        v.reduce_max(kmx[:], kn2[:], axis=AX.X).then_inc(p.sem(tg + "hrd"), 1)
                        p.inc(tg + "hrd")

                    @blk.scalar
                    def _(sc):
                        sc.wait_ge(p.sem(tg + "hrd"), p.val(tg + "hrd"))
                        sc.activation(qmx[:], qmx[:], AF.Sqrt)
                        sc.activation(kmx[:], kmx[:],
                                      AF.Sqrt).then_inc(p.sem(tg + "hsr"), 1)
                        p.inc(tg + "hsr")

                    @blk.vector
                    def _(v):
                        v.wait_ge(p.sem(tg + "hsr"), p.val(tg + "hsr"))
                        p.ss(v, v.tensor_scalar(mrow[:], qmx[:], kmx[:, 0:1], -scale,
                                                ALU.mult, ALU.mult))
                        v.tensor_copy(mrow_bf[:], mrow[:]).then_inc(p.sem(tg + "hmn"), 1)
                        p.inc(tg + "hmn")

                    @blk.tensor
                    def _(t):
                        t.wait_ge(p.sem(tg + "hmn"), p.val(tg + "hmn"))
                        t.matmul(pr[:, 0:4], io["ones2"][0:1, :], mrow_bf[:],
                                 start=True, stop=True)
                        pe_fence(p, t, io, trash, tg + "hmb")

                    @blk.vector
                    def _(v):
                        v.wait_ge(p.sem(tg + "hmb"), p.val(tg + "hmb"))
                        p.ss(v, v.tensor_copy(mcols[:], pr[:, 0:4]))
                        for qs in range(4):
                            v.tensor_scalar(biasm[:, qs * 32:(qs + 1) * 32],
                                            ac_h[:], mcols[:, qs:qs + 1], None,
                                            ALU.add)


            nkt_tab = [min(32, (2048 + 512 * q + 512) // 128) for q in range(nqs)]
            with nc.Block() as blk:
                for qs in range(nqs):
                    nkt = nkt_tab[qs]
                    nkt_prev = nkt_tab[qs - 1] if qs > 0 else 32
                    b_msc = p.val(tg + "msc")
                    b_mmk = p.val(tg + "mmk")
                    b_mpe = p.val(tg + "mpe")
                    b_mev = p.val(tg + "mev")
                    b_sst = p.val(tg + "sst")
                    b_xm1 = p.val(tg + "xm1")
                    b_xm2 = p.val(tg + "xm2")
                    b_xm3 = p.val(tg + "xm3")
                    b_xm4 = p.val(tg + "xm4")
                    b_xm5 = p.val(tg + "xm5")
                    b_ld = p.val(tg + "ld")

                    @blk.tensor
                    def _(t, qs=qs, nkt=nkt, b_mmk=b_mmk, b_mpe=b_mpe,
                          b_mev=b_mev, b_xm1=b_xm1, b_xm3=b_xm3, b_ld=b_ld):
                        if qs == 0 and not cfg["alibi"]:
                            t.wait_ge(p.sem(tg + "ld"), b_ld)
                        for kt in range(nkt):
                            if kt >= 2:
                                t.wait_ge(p.sem(tg + "mmk"), b_mmk + kt - 1)
                            if cfg["alibi"]:
                                t.matmul(psc[kt % 2][:], io["ones2"][:],
                                         sq_h[:, qs * 512:(qs + 1) * 512],
                                         start=True, stop=False)
                            t.matmul(psc[kt % 2][:],
                                     kt_sb[:, kt * 128:(kt + 1) * 128],
                                     qt_h[:, qs * 512:(qs + 1) * 512],
                                     start=not cfg["alibi"],
                                     stop=True).then_inc(p.sem(tg + "msc"), 1)
                            p.inc(tg + "msc")
                        pe_fence(p, t, io, trash, tg + "msc")
                        if not cfg["alibi"]:
                            t.wait_ge(p.sem(tg + "xm1"), b_xm1 + 1)
                            t.transpose(ptm[:], pm1b[:], io["ident"][:])
                            pe_fence(p, t, io, trash, tg + "xm2")
                            t.wait_ge(p.sem(tg + "xm3"), b_xm3 + 1)
                            t.matmul(pr[:, 0:1], io["ones2"][0:1, :], mnegb[:],
                                     start=True, stop=True)
                            pe_fence(p, t, io, trash, tg + "xm4")
                        if qs > 0:
                            t.wait_ge(p.sem(tg + "mev"), b_mev)
                        for kt in range(nkt):
                            t.wait_ge(p.sem(tg + "mpe"), b_mpe + kt + 1)
                            t.matmul(pav[:], v_sb[:, kt, :],
                                     pT[:, kt * 512:(kt + 1) * 512],
                                     start=(kt == 0), stop=(kt == nkt - 1))
                            t.matmul(pl[:], io["onescol"][:],
                                     pT[:, kt * 512:(kt + 1) * 512],
                                     start=(kt == 0), stop=(kt == nkt - 1))
                        pe_fence(p, t, io, trash, tg + "mavd")
                        t.wait_ge(p.sem(tg + "mlv"), p.val(tg + "mlv") + 1)
                        t.matmul(pr[:], io["ones2"][0:1, :], linv_bf[:],
                                 start=True, stop=True)
                        pe_fence(p, t, io, trash, tg + "mr")
                        if sgoT is not None:
                            t.wait_ge(p.sem(tg + "mev"), p.val(tg + "mev") + 1)
                            t.matmul(psg[:], io["onescol"][:], aev[:],
                                     start=True, stop=True)
                            pe_fence(p, t, io, trash, tg + "sgm")

                    @blk.vector
                    def _(v, qs=qs, nkt=nkt, nkt_prev=nkt_prev, b_msc=b_msc,
                          b_mpe=b_mpe, b_sst=b_sst,
                          b_xm2=b_xm2, b_xm4=b_xm4,
                          b_sgev=p.val(tg + "sgev")):
                        vv_base = 512 * qs + MST_OFF
                        for kt in range(nkt):
                            v.wait_ge(p.sem(tg + "msc"), b_msc + kt + 2)
                            if cfg["alibi"]:
                                if kt >= 2:
                                    v.wait_ge(p.sem(tg + "mpe"), b_mpe + kt - 1)
                                mdst = msk[kt % 2][:]
                            else:
                                if qs > 0:
                                    v.wait_ge(p.sem(tg + "mpe"),
                                              b_mpe - nkt_prev
                                              + min(kt + 1, nkt_prev))
                                mdst = mskbig[:, kt, :]
                            vv0 = vv_base - 128 * kt
                            v.tensor_add(mdst, psc[kt % 2][:],
                                         masterT_sb[:, vv0:vv0 + 512]
                                         ).then_inc(p.sem(tg + "mmk"), 1)
                            p.inc(tg + "mmk")
                        if not cfg["alibi"]:
                            p.ss(v, v.tensor_copy(io["fcell"][0:1, 0:1],
                                                  io["eps"][0:1, :]))
                            p.ss(v, v.reduce_max(permax[:, 0:nkt],
                                                 mskbig[:, 0:nkt, :],
                                                 axis=AX.X))
                            p.ss(v, v.reduce_max(pm1[:], permax[:, 0:nkt],
                                                 axis=AX.X))
                            v.tensor_copy(pm1b[:], pm1[:]).then_inc(
                                p.sem(tg + "xm1"), 1)
                            p.inc(tg + "xm1")
                            v.wait_ge(p.sem(tg + "xm2"), b_xm2 + 1)
                            p.ss(v, v.tensor_copy(prow128[:], ptm[0:1, :]))
                            p.ss(v, v.reduce_max(prow[:], prow128[:], axis=AX.X))
                            p.ss(v, v.tensor_scalar(mneg[:], prow[:],
                                                    -1.0 * scale, 70.0,
                                                    ALU.mult, ALU.add))
                            v.tensor_copy(mnegb[:], mneg[:]).then_inc(
                                p.sem(tg + "xm3"), 1)
                            p.inc(tg + "xm3")
                            v.wait_ge(p.sem(tg + "xm4"), b_xm4 + 1)
                            v.tensor_copy(mcol1[:], pr[:, 0:1]).then_inc(
                                p.sem(tg + "xm5"), 1)
                            p.inc(tg + "xm5")
                        v.wait_ge(p.sem(tg + "mavd"), p.val(tg + "mavd"))
                        p.ss(v, v.reciprocal(linv[:], pl[:]))
                        v.tensor_copy(linv_bf[:], linv[:]).then_inc(p.sem(tg + "mlv"), 1)
                        p.inc(tg + "mlv")
                        v.wait_ge(p.sem(tg + "mr"), p.val(tg + "mr"))
                        p.ss(v, v.tensor_copy(r_sb[:], pr[:]))
                        if qs > 0:
                            v.wait_ge(p.sem(tg + "sst"), b_sst)
                            if sgoT is not None:
                                v.wait_ge(p.sem(tg + "sgev"), b_sgev)
                        v.tensor_mul(aev[:], pav[:], r_sb[:]).then_inc(p.sem(tg + "mev"), 1)
                        p.inc(tg + "mev")
                        if sgoT is not None:
                            v.wait_ge(p.sem(tg + "sgm"), p.val(tg + "sgm"))
                            v.tensor_copy(sgrow[:, qs * 512:(qs + 1) * 512],
                                          psg[:]).then_inc(p.sem(tg + "sgev"), 1)
                            p.inc(tg + "sgev")

                    @blk.scalar
                    def _(sc, qs=qs, nkt=nkt, b_mmk=b_mmk, b_mpe=b_mpe,
                          b_xm5=b_xm5):
                        if cfg["alibi"]:
                            for kt in range(nkt):
                                sc.wait_ge(p.sem(tg + "mmk"), b_mmk + kt + 1)
                                bias_ap = biasm[:, qs * 32 + kt: qs * 32 + kt + 1]
                                sc.activation(pT[:, kt * 512:(kt + 1) * 512],
                                              msk[kt % 2][:], AF.Exp, bias=bias_ap,
                                              scale=scale).then_inc(p.sem(tg + "mpe"), 1)
                                p.inc(tg + "mpe")
                        else:
                            sc.wait_ge(p.sem(tg + "xm5"), b_xm5 + 1)
                            for kt in range(nkt):
                                sc.activation(pT[:, kt * 512:(kt + 1) * 512],
                                              mskbig[:, kt, :], AF.Exp,
                                              bias=mcol1[:, 0:1], scale=scale
                                              ).then_inc(p.sem(tg + "mpe"), 1)
                                p.inc(tg + "mpe")

                    @blk.sync
                    def _(sy, qs=qs):
                        sy.wait_ge(p.sem(tg + "mev"), p.val(tg + "mev"))
                        dma(sy, p, tg + "sst", attn_dst[h * 128:(h + 1) * 128,
                                                        qs * 512:(qs + 1) * 512],
                            aev[:])
                @blk.sync
                def _(sy):
                    if sgoT is not None:
                        sy.wait_ge(p.sem(tg + "sgev"), p.val(tg + "sgev"))
                        dma(sy, p, tg + "sgs", sgoT[h:h + 1, :], sgrow[:])
                        sy.wait_ge(p.sem(tg + "sgs"), p.val(tg + "sgs"))
                    sy.wait_ge(p.sem(tg + "sst"), p.val(tg + "sst"))


def emit_oproj(p, io, attn_src, dst_hbm, tag, w_load, bias_bcast=None,
               corr=None):
    nc = p.nc
    with contextlib.ExitStack() as st:
        wmov = [st.enter_context(nc.sbuf_tensor(f"{tag}w{i}", [128, DC, 512], BF16))
                for i in range(2)]
        acol = [st.enter_context(nc.sbuf_tensor(f"{tag}a{i}", [128, DC, 128], BF16))
                for i in range(2)]
        evb = [st.enter_context(nc.sbuf_tensor(f"{tag}e{i}", [128, 512], BF16))
               for i in range(2)]
        psq = [st.enter_context(nc.psum_tensor(f"{tag}p{i}", [128, 512], F32))
               for i in range(2)]
        trash = st.enter_context(nc.psum_tensor(tag + "tr", [128, 512], F32))
        b_wrdy = p.val(tag + "wrdy")
        for os_ in range(8):
            par = os_ % 2
            with nc.Block() as blk:
                if os_ == 0:
                    w_load(blk, 0, wmov[0])
                if os_ + 1 < 8:
                    w_load(blk, os_ + 1, wmov[1 - par])

                b_mm = p.val(tag + "mm")
                b_ev = p.val(tag + "ev")
                b_st = [p.val(tag + "st0"), p.val(tag + "st1")]
                b_al = [p.val(tag + "al0"), p.val(tag + "al1")]

                def load_acol(sy, tt):
                    src = attn_src[:, tt * 128:(tt + 1) * 128].rearrange(
                        "(c p) f -> p c f", p=128)
                    for r in range(4):
                        dma(sy, p, tag + "al" + str(tt % 2),
                            acol[tt % 2][:, r * 8:(r + 1) * 8, :],
                            src[:, r * 8:(r + 1) * 8, :])

                @blk.sync
                def _(sy):
                    for tt in range(NT):
                        if tt == 0:
                            load_acol(sy, 0)
                            load_acol(sy, 1)
                        if tt + 2 < NT:
                            sy.wait_ge(p.sem(tag + "mm"), b_mm + tt + 1)
                            load_acol(sy, tt + 2)
                        sy.wait_ge(p.sem(tag + "ev"), b_ev + tt + 1)
                        dma(sy, p, tag + "st" + str(tt % 2),
                            dst_hbm[tt * 128:(tt + 1) * 128,
                                    os_ * 512:(os_ + 1) * 512],
                            evb[tt % 2][:])
                    sy.wait_ge(p.sem(tag + "st0"), p.val(tag + "st0"))
                    sy.wait_ge(p.sem(tag + "st1"), p.val(tag + "st1"))

                @blk.tensor
                def _(t):
                    t.wait_ge(p.sem(tag + "wrdy"), b_wrdy + os_ + 1)
                    for tt in range(NT):
                        t.wait_ge(p.sem(tag + "al" + str(tt % 2)),
                                  b_al[tt % 2] + 64 * (tt // 2 + 1))
                        if tt >= 2:
                            t.wait_ge(p.sem(tag + "ev"), b_ev + tt - 1)
                        ps = psq[tt % 2]
                        last = corr is None
                        for c in range(DC):
                            t.matmul(ps[:], acol[tt % 2][:, c, :],
                                     wmov[par][:, c, :],
                                     start=(c == 0),
                                     stop=(c == DC - 1 and last))
                        if corr is not None:
                            sgoT, s8o = corr
                            t.matmul(ps[:], sgoT[:, tt * 128:(tt + 1) * 128],
                                     s8o[:, os_ * 512:(os_ + 1) * 512],
                                     start=False, stop=True)
                        pe_fence(p, t, io, trash, tag + "mm")

                @blk.vector
                def _(v):
                    for tt in range(NT):
                        v.wait_ge(p.sem(tag + "mm"), b_mm + tt + 1)
                        if tt >= 2:
                            v.wait_ge(p.sem(tag + "st" + str(tt % 2)),
                                      b_st[tt % 2] + 16 * (tt // 2))
                        if bias_bcast is not None:
                            ins = v.tensor_add(evb[tt % 2][:], psq[tt % 2][:],
                                               bias_bcast[:, os_ * 512:(os_ + 1) * 512])
                        else:
                            ins = v.tensor_copy(evb[tt % 2][:], psq[tt % 2][:])
                        ins.then_inc(p.sem(tag + "ev"), 1)
                        p.inc(tag + "ev")


def emit_epilogue(p, x_src, a_src, f_src, dst, tag):
    nc = p.nc
    with contextlib.ExitStack() as st:
        xst = st.enter_context(nc.sbuf_tensor(tag + "x", [128, D], F32))
        ast = st.enter_context(nc.sbuf_tensor(tag + "a", [128, D], BF16))
        fst = st.enter_context(nc.sbuf_tensor(tag + "f", [128, D], BF16))
        ost = st.enter_context(nc.sbuf_tensor(tag + "o", [128, D], F32))
        for tt in range(NT):
            with nc.Block() as blk:
                @blk.sync
                def _(sy):
                    dma(sy, p, tag + "ld", xst[:], x_src[tt * 128:(tt + 1) * 128, :])
                    dma(sy, p, tag + "ld", ast[:], a_src[tt * 128:(tt + 1) * 128, :])
                    dma(sy, p, tag + "ld", fst[:], f_src[tt * 128:(tt + 1) * 128, :])

                @blk.vector
                def _(v):
                    v.wait_ge(p.sem(tag + "ld"), p.val(tag + "ld"))
                    p.ss(v, v.tensor_add(ost[:], xst[:], ast[:]))
                    v.tensor_add(ost[:], ost[:], fst[:]).then_inc(p.sem(tag + "ev"), 1)
                    p.inc(tag + "ev")

                @blk.sync
                def _(sy):
                    sy.wait_ge(p.sem(tag + "ev"), p.val(tag + "ev"))
                    dma(sy, p, tag + "st", dst[tt * 128:(tt + 1) * 128, :], ost[:])
                    sy.wait_ge(p.sem(tag + "st"), p.val(tag + "st"))



def build_l0(debug=(), upto="full", fuse=None):
    if fuse is None:
        nc = bass.Bass("TRN2", target_bir_lowering=False)
        p = Prog(nc)
    else:
        nc = NcPfx(fuse["nc"], "zA")
        p = fuse["p"]
        p.nc = nc

    x0 = nc.dram_tensor("x0", [T, D], F32, kind="ExternalInput").ap()
    w_attn = nc.dram_tensor("w_attn", [D, 3 * D], F32, kind="ExternalInput").ap()
    b_attn = nc.dram_tensor("b_attn", [1, 3 * D], F32, kind="ExternalInput").ap()
    w_proj = nc.dram_tensor("w_proj", [D, D], F32, kind="ExternalInput").ap()
    b_proj = nc.dram_tensor("b_proj", [1, D], F32, kind="ExternalInput").ap()
    w_baseT = nc.dram_tensor("w_baseT", [D, D], F32, kind="ExternalInput").ap()
    b_base = nc.dram_tensor("b_base", [1, D], F32, kind="ExternalInput").ap()
    a4t = nc.dram_tensor("a4t", [D, 32], F32, kind="ExternalInput").ap()
    bblk = nc.dram_tensor("bblk", [32, D], F32, kind="ExternalInput").ap()
    w4t = nc.dram_tensor("w4t", [D, 4], F32, kind="ExternalInput").ap()
    rb4 = nc.dram_tensor("rb4", [1, 4], F32, kind="ExternalInput").ap()
    prior4 = nc.dram_tensor("prior4", [1, 4], F32, kind="ExternalInput").ap()
    b_qk_cols = nc.dram_tensor("b_qk_cols", [128, 64], F32, kind="ExternalInput").ap()
    masterT = nc.dram_tensor("masterT", [128, MST_W], BF16, kind="ExternalInput").ap()
    alibi_cols = nc.dram_tensor("alibi_cols", [32, 128, 32], F32, kind="ExternalInput").ap()
    sqrows = nc.dram_tensor("sqrows", [32, 2, T], BF16, kind="ExternalInput").ap()
    expmat = nc.dram_tensor("expmat", [4, 32], F32, kind="ExternalInput").ap()

    if fuse is None:
        x1 = nc.dram_tensor("x1", [T, D], F32, kind="ExternalOutput").ap()
    else:
        x1 = fuse["x1"]
    dbg = {}
    for name, shp, dt_ in (("xnT", [128, DC * T], BF16), ("qt", [D, T], BF16),
                           ("kt", [D, T], BF16), ("v", [T, D], BF16),
                           ("attnT", [D, T], BF16), ("ffn", [T, D], BF16),
                           ("atno", [T, D], BF16)):
        if name in debug:
            dbg[name] = nc.dram_tensor("dbg_" + name, shp, dt_,
                                       kind="ExternalOutput").ap()

    qt_s = nc.dram_tensor("qt_s", [D, T], BF16, kind="Internal").ap()
    ag_k_in = nc.dram_tensor("ag_k_in", [D, T], BF16, kind="Internal")
    ag_k_out = nc.dram_tensor("ag_k_out", [2 * D, T], BF16, kind="Internal")
    ag_vi = nc.dram_tensor("ag_vi", [T, D], BF16, kind="Internal")
    ag_vo = nc.dram_tensor("ag_vo", [S, D], BF16, kind="Internal")
    attn_s = nc.dram_tensor("attn_s", [D, T], BF16, kind="Internal").ap()
    ffn_s = nc.dram_tensor("ffn_s", [T, D], BF16, kind="Internal").ap()
    atno_s = nc.dram_tensor("atno_s", [T, D], BF16, kind="Internal").ap()

    gctx = contextlib.ExitStack()
    if fuse is None:
        io = _mk_io(nc, gctx)
    else:
        io = fuse["io"]

    with gctx:
        if fuse is None:
            emit_init_consts(p, io)

        with contextlib.ExitStack() as r1:
            xnT = r1.enter_context(nc.sbuf_tensor("xnT", [128, DC * T], BF16))
            emit_ln_xnt(p, io, x0, xnT, "ln")
            if "xnT" in dbg:
                with nc.Block() as blk:
                    @blk.sync
                    def _(sy):
                        dma(sy, p, "dbg", dbg["xnT"], xnT[:])
                        sy.wait_ge(p.sem("dbg"), p.val("dbg"))

            bcols = r1.enter_context(nc.sbuf_tensor("bcols", [128, 96], F32))
            with nc.Block() as blk:
                @blk.sync
                def _(sy):
                    dma(sy, p, "bcl", bcols[:, 0:64], b_qk_cols)
                    sy.wait_ge(p.sem("bcl"), p.val("bcl"))

            if upto == "ln":
                return _fin(nc)
            emit_featmaj_proj(p, io, xnT, w_attn, 0, 32, bcols, 0, qt_s, "pq")
            if upto == "q":
                if "qt" in dbg:
                    _copy_hbm(p, dbg["qt"], qt_s)
                return _fin(nc)
            emit_featmaj_proj(p, io, xnT, w_attn, D, 32, bcols, 32,
                              ag_k_in.ap(), "pk")
            with nc.Block() as blk:
                @blk.gpsimd
                def _(g):
                    g.collective_compute(
                        "AllGather", ALU.bypass, replica_groups=REPLICA_PAIRS,
                        ins=[ag_k_in[:]], outs=[ag_k_out[:]],
                    ).then_inc(p.sem("agk"), 1)
                    p.inc("agk")

            bvb = r1.enter_context(nc.sbuf_tensor("bvb", [128, D], BF16))
            bbb = r1.enter_context(nc.sbuf_tensor("bbb", [128, D], BF16))
            with nc.Block() as blk:
                @blk.gpsimd
                def _(g):
                    dma(g, p, "bcl", bvb[:],
                        b_attn[:, 2 * D:3 * D].to_broadcast((128, D)))
                    dma(g, p, "bcl", bbb[:], b_base.to_broadcast((128, D)))
                    g.wait_ge(p.sem("bcl"), p.val("bcl"))
            emit_tokmaj_proj(p, io, xnT, w_attn, D, 2 * D, 8, bvb,
                             ag_vi.ap(), "pv")
            with nc.Block() as blk:
                @blk.gpsimd
                def _(g):
                    g.collective_compute(
                        "AllGather", ALU.bypass, replica_groups=REPLICA_PAIRS,
                        ins=[ag_vi[:]], outs=[ag_vo[:]],
                    ).then_inc(p.sem("agv"), 1)
                    p.inc("agv")

            if upto == "v":
                if "qt" in dbg:
                    _copy_hbm(p, dbg["qt"], qt_s)
                with nc.Block() as blk:
                    @blk.gpsimd
                    def _(g):
                        g.wait_ge(p.sem("agk"), p.val("agk"))
                        g.wait_ge(p.sem("agv"), p.val("agv"))
                if "kt" in dbg:
                    _copy_hbm(p, dbg["kt"], ag_k_out.ap()[0:D, :])
                if "v" in dbg:
                    _copy_hbm(p, dbg["v"], ag_vo.ap()[0:T, :])
                return _fin(nc)
            zh_sb, bblk_sb = emit_lora_prep(p, io, r1, xnT, a4t, w4t, bblk,
                                            rb4, prior4, expmat)

            def lora_mm(t, tt, fs, ps):
                return t.matmul(ps[:], zh_sb[:, tt * 128:(tt + 1) * 128],
                                bblk_sb[:, fs * 512:(fs + 1) * 512],
                                start=False, stop=True)

            emit_tokmaj_proj(p, io, xnT, w_baseT, D, 0, 8, bbb,
                             ffn_s, "py", extra_mm=lora_mm)

        if "ffn" in dbg:
            _copy_hbm(p, dbg["ffn"], ffn_s)
        if "qt" in dbg:
            _copy_hbm(p, dbg["qt"], qt_s)
        if upto == "ffn":
            return _fin(nc)

        with contextlib.ExitStack() as r2:
            mst_sb = r2.enter_context(nc.sbuf_tensor("mst", [128, MST_W], BF16))
            with nc.Block() as blk:
                @blk.sync
                def _(sy):
                    dma(sy, p, "mstl", mst_sb[:], masterT)
                    sy.wait_ge(p.sem("mstl"), p.val("mstl"))

                @blk.gpsimd
                def _(g):
                    g.wait_ge(p.sem("agk"), p.val("agk"))
                    g.wait_ge(p.sem("agv"), p.val("agv"))
            if "kt" in dbg:
                _copy_hbm(p, dbg["kt"], ag_k_out.ap()[0:D, :])
            if "v" in dbg:
                _copy_hbm(p, dbg["v"], ag_vo.ap()[0:T, :])
            cfg = dict(n_qh=32, kv_per=1, dkv=D, scale=SCALE0, alibi=True,
                       tag="a0")
            emit_attention(p, io, cfg, qt_s, ag_k_out.ap(), ag_vo.ap(),
                           attn_s, mst_sb, alibi_cols, sqrows)
        if "attnT" in dbg:
            _copy_hbm(p, dbg["attnT"], attn_s)
        if upto == "attn":
            return _fin(nc)

        with contextlib.ExitStack() as r3:
            bpb = r3.enter_context(nc.sbuf_tensor("bpb", [128, D], F32))
            with nc.Block() as blk:
                @blk.sync
                def _(sy):
                    dma(sy, p, "bcl2", bpb[:], b_proj.to_broadcast((128, D)))
                    sy.wait_ge(p.sem("bcl2"), p.val("bcl2"))

            def wo_load(blk, os_, buf):
                @blk.gpsimd
                def _(g):
                    load_w_colstrip(g, p, "powl", buf, w_proj, os_ * 512, 512)
                    g.wait_ge(p.sem("powl"), p.val("powl"))
                    g.sem_inc(p.sem("powrdy"), 1)
                    p.inc("powrdy")

            emit_oproj(p, io, attn_s, atno_s, "po", wo_load, bias_bcast=bpb)
        if "atno" in dbg:
            _copy_hbm(p, dbg["atno"], atno_s)

        emit_epilogue(p, x0, atno_s, ffn_s, x1, "ep")
    return nc


def _fin(nc):
    return nc


def _copy_hbm(p, dst, src, sbuf=False):
    nc = p.nc
    if sbuf:
        with nc.Block() as blk:
            @blk.sync
            def _(sy):
                dma(sy, p, "cps", dst, src)
                sy.wait_ge(p.sem("cps"), p.val("cps"))
        return
    rows, cols = dst.shape
    with contextlib.ExitStack() as st:
        buf = st.enter_context(nc.sbuf_tensor("cph" + str(p.inc("cpn")),
                                              [128, cols], BF16))
        for r0 in range(0, rows, 128):
            with nc.Block() as blk:
                @blk.sync
                def _(sy):
                    dma(sy, p, "cpl", buf[:], src[r0:r0 + 128, :])
                    sy.wait_ge(p.sem("cpl"), p.val("cpl"))
            with nc.Block() as blk:
                @blk.sync
                def _(sy):
                    dma(sy, p, "cps", dst[r0:r0 + 128, :], buf[:])
                    sy.wait_ge(p.sem("cps"), p.val("cps"))



def _hilo_bf16(x):
    hi = x.astype(bfdt)
    lo = (x - hi.astype(np.float32)).astype(bfdt)
    return hi, lo


def host_prep_l0(inputs):
    slopes = _alibi_slopes(32)
    A4 = np.asarray(inputs["L0_ffn_lora_A"])[np.asarray(inputs["L0_ffn_active"])]
    B4 = np.asarray(inputs["L0_ffn_lora_B"])[np.asarray(inputs["L0_ffn_active"])]
    W4 = np.asarray(inputs["L0_ffn_router_W"])[np.asarray(inputs["L0_ffn_active"])]
    rb4 = np.asarray(inputs["L0_ffn_router_b"])[np.asarray(inputs["L0_ffn_active"])]
    pr4 = np.asarray(inputs["L0_ffn_domain_prior"])[np.asarray(inputs["L0_ffn_active"])]
    expmat = np.zeros((4, 32), np.float32)
    for e in range(4):
        expmat[e, e * 8:(e + 1) * 8] = 1.0

    shared = dict(
        w_attn=np.ascontiguousarray(inputs["L0_attn_w_attn"], np.float32),
        b_attn=np.ascontiguousarray(inputs["L0_attn_b_attn"], np.float32).reshape(1, -1),
        b_qk_cols=np.ascontiguousarray(
            np.asarray(inputs["L0_attn_b_attn"], np.float32)[:2 * D].reshape(64, 128).T),
        w_proj=np.ascontiguousarray(inputs["L0_attn_w_proj"], np.float32),
        b_proj=np.ascontiguousarray(inputs["L0_attn_b_proj"], np.float32).reshape(1, -1),
        w_baseT=np.ascontiguousarray(np.asarray(inputs["L0_ffn_W_base"]).T, np.float32),
        b_base=np.ascontiguousarray(inputs["L0_ffn_b_base"], np.float32).reshape(1, -1),
        a4t=np.ascontiguousarray(A4.transpose(2, 0, 1).reshape(D, 32), np.float32),
        bblk=np.ascontiguousarray(B4.transpose(0, 2, 1).reshape(32, D), np.float32),
        w4t=np.ascontiguousarray(W4.T, np.float32),
        rb4=np.ascontiguousarray(rb4, np.float32).reshape(1, 4),
        prior4=np.ascontiguousarray(pr4, np.float32).reshape(1, 4),
        expmat=expmat,
    )
    kk = (128 * np.arange(32)[None, :] + np.arange(128)[:, None]).astype(np.float32)
    ac = slopes[:, None, None] * kk[None]
    shared["alibi_cols"] = np.ascontiguousarray(ac, np.float32)

    x_full = np.asarray(inputs["L0_attn_hidden_states"])
    percore = []
    for c in range(NCORES):
        b, hh = c // 2, c % 2
        q0g = hh * T
        d = dict(x0=np.ascontiguousarray(x_full[b, q0g:q0g + T, :], np.float32))
        ii = np.arange(128)[:, None]
        vv = np.arange(MST_W)[None, :]
        mt = np.where(ii - (vv - MST_OFF) <= q0g, 0.0, NEGBIG).astype(np.float32)
        d["masterT"] = mt.astype(bfdt)
        qg = (q0g + np.arange(T)).astype(np.float32)
        v = -128.0 * slopes[:, None] * qg[None, :]
        hi, lo = _hilo_bf16(v)
        d["sqrows"] = np.stack([hi, lo], axis=1)
        percore.append(d)
    return shared, percore


def host_prep_l1(inputs, x1_shards=None):
    shared = dict(
        q_codes=np.ascontiguousarray(inputs["L1_attn_q_codes"], np.int32),
        q_scales=np.ascontiguousarray(inputs["L1_attn_q_scales"], np.float32),
        k_codes=np.ascontiguousarray(inputs["L1_attn_k_codes"], np.int32),
        k_scales=np.ascontiguousarray(inputs["L1_attn_k_scales"], np.float32),
        v_codes=np.ascontiguousarray(inputs["L1_attn_v_codes"], np.int32),
        v_scales=np.ascontiguousarray(inputs["L1_attn_v_scales"], np.float32),
        o_codes=np.ascontiguousarray(inputs["L1_attn_o_codes"], np.int32),
        o_scales=np.ascontiguousarray(inputs["L1_attn_o_scales"], np.float32),
        qw_gu=np.ascontiguousarray(inputs["L1_ffn_qweight_gu"], np.int32),
        qz_gu=np.ascontiguousarray(inputs["L1_ffn_qzeros_gu"], np.int32),
        sc_gu=np.ascontiguousarray(inputs["L1_ffn_scales_gu"], np.float32),
        qw_dn=np.ascontiguousarray(inputs["L1_ffn_qweight_dn"], np.int32),
        qz_dn=np.ascontiguousarray(inputs["L1_ffn_qzeros_dn"], np.int32),
        sc_dn=np.ascontiguousarray(inputs["L1_ffn_scales_dn"], np.float32),
    )
    inv_freq = (1.0 / 10000.0 ** (np.arange(0, 128, 2, dtype=np.float32) / 128.0))
    percore = []
    ii = np.arange(128)[:, None]
    vv = np.arange(MST_W)[None, :]
    for c in range(NCORES):
        q0g = (c % 2) * T
        pos = (q0g + np.arange(T)).astype(np.float32)
        f = pos[:, None] * inv_freq[None, :]
        mt = np.where(ii - (vv - MST_OFF) <= q0g, 0.0, NEGBIG).astype(np.float32)
        pc = dict(
            masterT=mt.astype(bfdt),
            rope_cos=np.cos(f).astype(np.float32),
            rope_sin=np.sin(f).astype(np.float32),
        )
        if x1_shards is not None:
            pc["x1"] = np.ascontiguousarray(x1_shards[c], np.float32)
        percore.append(pc)
    return shared, percore


def run_l1(inputs, x1_shards, debug=(), upto="full"):
    nc = build_l1(debug=debug, upto=upto)
    shared, percore = host_prep_l1(inputs, x1_shards)
    in_maps = [dict(shared, **pc) for pc in percore]
    res = run_bass_kernel_spmd(nc, in_maps, core_ids=list(range(NCORES)))
    return res.results


def run_l0(inputs, debug=()):
    nc = build_l0(debug=debug)
    shared, percore = host_prep_l0(inputs)
    in_maps = [dict(shared, **pc) for pc in percore]
    res = run_bass_kernel_spmd(nc, in_maps, core_ids=list(range(NCORES)))
    return res.results


def kernel_unfused(**inputs):
    res0 = run_l0(inputs)
    x1_shards = [r["x1"] for r in res0]
    res1 = run_l1(inputs, x1_shards)
    out = np.empty((4, S, D), np.float32)
    for c in range(NCORES):
        b, hh = c // 2, c % 2
        out[b, hh * T:(hh + 1) * T, :] = res1[c]["out"]
    return out


def build_full():
    nc = bass.Bass("TRN2", target_bir_lowering=False)
    p = Prog(nc)
    gctx = contextlib.ExitStack()
    io = _mk_io(nc, gctx)
    x1f = nc.dram_tensor("x1f", [T, D], F32, kind="Internal").ap()
    with gctx:
        emit_init_consts(p, io)
        build_l0(fuse=dict(nc=nc, p=p, io=io, x1=x1f))
        build_l1(fuse=dict(nc=nc, p=p, io=io, x1=x1f))
    return nc


def kernel(**inputs):
    nc = build_full()
    shared0, percore0 = host_prep_l0(inputs)
    shared1, percore1 = host_prep_l1(inputs)
    in_maps = []
    for c in range(NCORES):
        m = {}
        for k, v in dict(shared0, **percore0[c]).items():
            m["zA" + k] = v
        for k, v in dict(shared1, **percore1[c]).items():
            m["zB" + k] = v
        in_maps.append(m)
    res = run_bass_kernel_spmd(nc, in_maps, core_ids=list(range(NCORES))).results
    out = np.empty((4, S, D), np.float32)
    for c in range(NCORES):
        b, hh = c // 2, c % 2
        out[b, hh * T:(hh + 1) * T, :] = res[c]["zBout"]
    return out



def emit_dq_cs_strip(p, blk, io, codes_hbm, scales_hbm, col0, ncols, wdst, tag,
                     bufs, done_sem, nch=DC):
    cod, sbc = bufs
    b_dq = p.val(tag + "dq")
    b_cl = [p.val(tag + "cl0"), p.val(tag + "cl1")]

    @blk.gpsimd
    def _(g):
        for c in range(nch):
            g.wait_ge(p.sem(tag + "dq"), max(0, b_dq + c - 1))
            dma(g, p, tag + "cl" + str(c % 2), cod[c % 2][:, 0:ncols],
                codes_hbm[c * 128:(c + 1) * 128, col0:col0 + ncols])
            dma(g, p, tag + "cl" + str(c % 2), sbc[c % 2][:, 0:ncols],
                scales_hbm[c:c + 1, col0:col0 + ncols].to_broadcast((128, ncols)))

    @blk.vector
    def _(v):
        for c in range(nch):
            v.wait_ge(p.sem(tag + "cl" + str(c % 2)),
                      b_cl[c % 2] + 32 * (c // 2 + 1))
            ins = v.tensor_mul(wdst[:, c, :], cod[c % 2][:, 0:ncols],
                               sbc[c % 2][:, 0:ncols])
            ins.then_inc(p.sem(tag + "dq"), 1)
            p.inc(tag + "dq")


def emit_l1_zs(p, io, qz_gu, sc_gu, qz_dn, sc_dn):
    nc = p.nc
    INTER = 11008
    IC = INTER // 128
    zs_gu = nc.dram_tensor("zs_gu_s", [32, 2 * INTER], BF16, kind="Internal").ap()
    zs_dn = nc.dram_tensor("zs_dn_s", [IC, D], BF16, kind="Internal").ap()
    with contextlib.ExitStack() as st:
        qzp = st.enter_context(nc.sbuf_tensor("zsqz", [IC, 512], I32))
        zint = st.enter_context(nc.sbuf_tensor("zszi", [IC, 4096], I32))
        scp = st.enter_context(nc.sbuf_tensor("zssc", [IC, 4096], F32))
        zsp = st.enter_context(nc.sbuf_tensor("zszs", [IC, 4096], BF16))
        jobs = [(qz_gu, sc_gu, zs_gu, 32, 2752, 8)]
        for pc in range(8):
            with nc.Block() as blk:
                @blk.sync
                def _(sy, pc=pc):
                    dma(sy, p, "zsl", qzp[0:32, 0:344],
                        qz_gu[:, pc * 344:(pc + 1) * 344])
                    dma(sy, p, "zsl", scp[0:32, 0:2752],
                        sc_gu[:, pc * 2752:(pc + 1) * 2752])
                    sy.wait_ge(p.sem("zsl"), p.val("zsl"))

                @blk.vector
                def _(v):
                    v.wait_ge(p.sem("zsl"), p.val("zsl"))
                    for j in range(8):
                        v.tensor_scalar(zint[0:32, 0:2752][:, j::8],
                                        qzp[0:32, 0:344], 4 * j, 15,
                                        ALU.logical_shift_right, ALU.bitwise_and)
                    p.ss(v, v.tensor_copy(io["fcell"][0:1, 0:1], io["eps"][0:1, :]))
                    v.scalar_tensor_tensor(zsp[0:32, 0:2752], zint[0:32, 0:2752],
                                           -1.0, scp[0:32, 0:2752],
                                           ALU.mult, ALU.mult).then_inc(p.sem("zse"), 1)
                    p.inc("zse")

                @blk.sync
                def _(sy, pc=pc):
                    sy.wait_ge(p.sem("zse"), p.val("zse"))
                    dma(sy, p, "zss", zs_gu[:, pc * 2752:(pc + 1) * 2752],
                        zsp[0:32, 0:2752])
                    sy.wait_ge(p.sem("zss"), p.val("zss"))
        with nc.Block() as blk:
            @blk.sync
            def _(sy):
                dma(sy, p, "zsl", qzp[:, 0:512], qz_dn)
                dma(sy, p, "zsl", scp[:, 0:4096], sc_dn)
                sy.wait_ge(p.sem("zsl"), p.val("zsl"))

            @blk.vector
            def _(v):
                v.wait_ge(p.sem("zsl"), p.val("zsl"))
                for j in range(8):
                    v.tensor_scalar(zint[:, j::8], qzp[:, 0:512], 4 * j, 15,
                                    ALU.logical_shift_right, ALU.bitwise_and)
                p.ss(v, v.tensor_copy(io["fcell"][0:1, 0:1], io["eps"][0:1, :]))
                v.scalar_tensor_tensor(zsp[:], zint[:], -1.0, scp[:, 0:4096],
                                       ALU.mult, ALU.mult).then_inc(p.sem("zse"), 1)
                p.inc("zse")

            @blk.sync
            def _(sy):
                sy.wait_ge(p.sem("zse"), p.val("zse"))
                dma(sy, p, "zss", zs_dn, zsp[:])
                sy.wait_ge(p.sem("zss"), p.val("zss"))
    return zs_gu, zs_dn


def emit_l1_gu(p, io, xnT, sigT, qw_gu, sc_gu, h_dst, sgh_hbm, zs_gu):
    nc = p.nc
    INTER = 11008
    IC = INTER // 128
    with contextlib.ExitStack() as st:
        wpk = st.enter_context(nc.sbuf_tensor("gupk", [128, DC, 16], I32))
        wint = st.enter_context(nc.sbuf_tensor("guwi", [128, DC, 128], I32))
        wbf = st.enter_context(nc.sbuf_tensor("guwb", [128, DC, 128], BF16))
        sgub = st.enter_context(nc.sbuf_tensor("gusb", [128, DC, 128], BF16))
        zsl = st.enter_context(nc.sbuf_tensor("guzsl", [32, 128], BF16))
        gT = st.enter_context(nc.sbuf_tensor("gugt", [128, T], BF16))
        hT = st.enter_context(nc.sbuf_tensor("guht", [128, 512], BF16))
        sgrow = st.enter_context(nc.sbuf_tensor("gusgr", [1, T], F32))
        psq = [st.enter_context(nc.psum_tensor(f"gup{i}", [128, 512], F32))
               for i in range(4)]
        psn = st.enter_context(nc.psum_tensor("gusn", [1, 512], F32))
        trash = st.enter_context(nc.psum_tensor("gutr", [128, 512], F32))
        for i86 in range(IC):
            for half in range(2):
                ncol = half * INTER + i86 * 128
                with nc.Block() as blk:
                    @blk.gpsimd
                    def _(g, ncol=ncol):
                        load_colblock(g, p, "guwl", wpk, qw_gu, ncol // 8, 16, D)
                        dma(g, p, "guwl", sgub[:],
                            sc_gu[:, ncol:ncol + 128].unsqueeze(0)
                            .broadcast_to((128, DC, 128)))

                    @blk.sync
                    def _(sy, ncol=ncol):
                        dma(sy, p, "guzll", zsl[:], zs_gu[:, ncol:ncol + 128])

                    @blk.vector
                    def _(v):
                        v.wait_ge(p.sem("guwl"), p.val("guwl"))
                        for j in range(8):
                            v.tensor_scalar(wint[:, :, j::8], wpk[:], 4 * j, 15,
                                            ALU.logical_shift_right, ALU.bitwise_and)
                        p.ss(v, v.tensor_copy(io["fcell"][0:1, 0:1], io["eps"][0:1, :]))
                        v.tensor_mul(wbf[:], wint[:], sgub[:]).then_inc(p.sem("gudq"), 1)
                        p.inc("gudq")

                    b_gm = p.val("gumm")
                    b_ge = p.val("guev")
                    b_hs = p.val("guhs")
                    b_sg = p.val("gusg")

                    @blk.tensor
                    def _(t, half=half):
                        t.wait_ge(p.sem("gudq"), p.val("gudq"))
                        t.wait_ge(p.sem("guzll"), p.val("guzll"))
                        for ts in range(4):
                            for c in range(DC):
                                t.matmul(psq[ts][:], wbf[:, c, :],
                                         xnT[:, c * T + ts * 512: c * T + ts * 512 + 512],
                                         start=(c == 0), stop=False)
                            t.matmul(psq[ts][:], zsl[:],
                                     sigT[:, ts * 512:(ts + 1) * 512],
                                     start=False, stop=True)
                        pe_fence(p, t, io, trash, "gumm")
                        if half == 1:
                            for ts in range(4):
                                t.wait_ge(p.sem("guev"), b_ge + ts + 1)
                                t.matmul(psn[:], io["onescol"][:], hT[:],
                                         start=True, stop=True)
                                pe_fence(p, t, io, trash, "gusg")

                    if half == 0:
                        @blk.scalar
                        def _(sc):
                            sc.wait_ge(p.sem("gumm"), b_gm + 1)
                            for ts in range(4):
                                sc.activation(gT[:, ts * 512:(ts + 1) * 512],
                                              psq[ts][:], AF.Silu)
                    else:
                        @blk.vector
                        def _(v, i86=i86):
                            for ts in range(4):
                                v.wait_ge(p.sem("gumm"), b_gm + 1)
                                if ts > 0:
                                    v.wait_ge(p.sem("guhs"), b_hs + 16 * ts)
                                    v.wait_ge(p.sem("gusg"), b_sg + ts)
                                v.tensor_mul(hT[:], psq[ts][:],
                                             gT[:, ts * 512:(ts + 1) * 512]
                                             ).then_inc(p.sem("guev"), 1)
                                p.inc("guev")
                                v.wait_ge(p.sem("gusg"), b_sg + ts + 1)
                                v.tensor_copy(sgrow[:, ts * 512:(ts + 1) * 512],
                                              psn[:]).then_inc(p.sem("gusge"), 1)
                                p.inc("gusge")

                        @blk.sync
                        def _(sy, i86=i86):
                            for ts in range(4):
                                sy.wait_ge(p.sem("guev"), b_ge + ts + 1)
                                dma(sy, p, "guhs",
                                    h_dst[i86 * 128:(i86 + 1) * 128,
                                          ts * 512:(ts + 1) * 512], hT[:])
                            sy.wait_ge(p.sem("gusge"), p.val("gusge"))
                            dma(sy, p, "gusgs", sgh_hbm[i86:i86 + 1, :], sgrow[:])
                            sy.wait_ge(p.sem("guhs"), p.val("guhs"))
                            sy.wait_ge(p.sem("gusgs"), p.val("gusgs"))


def emit_l1_down(p, io, qw_dn, sc_dn, h_src, sghT, ffn_dst, zs_dn):
    nc = p.nc
    INTER = 11008
    IC = INTER // 128
    with contextlib.ExitStack() as st:
        wmov = st.enter_context(nc.sbuf_tensor("dnw", [128, IC, 512], BF16))
        cod = [st.enter_context(nc.sbuf_tensor(f"dnc{i}", [128, 64], I32))
               for i in range(2)]
        uint = [st.enter_context(nc.sbuf_tensor(f"dnu{i}", [128, 512], I32))
                for i in range(2)]
        sbc = [st.enter_context(nc.sbuf_tensor(f"dns{i}", [128, 512], F32))
               for i in range(2)]
        zdl = st.enter_context(nc.sbuf_tensor("dnzs", [IC, 512], BF16))
        acol = [st.enter_context(nc.sbuf_tensor(f"dna{i}", [128, IC, 128], BF16))
                for i in range(2)]
        evb = [st.enter_context(nc.sbuf_tensor(f"dne{i}", [128, 512], BF16))
               for i in range(2)]
        psq = [st.enter_context(nc.psum_tensor(f"dnp{i}", [128, 512], F32))
               for i in range(2)]
        trash = st.enter_context(nc.psum_tensor("dntr", [128, 512], F32))
        for os_ in range(8):
            with nc.Block() as blk:
                b_cl = [p.val("dncl0"), p.val("dncl1")]
                b_dq = p.val("dndq")

                @blk.gpsimd
                def _(g, os_=os_):
                    for c in range(IC):
                        g.wait_ge(p.sem("dndq"), max(0, b_dq + c - 1))
                        dma(g, p, "dncl" + str(c % 2), cod[c % 2][:],
                            qw_dn[c * 128:(c + 1) * 128, os_ * 64:(os_ + 1) * 64])
                        dma(g, p, "dncl" + str(c % 2), sbc[c % 2][:],
                            sc_dn[c:c + 1, os_ * 512:(os_ + 1) * 512]
                            .to_broadcast((128, 512)))

                @blk.sync
                def _(sy, os_=os_):
                    dma(sy, p, "dnzl", zdl[:], zs_dn[:, os_ * 512:(os_ + 1) * 512])

                @blk.vector
                def _(v):
                    for c in range(IC):
                        v.wait_ge(p.sem("dncl" + str(c % 2)),
                                  b_cl[c % 2] + 32 * (c // 2 + 1))
                        for j in range(8):
                            v.tensor_scalar(uint[c % 2][:, j::8], cod[c % 2][:],
                                            4 * j, 15,
                                            ALU.logical_shift_right, ALU.bitwise_and)
                        p.ss(v, v.tensor_copy(io["fcell"][0:1, 0:1], io["eps"][0:1, :]))
                        v.tensor_mul(wmov[:, c, :], uint[c % 2][:],
                                     sbc[c % 2][:]).then_inc(p.sem("dndq"), 1)
                        p.inc("dndq")

                b_mm = p.val("dnmm")
                b_ev = p.val("dnev")
                b_al = [p.val("dnal0"), p.val("dnal1")]
                b_sp = [p.val("dnsp0"), p.val("dnsp1")]
                dq_target = p.val("dndq")

                @blk.sync
                def _(sy, os_=os_):
                    for tt in range(NT):
                        if tt == 0:
                            load_colblock(sy, p, "dnal0", acol[0],
                                          h_src, 0, 128, INTER)
                            load_colblock(sy, p, "dnal1", acol[1],
                                          h_src, 128, 128, INTER)
                        if tt + 2 < NT:
                            sy.wait_ge(p.sem("dnmm"), b_mm + tt + 1)
                            load_colblock(sy, p, "dnal" + str(tt % 2), acol[tt % 2],
                                          h_src, (tt + 2) * 128, 128, INTER)
                        sy.wait_ge(p.sem("dnev"), b_ev + tt + 1)
                        dma(sy, p, "dnsp" + str(tt % 2),
                            ffn_dst[tt * 128:(tt + 1) * 128,
                                    os_ * 512:(os_ + 1) * 512],
                            evb[tt % 2][:])
                    sy.wait_ge(p.sem("dnsp0"), p.val("dnsp0"))
                    sy.wait_ge(p.sem("dnsp1"), p.val("dnsp1"))

                nload = 11 * 16

                @blk.tensor
                def _(t, os_=os_):
                    t.wait_ge(p.sem("dndq"), dq_target)
                    t.wait_ge(p.sem("dnzl"), p.val("dnzl"))
                    for tt in range(NT):
                        t.wait_ge(p.sem("dnal" + str(tt % 2)),
                                  b_al[tt % 2] + nload * (tt // 2 + 1))
                        if tt >= 2:
                            t.wait_ge(p.sem("dnev"), b_ev + tt - 1)
                        ps = psq[tt % 2]
                        for c in range(IC):
                            t.matmul(ps[:], acol[tt % 2][:, c, :],
                                     wmov[:, c, :], start=(c == 0), stop=False)
                        t.matmul(ps[:], sghT[:, tt * 128:(tt + 1) * 128],
                                 zdl[:], start=False, stop=True)
                        pe_fence(p, t, io, trash, "dnmm")

                @blk.vector
                def _(v):
                    for tt in range(NT):
                        v.wait_ge(p.sem("dnmm"), b_mm + tt + 1)
                        if tt >= 2:
                            v.wait_ge(p.sem("dnsp" + str(tt % 2)),
                                      b_sp[tt % 2] + 16 * (tt // 2))
                        v.tensor_copy(evb[tt % 2][:],
                                      psq[tt % 2][:]).then_inc(p.sem("dnev"), 1)
                        p.inc("dnev")


def build_l1(debug=(), upto="full", fuse=None):
    if fuse is None:
        nc = bass.Bass("TRN2", target_bir_lowering=False)
        p = Prog(nc)
    else:
        nc = NcPfx(fuse["nc"], "zB")
        p = fuse["p"]
        p.nc = nc

    NKV = 8
    DKV = 1024
    INTER = 11008
    IC = INTER // 128

    if fuse is None:
        x1 = nc.dram_tensor("x1", [T, D], F32, kind="ExternalInput").ap()
    else:
        x1 = fuse["x1"]
    q_codes = nc.dram_tensor("q_codes", [D, D], I32, kind="ExternalInput").ap()
    q_scales = nc.dram_tensor("q_scales", [32, D], F32, kind="ExternalInput").ap()
    k_codes = nc.dram_tensor("k_codes", [D, DKV], I32, kind="ExternalInput").ap()
    k_scales = nc.dram_tensor("k_scales", [32, DKV], F32, kind="ExternalInput").ap()
    v_codes = nc.dram_tensor("v_codes", [D, DKV], I32, kind="ExternalInput").ap()
    v_scales = nc.dram_tensor("v_scales", [32, DKV], F32, kind="ExternalInput").ap()
    o_codes = nc.dram_tensor("o_codes", [D, D], I32, kind="ExternalInput").ap()
    o_scales = nc.dram_tensor("o_scales", [32, D], F32, kind="ExternalInput").ap()
    qw_gu = nc.dram_tensor("qw_gu", [D, 2 * INTER // 8], I32, kind="ExternalInput").ap()
    qz_gu = nc.dram_tensor("qz_gu", [32, 2 * INTER // 8], I32, kind="ExternalInput").ap()
    sc_gu = nc.dram_tensor("sc_gu", [32, 2 * INTER], F32, kind="ExternalInput").ap()
    qw_dn = nc.dram_tensor("qw_dn", [INTER, D // 8], I32, kind="ExternalInput").ap()
    qz_dn = nc.dram_tensor("qz_dn", [IC, D // 8], I32, kind="ExternalInput").ap()
    sc_dn = nc.dram_tensor("sc_dn", [IC, D], F32, kind="ExternalInput").ap()
    masterT = nc.dram_tensor("masterT", [128, MST_W], BF16, kind="ExternalInput").ap()
    rope_cos = nc.dram_tensor("rope_cos", [T, 64], F32, kind="ExternalInput").ap()
    rope_sin = nc.dram_tensor("rope_sin", [T, 64], F32, kind="ExternalInput").ap()

    out = nc.dram_tensor("out", [T, D], F32, kind="ExternalOutput").ap()
    dbg = {}
    for name, shp, dt_ in (("xnT", [128, DC * T], BF16), ("qt", [D, T], BF16),
                           ("kt", [DKV, T], BF16), ("v", [T, DKV], BF16),
                           ("attnT", [D, T], BF16), ("ffn", [T, D], BF16),
                           ("atno", [T, D], BF16), ("h", [INTER, T], BF16),
                           ("sgo", [32, T], BF16)):
        if name in debug:
            dbg[name] = nc.dram_tensor("dbg_" + name, shp, dt_,
                                       kind="ExternalOutput").ap()

    qt_s = nc.dram_tensor("qt_s", [D, T], BF16, kind="Internal").ap()
    ag_k_in = nc.dram_tensor("ag_k_in", [DKV, T], BF16, kind="Internal")
    ag_k_out = nc.dram_tensor("ag_k_out", [2 * DKV, T], BF16, kind="Internal")
    ag_vi = nc.dram_tensor("ag_vi", [T, DKV], BF16, kind="Internal")
    ag_vo = nc.dram_tensor("ag_vo", [S, DKV], BF16, kind="Internal")
    attn_s = nc.dram_tensor("attn_s", [D, T], BF16, kind="Internal").ap()
    ffn_s = nc.dram_tensor("ffn_s", [T, D], BF16, kind="Internal").ap()
    atno_s = nc.dram_tensor("atno_s", [T, D], BF16, kind="Internal").ap()
    h_s = nc.dram_tensor("h_s", [INTER, T], BF16, kind="Internal").ap()

    gctx = contextlib.ExitStack()
    if fuse is None:
        io = _mk_io(nc, gctx)
    else:
        io = fuse["io"]
    sigT = gctx.enter_context(nc.sbuf_tensor("sigT", [32, T], BF16))
    sgo_hbm = nc.dram_tensor("sgo_hbm", [32, T], F32, kind="Internal").ap()
    sgh_hbm = nc.dram_tensor("sgh_hbm", [IC, T], F32, kind="Internal").ap()

    with gctx:
        if fuse is None:
            emit_init_consts(p, io)
        zs_gu_hbm, zs_dn_hbm = emit_l1_zs(p, io, qz_gu, sc_gu, qz_dn, sc_dn)

        def make_s8(st_, name, src_sc, width):
            dst = st_.enter_context(nc.sbuf_tensor(name, [32, width], BF16))
            with contextlib.ExitStack() as st2:
                tmp = st2.enter_context(nc.sbuf_tensor(name + "t", [32, width], F32))
                with nc.Block() as blk:
                    @blk.sync
                    def _(sy):
                        dma(sy, p, "s8l", tmp[:], src_sc)
                        sy.wait_ge(p.sem("s8l"), p.val("s8l"))
                with nc.Block() as blk:
                    @blk.vector
                    def _(v):
                        v.tensor_scalar(dst[:], tmp[:], -8.0, None, ALU.mult)
            return dst

        r1 = contextlib.ExitStack()
        xnT = r1.enter_context(nc.sbuf_tensor("xnT", [128, DC * T], BF16))
        sgtok = r1.enter_context(nc.sbuf_tensor("sgtok", [128, NT * 32], F32))
        with r1:
            def sig_cb(v, tt, xnb):
                v.tensor_reduce(sgtok[:, tt * 32:(tt + 1) * 32],
                                xnb[:].rearrange("p (c f) -> p c f", f=128),
                                axis=AX.X, op=ALU.add)

            emit_ln_xnt(p, io, x1, xnT, "ln", sig_cb=sig_cb)
            with contextlib.ExitStack() as st:
                sgbf = st.enter_context(nc.sbuf_tensor("sgbf", [128, NT * 32], BF16))
                pstr = st.enter_context(nc.psum_tensor("sgps", [32, 128], BF16))
                trs = st.enter_context(nc.psum_tensor("sgtr", [128, 512], F32))
                with nc.Block() as blk:
                    @blk.vector
                    def _(v):
                        v.tensor_copy(sgbf[:], sgtok[:])
                for tt in range(NT):
                    with nc.Block() as blk:
                        @blk.tensor
                        def _(t):
                            t.transpose(pstr[:], sgbf[:, tt * 32:(tt + 1) * 32],
                                        io["ident"][:])
                            pe_fence(p, t, io, trs, "sgtr")

                        @blk.vector
                        def _(v):
                            v.wait_ge(p.sem("sgtr"), p.val("sgtr"))
                            v.tensor_copy(sigT[:, tt * 128:(tt + 1) * 128], pstr[:])
            if "xnT" in dbg:
                _copy_hbm(p, dbg["xnT"], xnT[:], sbuf=True)
            if upto == "ln":
                return nc

            s8q = make_s8(r1, "s8q", q_scales, D)
            s8k = make_s8(r1, "s8k", k_scales, DKV)
            s8v = make_s8(r1, "s8v", v_scales, DKV)
            which = "qkv"
            if upto.startswith("m_"):
                which = upto[2:]
            emit_l1_qkv(p, io, xnT, sigT, q_codes, q_scales, k_codes, k_scales,
                        v_codes, v_scales, s8q, s8k, s8v, rope_cos, rope_sin,
                        qt_s, ag_k_in.ap(), ag_vi.ap(), which=which)
            if upto.startswith("m_"):
                if "qt" in dbg and "q" in which:
                    _copy_hbm(p, dbg["qt"], qt_s)
                if "kt" in dbg and "k" in which:
                    _copy_hbm(p, dbg["kt"], ag_k_in.ap())
                if "v" in dbg and "v" in which:
                    _copy_hbm(p, dbg["v"], ag_vi.ap())
                return nc
            if upto == "qonly":
                if "qt" in dbg:
                    _copy_hbm(p, dbg["qt"], qt_s)
                if "kt" in dbg:
                    _copy_hbm(p, dbg["kt"], ag_k_in.ap())
                if "v" in dbg:
                    _copy_hbm(p, dbg["v"], ag_vi.ap())
                return nc
            with nc.Block() as blk:
                @blk.gpsimd
                def _(g):
                    g.collective_compute(
                        "AllGather", ALU.bypass, replica_groups=REPLICA_PAIRS,
                        ins=[ag_k_in[:]], outs=[ag_k_out[:]],
                    ).then_inc(p.sem("agk"), 1)
                    p.inc("agk")
                    g.collective_compute(
                        "AllGather", ALU.bypass, replica_groups=REPLICA_PAIRS,
                        ins=[ag_vi[:]], outs=[ag_vo[:]],
                    ).then_inc(p.sem("agv"), 1)
                    p.inc("agv")
            if "qt" in dbg:
                _copy_hbm(p, dbg["qt"], qt_s)
            if upto == "qkv":
                with nc.Block() as blk:
                    @blk.gpsimd
                    def _(g):
                        g.wait_ge(p.sem("agk"), p.val("agk"))
                        g.wait_ge(p.sem("agv"), p.val("agv"))
                if "kt" in dbg:
                    _copy_hbm(p, dbg["kt"], ag_k_out.ap()[0:DKV, :])
                if "v" in dbg:
                    _copy_hbm(p, dbg["v"], ag_vo.ap()[0:T, :])
                return nc

            if upto != "attn1":
                emit_l1_gu(p, io, xnT, sigT, qw_gu, sc_gu, h_s, sgh_hbm, zs_gu_hbm)
        if "h" in dbg:
            _copy_hbm(p, dbg["h"], h_s)
        if upto == "gu":
            return nc

        if upto == "attn1":
            class _Skip: pass
        with contextlib.ExitStack() as rdn:
            sghT = rdn.enter_context(nc.sbuf_tensor("sghT", [IC, T], BF16))
            if upto != "attn1":
                with nc.Block() as blk:
                    @blk.gpsimd
                    def _(g):
                        dma(g, p, "sghl", sghT[:], sgh_hbm)
                        g.wait_ge(p.sem("sghl"), p.val("sghl"))
                emit_l1_down(p, io, qw_dn, sc_dn, h_s, sghT, ffn_s, zs_dn_hbm)
        if "ffn" in dbg:
            _copy_hbm(p, dbg["ffn"], ffn_s)
        if upto == "ffn":
            return nc

        with contextlib.ExitStack() as r2:
            mst_sb = r2.enter_context(nc.sbuf_tensor("mst", [128, MST_W], BF16))
            with nc.Block() as blk:
                @blk.sync
                def _(sy):
                    dma(sy, p, "mstl", mst_sb[:], masterT)
                    sy.wait_ge(p.sem("mstl"), p.val("mstl"))

                @blk.gpsimd
                def _(g):
                    g.wait_ge(p.sem("agk"), p.val("agk"))
                    g.wait_ge(p.sem("agv"), p.val("agv"))
            if "kt" in dbg:
                _copy_hbm(p, dbg["kt"], ag_k_out.ap()[0:DKV, :])
            if "v" in dbg:
                _copy_hbm(p, dbg["v"], ag_vo.ap()[0:T, :])
            cfg = dict(n_qh=32, kv_per=4, dkv=DKV, scale=SCALE1, alibi=False,
                       tag="a1")
            emit_attention(p, io, cfg, qt_s, ag_k_out.ap(), ag_vo.ap(),
                           attn_s, mst_sb, sgoT=sgo_hbm)
        if "attnT" in dbg:
            _copy_hbm(p, dbg["attnT"], attn_s)
        if upto in ("attn", "attn1"):
            return nc

        with contextlib.ExitStack() as r3:
            s8o = make_s8(r3, "s8o", o_scales, D)
            sgoT = r3.enter_context(nc.sbuf_tensor("sgoT", [32, T], BF16))
            with nc.Block() as blk:
                @blk.gpsimd
                def _(g):
                    dma(g, p, "sgol", sgoT[:], sgo_hbm)
                    g.wait_ge(p.sem("sgol"), p.val("sgol"))
            codq = [r3.enter_context(nc.sbuf_tensor(f"oqc{i}", [128, 512], I32))
                    for i in range(2)]
            sbcq = [r3.enter_context(nc.sbuf_tensor(f"oqs{i}", [128, 512], F32))
                    for i in range(2)]

            def wo_load(blk, os_, buf):
                emit_dq_cs_strip(p, blk, io, o_codes, o_scales, os_ * 512, 512,
                                 buf, "podq", (codq, sbcq), None)
                tgt = p.val("podqdq")

                @blk.gpsimd
                def _(g):
                    g.wait_ge(p.sem("podqdq"), tgt)
                    g.sem_inc(p.sem("powrdy"), 1)
                    p.inc("powrdy")

            emit_oproj(p, io, attn_s, atno_s, "po", wo_load,
                       corr=(sgoT, s8o))
        if "atno" in dbg:
            _copy_hbm(p, dbg["atno"], atno_s)

        emit_epilogue(p, x1, atno_s, ffn_s, out, "ep")
    return nc


def load_colblock(eng, p, sname, dst3, src_hbm, col0, ncols, nrows, max_c=8):
    nch = nrows // 128
    src = src_hbm[:, col0:col0 + ncols].rearrange("(c p) f -> p c f", p=128)
    c0 = 0
    while c0 < nch:
        c1 = min(c0 + max_c, nch)
        dma(eng, p, sname, dst3[:, c0:c1, :], src[:, c0:c1, :])
        c0 = c1


def emit_l1_qkv(p, io, xnT, sigT, q_codes, q_scales, k_codes, k_scales,
                v_codes, v_scales, s8q, s8k, s8v, rope_cos, rope_sin,
                qt_dst, agk_dst, agv_dst, which="qkv"):
    nc = p.nc
    with contextlib.ExitStack() as st:
        wmov = st.enter_context(nc.sbuf_tensor("qkw", [128, DC, 512], BF16))
        cod = [st.enter_context(nc.sbuf_tensor(f"qkc{i}", [128, 512], I32))
               for i in range(2)]
        sbc = [st.enter_context(nc.sbuf_tensor(f"qks{i}", [128, 512], F32))
               for i in range(2)]
        cosb = [st.enter_context(nc.sbuf_tensor(f"qkcos{i}", [128, 256], F32))
                for i in range(2)]
        sinb = [st.enter_context(nc.sbuf_tensor(f"qksin{i}", [128, 256], F32))
                for i in range(2)]
        rt1 = st.enter_context(nc.sbuf_tensor("qkrt1", [128, 256], F32))
        rt2 = st.enter_context(nc.sbuf_tensor("qkrt2", [128, 256], F32))
        evb = [st.enter_context(nc.sbuf_tensor(f"qke{i}", [128, 512], BF16))
               for i in range(2)]
        stg = [st.enter_context(nc.sbuf_tensor(f"qkg{i}", [128, 512], BF16))
               for i in range(2)]
        psq = [st.enter_context(nc.psum_tensor(f"qkp{i}", [128, 512], F32))
               for i in range(2)]
        tps = [st.enter_context(nc.psum_tensor(f"qkt{i}", [128, 128], BF16))
               for i in range(2)]
        trash = st.enter_context(nc.psum_tensor("qktr", [128, 512], F32))

        mats = [("q", q_codes, q_scales, s8q, 8, "rope_t"),
                ("k", k_codes, k_scales, s8k, 2, "rope_t"),
                ("v", v_codes, v_scales, s8v, 2, "plain")]
        if which == "K":
            mats = [("v", k_codes, k_scales, s8k, 2, "plain")]
        elif which == "R":
            mats = [("v", k_codes, k_scales, s8k, 2, "rope_p")]
        else:
            mats = [m for m in mats if m[0] in which]
        for mname, codes, scales, s8x, nfs, mode in mats:
            for fs in range(nfs):
                with nc.Block() as blk:
                    emit_dq_cs_strip(p, blk, io, codes, scales, fs * 512, 512,
                                     wmov, "xdq", (cod, sbc), None)
                    dq_tgt = p.val("xdqdq")
                    b_mm = p.val("qmm")
                    b_rp = p.val("qrp")
                    b_tn = p.val("qtn")
                    b_cp = p.val("qcp")
                    b_cs = [p.val("qcs0"), p.val("qcs1")]
                    b_sp = [p.val("qsp0"), p.val("qsp1")]

                    @blk.gpsimd
                    def _(g, mode=mode):
                        if mode.startswith("rope"):
                            for tt in range(NT):
                                if tt >= 2:
                                    g.wait_ge(p.sem("qrp"), b_rp + tt - 1)
                                csrc = rope_cos[tt * 128:(tt + 1) * 128, :]
                                ssrc = rope_sin[tt * 128:(tt + 1) * 128, :]
                                dma(g, p, "qcs" + str(tt % 2),
                                    cosb[tt % 2][:].rearrange("p (r f) -> p r f", f=64),
                                    csrc.unsqueeze(1).broadcast_to((128, 4, 64)))
                                dma(g, p, "qcs" + str(tt % 2),
                                    sinb[tt % 2][:].rearrange("p (r f) -> p r f", f=64),
                                    ssrc.unsqueeze(1).broadcast_to((128, 4, 64)))

                    def emit_trans(t, tt):
                        for j in range(4):
                            n = 4 * tt + j
                            if n >= 2:
                                t.wait_ge(p.sem("qcp"), b_cp + n - 1)
                            t.transpose(tps[j % 2][:],
                                        evb[tt % 2][:, j * 128:(j + 1) * 128],
                                        io["ident"][:]).then_inc(p.sem("qtn"), 1)
                            p.inc("qtn")
                        pe_fence(p, t, io, trash, "qtn")

                    @blk.tensor
                    def _(t, s8x=s8x, fs=fs, mode=mode):
                        t.wait_ge(p.sem("xdqdq"), dq_tgt)
                        for tt in range(NT):
                            if tt >= 2:
                                t.wait_ge(p.sem("qrp"), b_rp + tt - 1)
                            ps = psq[tt % 2]
                            for c in range(DC):
                                t.matmul(ps[:], xnT[:, c * T + tt * 128: c * T + tt * 128 + 128],
                                         wmov[:, c, :], start=(c == 0), stop=False)
                            t.matmul(ps[:], sigT[:, tt * 128:(tt + 1) * 128],
                                     s8x[:, fs * 512:(fs + 1) * 512],
                                     start=False, stop=True)
                            pe_fence(p, t, io, trash, "qmm")
                            if mode == "rope_t" and tt >= 1:
                                t.wait_ge(p.sem("qrp"), b_rp + tt)
                                emit_trans(t, tt - 1)
                        if mode == "rope_t":
                            t.wait_ge(p.sem("qrp"), b_rp + NT)
                            emit_trans(t, NT - 1)

                    def emit_copies(v, tt):
                        for j in range(4):
                            v.wait_ge(p.sem("qtn"), b_tn + 5 * tt + j + 2)
                            sl = (tt * 4 + j) % 2
                            if tt * 4 + j >= 2:
                                v.wait_ge(p.sem("qsp" + str(sl)),
                                          b_sp[sl] + 16 * ((tt * 4 + j) // 2))
                            v.tensor_copy(stg[sl][:, 0:128],
                                          tps[j % 2][:]).then_inc(p.sem("qcp"), 1)
                            p.inc("qcp")

                    @blk.vector
                    def _(v, mode=mode):
                        for tt in range(NT):
                            v.wait_ge(p.sem("qmm"), b_mm + tt + 1)
                            ps = psq[tt % 2]
                            eb = evb[tt % 2]
                            if mode.startswith("rope"):
                                v.wait_ge(p.sem("qcs" + str(tt % 2)),
                                          b_cs[tt % 2] + 32 * (tt // 2 + 1))
                                if mode == "rope_t" and tt >= 2:
                                    v.wait_ge(p.sem("qtn"), b_tn + 5 * (tt - 1))
                                if mode == "rope_p" and tt >= 2:
                                    v.wait_ge(p.sem("qsp" + str(tt % 2)),
                                              b_sp[tt % 2] + 16 * (tt // 2))
                                v.wait_ge(p.sem("qrp"), p.val("qrp"))
                                pse = ps[:].rearrange("p (a two) -> p two a", two=2)[:, 0, :]
                                pso = ps[:].rearrange("p (a two) -> p two a", two=2)[:, 1, :]
                                p.ss(v, v.tensor_mul(rt1[:], pse, cosb[tt % 2][:]))
                                p.ss(v, v.tensor_mul(rt2[:], pso, sinb[tt % 2][:]))
                                ebe = eb[:].rearrange("p (a two) -> p two a", two=2)[:, 0, :]
                                ebo = eb[:].rearrange("p (a two) -> p two a", two=2)[:, 1, :]
                                p.ss(v, v.tensor_sub(ebe, rt1[:], rt2[:]))
                                p.ss(v, v.tensor_mul(rt1[:], pso, cosb[tt % 2][:]))
                                p.ss(v, v.tensor_mul(rt2[:], pse, sinb[tt % 2][:]))
                                ins = v.tensor_add(ebo, rt1[:], rt2[:])
                                ins.then_inc(p.sem("qrp"), 1)
                                p.inc("qrp")
                                if mode == "rope_t" and tt >= 1:
                                    emit_copies(v, tt - 1)
                            else:
                                if tt >= 2:
                                    v.wait_ge(p.sem("qsp" + str(tt % 2)),
                                              b_sp[tt % 2] + 16 * (tt // 2))
                                v.tensor_copy(eb[:], ps[:]).then_inc(p.sem("qrp"), 1)
                                p.inc("qrp")
                        if mode == "rope_t":
                            emit_copies(v, NT - 1)

                    if mode == "rope_t":
                        @blk.sync
                        def _(sy, mname=mname, fs=fs):
                            dst = qt_dst if mname == "q" else agk_dst
                            for tt in range(NT):
                                for j in range(4):
                                    sy.wait_ge(p.sem("qcp"), b_cp + 4 * tt + j + 1)
                                    sl = (tt * 4 + j) % 2
                                    r0 = fs * 512 + j * 128
                                    dma(sy, p, "qsp" + str(sl),
                                        dst[r0:r0 + 128, tt * 128:(tt + 1) * 128],
                                        stg[sl][:, 0:128])
                            sy.wait_ge(p.sem("qsp0"), p.val("qsp0"))
                            sy.wait_ge(p.sem("qsp1"), p.val("qsp1"))
                    if mode != "rope_t":
                        @blk.sync
                        def _(sy, fs=fs):
                            for tt in range(NT):
                                sy.wait_ge(p.sem("qrp"), b_rp + tt + 1)
                                dma(sy, p, "qsp" + str(tt % 2),
                                    agv_dst[tt * 128:(tt + 1) * 128,
                                            fs * 512:(fs + 1) * 512],
                                    evb[tt % 2][:])
                            sy.wait_ge(p.sem("qsp0"), p.val("qsp0"))
                            sy.wait_ge(p.sem("qsp1"), p.val("qsp1"))



# version 07
# speedup vs baseline: 1.1522x; 1.1522x over previous
import contextlib
import math
import numpy as np
import ml_dtypes

import concourse.bass as bass
import concourse.mybir as mybir
from concourse.bass_utils import run_bass_kernel_spmd

F32 = mybir.dt.float32
BF16 = mybir.dt.bfloat16
I32 = mybir.dt.int32
AF = mybir.ActivationFunctionType
ALU = mybir.AluOpType
AX = mybir.AxisListType
bfdt = ml_dtypes.bfloat16

NCORES = 8
T = 2048
D = 4096
NT = T // 128
DC = D // 128
S = 4096
NEGBIG = -3.0e38
LN_EPS = 1e-5
SCALE0 = 1.0 / 128.0
SCALE1 = 128.0 ** -0.5
MST_W = 6016
MST_OFF = 3968

REPLICA_PAIRS = [[0, 1], [2, 3], [4, 5], [6, 7]]


def _alibi_slopes(n_heads):
    cp2 = 2 ** math.floor(math.log2(n_heads))
    base = 2 ** (-(2 ** (-(math.log2(cp2) - 3))))
    slopes = base ** np.arange(1, cp2 + 1, dtype=np.float32)
    if cp2 != n_heads:
        extra_base = 2 ** (-(2 ** (-(math.log2(2 * cp2) - 3))))
        n_rem = min(cp2, n_heads - cp2)
        extra = extra_base ** np.arange(1, 1 + 2 * n_rem, 2, dtype=np.float32)
        slopes = np.concatenate([slopes, extra])
    return slopes.astype(np.float32)


SEM_ALIAS = (("a1", "a0"), ("pk", "pq"), ("py", "pv"), ("bcl2", "bcl"),
             ("xdq", "podq"), ("sgol", "sghl"), ("s8l", "bcl"))


def _sem_alias(name):
    for a, b in SEM_ALIAS:
        if name.startswith(a):
            return b + name[len(a):]
    return name


class Prog:
    def __init__(self, nc):
        self.nc = nc
        self._sems = {}
        self.ctr = {}

    def sem(self, name):
        name = _sem_alias(name)
        if name not in self._sems:
            self._sems[name] = self.nc.alloc_semaphore(name)
            self.ctr[name] = 0
        return self._sems[name]

    def inc(self, name, n=1):
        name = _sem_alias(name)
        self.sem(name)
        self.ctr[name] += n
        return self.ctr[name]

    def val(self, name):
        name = _sem_alias(name)
        self.sem(name)
        return self.ctr[name]

    def ss(self, eng, ins, name="zss"):
        ins.then_inc(self.sem(name), 1)
        eng.wait_ge(self.sem(name), self.inc(name))
        return ins


class NcPfx:
    def __init__(self, nc, pfx):
        self._nc = nc
        self._pfx = pfx

    def sbuf_tensor(self, name, *a, **k):
        return self._nc.sbuf_tensor(self._pfx + name, *a, **k)

    def psum_tensor(self, name, *a, **k):
        return self._nc.psum_tensor(self._pfx + name, *a, **k)

    def dram_tensor(self, name, *a, **k):
        return self._nc.dram_tensor(self._pfx + name, *a, **k)

    def __getattr__(self, n):
        return getattr(self._nc, n)


def _mk_io(nc, stack):
    io = {}
    io["ident"] = stack.enter_context(nc.sbuf_tensor("ident", [128, 128], BF16))
    io["ones2"] = stack.enter_context(nc.sbuf_tensor("ones2", [2, 128], BF16))
    io["onescol"] = stack.enter_context(nc.sbuf_tensor("onescol", [128, 1], BF16))
    io["iot"] = stack.enter_context(nc.sbuf_tensor("iot", [128, 128], F32))
    io["eps"] = stack.enter_context(nc.sbuf_tensor("epsc", [128, 1], F32))
    io["eps12"] = stack.enter_context(nc.sbuf_tensor("eps12", [128, 1], F32))
    io["orow"] = stack.enter_context(nc.sbuf_tensor("orow", [1, 512], BF16))
    io["fcell"] = stack.enter_context(nc.sbuf_tensor("fcell", [1, 4], F32))
    return io


def pe_fence(p, t, io, trash, sname=None):
    ins = t.matmul(trash[:, 0:192], io["orow"][0:1, 0:128],
                   io["orow"][0:1, 0:192], start=True, stop=True)
    if sname is not None:
        ins.then_inc(p.sem(sname), 1)
        p.inc(sname)
    return ins


def dma(eng, p, sname, dst, src):
    eng.dma_start(dst, src).then_inc(p.sem(sname), 16)
    p.inc(sname, 16)


def load_w_colstrip(g, p, sname, dst3, w_hbm, col0, ncols, nrows=D, rsplit=4):
    nch = nrows // 128
    per = nch // rsplit
    src = w_hbm[:, col0:col0 + ncols].rearrange("(c p) f -> p c f", p=128)
    for r in range(rsplit):
        dma(g, p, sname, dst3[:, r * per:(r + 1) * per, :],
            src[:, r * per:(r + 1) * per, :])



def emit_init_consts(p, io):
    nc = p.nc
    with nc.Block() as blk:
        @blk.gpsimd
        def _(g):
            g.iota(io["iot"][:], [[1, 128]], base=0, channel_multiplier=-1,
                   allow_small_or_imprecise_dtypes=True)
            g.memset(io["ones2"][:], 1.0)
            g.memset(io["onescol"][:], 1.0)
            g.memset(io["eps"][:], LN_EPS)
            g.memset(io["eps12"][:], 1e-12)
            g.memset(io["orow"][:], 1.0)
    with nc.Block() as blk:
        @blk.vector
        def _(v):
            v.tensor_scalar(io["ident"][:], io["iot"][:], 0.0, 1.0,
                            ALU.is_equal, ALU.mult)


def emit_ln_xnt(p, io, x_src, xnT, tag, sig_cb=None):
    nc = p.nc
    ld, sc1, v2, sc2, v3, tr, ev = (tag + s for s in
                                    ("ld", "sc1", "v2", "sc2", "v3", "tr", "ev"))
    with contextlib.ExitStack() as st:
        xst = st.enter_context(nc.sbuf_tensor(tag + "x", [128, D], F32))
        xsq = st.enter_context(nc.sbuf_tensor(tag + "sq", [128, D], BF16))
        xnb = st.enter_context(nc.sbuf_tensor(tag + "xn", [128, D], BF16))
        s1 = st.enter_context(nc.sbuf_tensor(tag + "s1", [128, 1], F32))
        s2 = st.enter_context(nc.sbuf_tensor(tag + "s2", [128, 1], F32))
        mu = st.enter_context(nc.sbuf_tensor(tag + "mu", [128, 1], F32))
        msq = st.enter_context(nc.sbuf_tensor(tag + "msq", [128, 1], F32))
        var = st.enter_context(nc.sbuf_tensor(tag + "var", [128, 1], F32))
        sd = st.enter_context(nc.sbuf_tensor(tag + "sd", [128, 1], F32))
        rsd = st.enter_context(nc.sbuf_tensor(tag + "rsd", [128, 1], F32))
        ps0 = st.enter_context(nc.psum_tensor(tag + "p0", [128, 128], BF16))
        ps1 = st.enter_context(nc.psum_tensor(tag + "p1", [128, 128], BF16))
        pst = [ps0, ps1]
        for tt in range(NT):
            with nc.Block() as blk:
                @blk.sync
                def _(sy):
                    dma(sy, p, ld, xst[:], x_src[tt * 128:(tt + 1) * 128, :])

                @blk.scalar
                def _(sc):
                    sc.wait_ge(p.sem(ld), p.val(ld))
                    sc.activation(xsq[:], xst[:], AF.Square,
                                  accum_out=s2[:]).then_inc(p.sem(sc1), 1)
                    p.inc(sc1)

                @blk.vector
                def _(v):
                    v.wait_ge(p.sem(ld), p.val(ld))
                    p.ss(v, v.reduce_sum(s1[:], xst[:], axis=AX.X))
                    p.ss(v, v.tensor_scalar(mu[:], s1[:], 1.0 / D, None, ALU.mult))
                    p.ss(v, v.tensor_mul(msq[:], mu[:], mu[:]))
                    v.wait_ge(p.sem(sc1), p.val(sc1))
                    p.ss(v, v.tensor_scalar(var[:], s2[:], 1.0 / D, None, ALU.mult))
                    v.tensor_sub(var[:], var[:], msq[:]).then_inc(p.sem(v2), 1)
                    p.inc(v2)

                @blk.scalar
                def _(sc):
                    sc.wait_ge(p.sem(v2), p.val(v2))
                    sc.activation(sd[:], var[:], AF.Sqrt,
                                  bias=io["eps"][:]).then_inc(p.sem(sc2), 1)
                    p.inc(sc2)

                @blk.vector
                def _(v):
                    v.wait_ge(p.sem(sc2), p.val(sc2))
                    p.ss(v, v.reciprocal(rsd[:], sd[:]))
                    ins = v.tensor_scalar(xnb[:], xst[:], mu[:], rsd[:],
                                          ALU.subtract, ALU.mult)
                    if sig_cb is None:
                        ins.then_inc(p.sem(v3), 1)
                        p.inc(v3)
                    else:
                        p.ss(v, ins)
                        sig_cb(v, tt, xnb)
                        v.tensor_copy(io["fcell"][0:1, 0:1],
                                      io["eps"][0:1, :]).then_inc(p.sem(v3), 1)
                        p.inc(v3)

                base_tr = p.val(tr)
                base_ev = p.val(ev)

                @blk.tensor
                def _(t):
                    t.wait_ge(p.sem(v3), p.val(v3))
                    for c in range(DC):
                        if c >= 2:
                            t.wait_ge(p.sem(ev), base_ev + c - 1)
                        t.transpose(pst[c % 2][:], xnb[:, c * 128:(c + 1) * 128],
                                    io["ident"][:]).then_inc(p.sem(tr), 1)
                        p.inc(tr)
                    t.wait_ge(p.sem(ev), base_ev + DC - 1)
                    t.transpose(pst[0][:], xnb[:, 0:128],
                                io["ident"][:]).then_inc(p.sem(tr), 1)
                    p.inc(tr)

                @blk.vector
                def _(v):
                    for c in range(DC):
                        v.wait_ge(p.sem(tr), base_tr + c + 2)
                        col = c * T + tt * 128
                        v.tensor_copy(xnT[:, col:col + 128],
                                      pst[c % 2][:]).then_inc(p.sem(ev), 1)
                        p.inc(ev)


def emit_featmaj_proj(p, io, xnT, w_hbm, wcol0, nfc, bias_cols, bias_off,
                      dst_hbm, tag, dst_row0=0):
    nc = p.nc
    wl, mm, evq, stq = (tag + s for s in ("wl", "mm", "ev", "st"))
    with contextlib.ExitStack() as st:
        wbuf = [st.enter_context(nc.sbuf_tensor(f"{tag}w{i}", [128, DC, 128], BF16))
                for i in range(2)]
        evb = [st.enter_context(nc.sbuf_tensor(f"{tag}e{i}", [128, T], BF16))
               for i in range(2)]
        psq = [st.enter_context(nc.psum_tensor(f"{tag}p{i}", [128, 512], F32))
               for i in range(4)]
        trash = st.enter_context(nc.psum_tensor(tag + "tr", [128, 512], F32))
        for fc in range(nfc):
            par = fc % 2
            with nc.Block() as blk:
                @blk.gpsimd
                def _(g):
                    if fc == 0:
                        load_w_colstrip(g, p, wl + "0", wbuf[0], w_hbm, wcol0, 128)
                    if fc + 1 < nfc:
                        load_w_colstrip(g, p, wl + str(1 - par), wbuf[1 - par],
                                        w_hbm, wcol0 + (fc + 1) * 128, 128)

                @blk.tensor
                def _(t):
                    t.wait_ge(p.sem(wl + str(par)), p.val(wl + str(par)) -
                              (64 if (fc + 1 < nfc and (1 - par) == par) else 0))
                    for c in range(DC):
                        for ts in range(4):
                            t.matmul(psq[ts][:], wbuf[par][:, c, :],
                                     xnT[:, c * T + ts * 512: c * T + ts * 512 + 512],
                                     start=(c == 0), stop=(c == DC - 1))
                    pe_fence(p, t, io, trash, mm)

                @blk.vector
                def _(v):
                    v.wait_ge(p.sem(mm), p.val(mm))
                    for ts in range(4):
                        if bias_cols is not None:
                            bi = bias_off + fc
                            ins = v.tensor_scalar(evb[par][:, ts * 512:(ts + 1) * 512],
                                                  psq[ts][:],
                                                  bias_cols[:, bi:bi + 1], None,
                                                  ALU.add)
                        else:
                            ins = v.tensor_copy(evb[par][:, ts * 512:(ts + 1) * 512],
                                                psq[ts][:])
                    ins.then_inc(p.sem(evq), 1)
                    p.inc(evq)

                @blk.sync
                def _(sy):
                    sy.wait_ge(p.sem(evq), p.val(evq))
                    r0 = dst_row0 + fc * 128
                    dma(sy, p, stq, dst_hbm[r0:r0 + 128, :], evb[par][:])
                    sy.wait_ge(p.sem(stq), p.val(stq))


def emit_tokmaj_proj(p, io, xnT, w_hbm, wrow_total, wcol0, nfs, bias_bcast,
                     dst_hbm, tag, extra_mm=None, wcast=True):
    nc = p.nc
    wl, mm, evq, stq = (tag + s for s in ("wl", "mm", "ev", "st"))
    with contextlib.ExitStack() as st:
        wmov = st.enter_context(nc.sbuf_tensor(tag + "w", [128, DC, 512], BF16))
        evb = [st.enter_context(nc.sbuf_tensor(f"{tag}e{i}", [128, 512], BF16))
               for i in range(2)]
        psq = [st.enter_context(nc.psum_tensor(f"{tag}p{i}", [128, 512], F32))
               for i in range(2)]
        trash = st.enter_context(nc.psum_tensor(tag + "tr", [128, 512], F32))
        for fs in range(nfs):
            with nc.Block() as blk:
                @blk.gpsimd
                def _(g):
                    load_w_colstrip(g, p, wl, wmov, w_hbm, wcol0 + fs * 512, 512)

                base_mm = p.val(mm)
                base_ev = p.val(evq)
                base_st = [p.val(stq + "0"), p.val(stq + "1")]

                @blk.tensor
                def _(t):
                    t.wait_ge(p.sem(wl), p.val(wl))
                    for tt in range(NT):
                        if tt >= 2:
                            t.wait_ge(p.sem(evq), base_ev + tt - 1)
                        ps = psq[tt % 2]
                        for c in range(DC):
                            t.matmul(ps[:], xnT[:, c * T + tt * 128: c * T + tt * 128 + 128],
                                     wmov[:, c, :],
                                     start=(c == 0),
                                     stop=(c == DC - 1 and extra_mm is None))
                        if extra_mm is not None:
                            extra_mm(t, tt, fs, ps)
                        pe_fence(p, t, io, trash, mm)

                @blk.vector
                def _(v):
                    for tt in range(NT):
                        v.wait_ge(p.sem(mm), base_mm + tt + 1)
                        if tt >= 2:
                            v.wait_ge(p.sem(stq + str(tt % 2)),
                                      base_st[tt % 2] + 16 * (tt // 2))
                        ps = psq[tt % 2]
                        eb = evb[tt % 2]
                        if bias_bcast is not None:
                            ins = v.tensor_add(eb[:], ps[:],
                                               bias_bcast[:, fs * 512:(fs + 1) * 512])
                        else:
                            ins = v.tensor_copy(eb[:], ps[:])
                        ins.then_inc(p.sem(evq), 1)
                        p.inc(evq)

                @blk.sync
                def _(sy):
                    for tt in range(NT):
                        sy.wait_ge(p.sem(evq), base_ev + tt + 1)
                        dma(sy, p, stq + str(tt % 2),
                            dst_hbm[tt * 128:(tt + 1) * 128,
                                    fs * 512:(fs + 1) * 512],
                            evb[tt % 2][:])
                    sy.wait_ge(p.sem(stq + "0"), p.val(stq + "0"))
                    sy.wait_ge(p.sem(stq + "1"), p.val(stq + "1"))


def emit_lora_prep(p, io, st, xnT, a4t, w4t, bblk, rb4, prior4, expmat):
    nc = p.nc
    a4t_sb = st.enter_context(nc.sbuf_tensor("lra", [128, DC, 32], BF16))
    w4t_sb = st.enter_context(nc.sbuf_tensor("lrw4", [128, DC, 4], BF16))
    bblk_sb = st.enter_context(nc.sbuf_tensor("lrb", [32, D], BF16))
    exp_sb = st.enter_context(nc.sbuf_tensor("lrexp", [4, 32], BF16))
    pri_sb = st.enter_context(nc.sbuf_tensor("lrpri", [1, 4], F32))
    rb_sb = st.enter_context(nc.sbuf_tensor("lrrb", [1, 4], F32))
    lp_sb = st.enter_context(nc.sbuf_tensor("lrlp", [1, 4], F32))
    lp_bf = st.enter_context(nc.sbuf_tensor("lrlpb", [1, 4], BF16))
    lp2b = st.enter_context(nc.sbuf_tensor("lrlp2", [128, 4], F32))
    zt_sb = st.enter_context(nc.sbuf_tensor("lrzt", [32, T], BF16))
    lg_sb = st.enter_context(nc.sbuf_tensor("lrlg", [128, NT * 4], F32))
    la_sb = st.enter_context(nc.sbuf_tensor("lrla", [128, 4], F32))
    mx_sb = st.enter_context(nc.sbuf_tensor("lrmx", [128, 1], F32))
    ssum = st.enter_context(nc.sbuf_tensor("lrss", [128, 1], F32))
    rs_sb = st.enter_context(nc.sbuf_tensor("lrrs", [128, 1], F32))
    al_sb = st.enter_context(nc.sbuf_tensor("lral", [128, NT * 4], BF16))
    alT = st.enter_context(nc.sbuf_tensor("lralT", [4, T], BF16))
    zh_sb = st.enter_context(nc.sbuf_tensor("lrzh", [32, T], BF16))

    with nc.Block() as blk:
        @blk.gpsimd
        def _(g):
            load_w_colstrip(g, p, "lrl", a4t_sb, a4t, 0, 32)
            load_w_colstrip(g, p, "lrl", w4t_sb, w4t, 0, 4)
            dma(g, p, "lrl", bblk_sb[:], bblk)
            dma(g, p, "lrl", exp_sb[:], expmat)

        @blk.sync
        def _(sy):
            dma(sy, p, "lrl", pri_sb[:], prior4)
            dma(sy, p, "lrl", rb_sb[:], rb4)

    with contextlib.ExitStack() as psz_st:
        psz = psz_st.enter_context(nc.psum_tensor("lrpz", [32, T], F32))
        trz = psz_st.enter_context(nc.psum_tensor("lrtz", [128, 512], F32))
        with nc.Block() as blk:
            @blk.tensor
            def _(t):
                t.wait_ge(p.sem("lrl"), p.val("lrl"))
                for c in range(DC):
                    for ts in range(4):
                        t.matmul(psz[:, ts * 512:(ts + 1) * 512],
                                 a4t_sb[:, c, :],
                                 xnT[:, c * T + ts * 512: c * T + ts * 512 + 512],
                                 start=(c == 0), stop=(c == DC - 1))
                pe_fence(p, t, io, trz, "lrz")

            @blk.vector
            def _(v):
                v.wait_ge(p.sem("lrz"), p.val("lrz"))
                v.tensor_copy(zt_sb[:], psz[:])

    with contextlib.ExitStack() as ps_st:
        pslg = [ps_st.enter_context(nc.psum_tensor(f"lrpl{i}", [128, 4], F32))
                for i in range(2)]
        psb = ps_st.enter_context(nc.psum_tensor("lrpb", [128, 4], F32))
        pstr = [ps_st.enter_context(nc.psum_tensor(f"lrpt{i}", [4, 128], BF16))
                for i in range(2)]
        psax = ps_st.enter_context(nc.psum_tensor("lrpax", [32, 512], F32))
        trb = ps_st.enter_context(nc.psum_tensor("lrtb", [128, 512], F32))

        base_ev = p.val("lrgev")
        with nc.Block() as blk:
            @blk.tensor
            def _(t):
                for tt in range(NT):
                    if tt >= 2:
                        t.wait_ge(p.sem("lrgev"), base_ev + tt - 1)
                    for c in range(DC):
                        t.matmul(pslg[tt % 2][:],
                                 xnT[:, c * T + tt * 128: c * T + tt * 128 + 128],
                                 w4t_sb[:, c, :],
                                 start=(c == 0), stop=(c == DC - 1))
                    pe_fence(p, t, io, trb, "lrgmm")

            @blk.vector
            def _(v):
                for tt in range(NT):
                    v.wait_ge(p.sem("lrgmm"), base_ev + tt + 1)
                    v.tensor_copy(lg_sb[:, tt * 4:(tt + 1) * 4],
                                  pslg[tt % 2][:]).then_inc(p.sem("lrgev"), 1)
                    p.inc("lrgev")

        with nc.Block() as blk:
            @blk.scalar
            def _(sc):
                sc.activation(lp_sb[:], pri_sb[:], AF.Ln, bias=io["eps12"][0:1, :])
        with nc.Block() as blk:
            @blk.vector
            def _(v):
                v.tensor_add(lp_sb[:], lp_sb[:], rb_sb[:])
                v.tensor_copy(lp_bf[:], lp_sb[:])
        with nc.Block() as blk:
            @blk.tensor
            def _(t):
                t.matmul(psb[:], io["ones2"][0:1, :], lp_bf[:],
                         start=True, stop=True)
                pe_fence(p, t, io, trb)
        with nc.Block() as blk:
            @blk.vector
            def _(v):
                v.tensor_copy(lp2b[:], psb[:])

        for tt in range(NT):
            with nc.Block() as blk:
                @blk.vector
                def _(v):
                    p.ss(v, v.tensor_add(la_sb[:], lg_sb[:, tt * 4:(tt + 1) * 4], lp2b[:]))
                    p.ss(v, v.reduce_max(mx_sb[:], la_sb[:], axis=AX.X))
                    v.tensor_scalar(mx_sb[:], mx_sb[:], -1.0, None,
                                    ALU.mult).then_inc(p.sem("lrs1"), 1)
                    p.inc("lrs1")

                @blk.scalar
                def _(sc):
                    sc.wait_ge(p.sem("lrs1"), p.val("lrs1"))
                    sc.activation(la_sb[:], la_sb[:], AF.Exp, bias=mx_sb[:],
                                  accum_out=ssum[:]).then_inc(p.sem("lrs2"), 1)
                    p.inc("lrs2")

                @blk.vector
                def _(v):
                    v.wait_ge(p.sem("lrs2"), p.val("lrs2"))
                    p.ss(v, v.reciprocal(rs_sb[:], ssum[:]))
                    v.tensor_scalar(al_sb[:, tt * 4:(tt + 1) * 4], la_sb[:],
                                    rs_sb[:], None, ALU.mult).then_inc(p.sem("lrs3"), 1)
                    p.inc("lrs3")

                @blk.tensor
                def _(t):
                    t.wait_ge(p.sem("lrs3"), p.val("lrs3"))
                    t.transpose(pstr[tt % 2][:], al_sb[:, tt * 4:(tt + 1) * 4],
                                io["ident"][:])
                    pe_fence(p, t, io, trb, "lrs4")

                @blk.vector
                def _(v):
                    v.wait_ge(p.sem("lrs4"), p.val("lrs4"))
                    v.tensor_copy(alT[:, tt * 128:(tt + 1) * 128], pstr[tt % 2][:])

        for ts in range(4):
            with nc.Block() as blk:
                @blk.tensor
                def _(t):
                    t.matmul(psax[:], exp_sb[:], alT[:, ts * 512:(ts + 1) * 512],
                             start=True, stop=True)
                    pe_fence(p, t, io, trb, "lrx")

                @blk.vector
                def _(v):
                    v.wait_ge(p.sem("lrx"), p.val("lrx"))
                    v.scalar_tensor_tensor(zh_sb[:, ts * 512:(ts + 1) * 512],
                                           zt_sb[:, ts * 512:(ts + 1) * 512],
                                           2.0, psax[:], ALU.mult, ALU.mult)
    return zh_sb, bblk_sb


def emit_attention(p, io, cfg, qt_hbm, agk_out, agv_out, attn_dst, masterT_sb,
                   alibi_cols=None, sqrows=None, sgoT=None):
    nc = p.nc
    tg = cfg["tag"]
    n_qh, kv_per, dkv, scale = cfg["n_qh"], cfg["kv_per"], cfg["dkv"], cfg["scale"]
    with contextlib.ExitStack() as st:
        qt_h = st.enter_context(nc.sbuf_tensor(tg + "qt", [128, T], BF16))
        kt_sb = st.enter_context(nc.sbuf_tensor(tg + "kt", [128, S], BF16))
        v_sb = st.enter_context(nc.sbuf_tensor(tg + "v", [128, 32, 128], BF16))
        if cfg["alibi"]:
            qsq = st.enter_context(nc.sbuf_tensor(tg + "qsq", [128, T], BF16))
            ksq = st.enter_context(nc.sbuf_tensor(tg + "ksq", [128, S], BF16))
            qn2 = st.enter_context(nc.sbuf_tensor(tg + "qn2", [1, T], F32))
            kn2 = st.enter_context(nc.sbuf_tensor(tg + "kn2", [1, S], F32))
            qmx = st.enter_context(nc.sbuf_tensor(tg + "qmx", [1, 4], F32))
            kmx = st.enter_context(nc.sbuf_tensor(tg + "kmx", [1, 1], F32))
            mrow = st.enter_context(nc.sbuf_tensor(tg + "mrow", [1, 4], F32))
            mrow_bf = st.enter_context(nc.sbuf_tensor(tg + "mrbf", [1, 4], BF16))
            mcols = st.enter_context(nc.sbuf_tensor(tg + "mcols", [128, 4], F32))
            biasm = st.enter_context(nc.sbuf_tensor(tg + "biasm", [128, 128], F32))
            ac_h = st.enter_context(nc.sbuf_tensor(tg + "ach", [128, 32], F32))
            sq_h = st.enter_context(nc.sbuf_tensor(tg + "sqh", [2, T], BF16))
            msk = [st.enter_context(nc.sbuf_tensor(tg + f"msk{i}", [128, 512], F32))
                   for i in range(2)]
        pT = st.enter_context(nc.sbuf_tensor(tg + "pT", [128, 32 * 512], BF16))
        linv = st.enter_context(nc.sbuf_tensor(tg + "linv", [1, 512], F32))
        linv_bf = st.enter_context(nc.sbuf_tensor(tg + "linb", [1, 512], BF16))
        r_sb = st.enter_context(nc.sbuf_tensor(tg + "rsb", [128, 512], BF16))
        aev = st.enter_context(nc.sbuf_tensor(tg + "aev", [128, 512], BF16))
        psc = [st.enter_context(nc.psum_tensor(tg + f"psc{i}", [128, 512], F32))
               for i in range(2)]
        pav = st.enter_context(nc.psum_tensor(tg + "pav", [128, 512], F32))
        pl = st.enter_context(nc.psum_tensor(tg + "pl", [1, 512], F32))
        pr = st.enter_context(nc.psum_tensor(tg + "pr", [128, 512], F32))
        psn = st.enter_context(nc.psum_tensor(tg + "psn", [1, 512], F32))
        trash = st.enter_context(nc.psum_tensor(tg + "tr", [128, 512], F32))
        psg = psn if sgoT is not None else None
        sgrow = st.enter_context(nc.sbuf_tensor(tg + "sgr", [1, T], F32)) \
            if sgoT is not None else None
        if not cfg["alibi"]:
            mskbig = st.enter_context(nc.sbuf_tensor(tg + "msb", [128, 32, 512], F32))
            permax = st.enter_context(nc.sbuf_tensor(tg + "pmx", [128, 32], F32))
            pm1 = st.enter_context(nc.sbuf_tensor(tg + "pm1", [128, 1], F32))
            pm1b = st.enter_context(nc.sbuf_tensor(tg + "pm1b", [128, 1], BF16))
            prow128 = st.enter_context(nc.sbuf_tensor(tg + "pr128", [1, 128], F32))
            prow = st.enter_context(nc.sbuf_tensor(tg + "prw", [1, 1], F32))
            mneg = st.enter_context(nc.sbuf_tensor(tg + "mng", [1, 1], F32))
            mnegb = st.enter_context(nc.sbuf_tensor(tg + "mngb", [1, 1], BF16))
            mcol1 = st.enter_context(nc.sbuf_tensor(tg + "mc1", [128, 1], F32))
            pTh = st.enter_context(nc.sbuf_tensor(tg + "pth", [128, 512], BF16))
            ptm = st.enter_context(nc.psum_tensor(tg + "ptm", [1, 128], BF16))

        nqs = T // 512
        nkstr_q = T // 512
        nkstr_k = S // 512
        for h in range(n_qh):
            kvh = h // kv_per
            with nc.Block() as blk:
                @blk.sync
                def _(sy):
                    dma(sy, p, tg + "ld", qt_h[:], qt_hbm[h * 128:(h + 1) * 128, :])
                    dma(sy, p, tg + "ld", kt_sb[:, 0:T],
                        agk_out[kvh * 128:(kvh + 1) * 128, :])
                    dma(sy, p, tg + "ld", kt_sb[:, T:S],
                        agk_out[dkv + kvh * 128: dkv + (kvh + 1) * 128, :])
                    vsrc = agv_out[:, kvh * 128:(kvh + 1) * 128].rearrange(
                        "(kc p) f -> p kc f", p=128)
                    for r in range(4):
                        dma(sy, p, tg + "ld", v_sb[:, r * 8:(r + 1) * 8, :],
                            vsrc[:, r * 8:(r + 1) * 8, :])
                    if cfg["alibi"]:
                        dma(sy, p, tg + "ld", ac_h[:], alibi_cols[h])
                        dma(sy, p, tg + "ld", sq_h[:], sqrows[h])

                if cfg["alibi"]:
                    @blk.scalar
                    def _(sc):
                        sc.wait_ge(p.sem(tg + "ld"), p.val(tg + "ld"))
                        sc.activation(qsq[:], qt_h[:], AF.Square)
                        sc.activation(ksq[:], kt_sb[:],
                                      AF.Square).then_inc(p.sem(tg + "hsq"), 1)
                        p.inc(tg + "hsq")

                    nbase = p.val(tg + "hnev")
                    strips = [(qsq, qn2, i) for i in range(nkstr_q)] + \
                             [(ksq, kn2, i) for i in range(nkstr_k)]

                    @blk.tensor
                    def _(t):
                        t.wait_ge(p.sem(tg + "hsq"), p.val(tg + "hsq"))
                        for i, (srcb, dstb, si) in enumerate(strips):
                            if i >= 1:
                                t.wait_ge(p.sem(tg + "hnev"), nbase + i)
                            t.matmul(psn[:], io["onescol"][:],
                                     srcb[:, si * 512:(si + 1) * 512],
                                     start=True, stop=True)
                            pe_fence(p, t, io, trash, tg + "hnmm")

                    @blk.vector
                    def _(v):
                        for i, (srcb, dstb, si) in enumerate(strips):
                            v.wait_ge(p.sem(tg + "hnmm"), nbase + i + 1)
                            v.tensor_copy(dstb[:, si * 512:(si + 1) * 512],
                                          psn[:]).then_inc(p.sem(tg + "hnev"), 1)
                            p.inc(tg + "hnev")
                        p.ss(v, v.tensor_copy(io["fcell"][0:1, 0:1], io["eps"][0:1, :]))
                        v.reduce_max(qmx[:], qn2[:].rearrange("a (q x) -> a q x", x=512),
                                     axis=AX.X)
                        v.reduce_max(kmx[:], kn2[:], axis=AX.X).then_inc(p.sem(tg + "hrd"), 1)
                        p.inc(tg + "hrd")

                    @blk.scalar
                    def _(sc):
                        sc.wait_ge(p.sem(tg + "hrd"), p.val(tg + "hrd"))
                        sc.activation(qmx[:], qmx[:], AF.Sqrt)
                        sc.activation(kmx[:], kmx[:],
                                      AF.Sqrt).then_inc(p.sem(tg + "hsr"), 1)
                        p.inc(tg + "hsr")

                    @blk.vector
                    def _(v):
                        v.wait_ge(p.sem(tg + "hsr"), p.val(tg + "hsr"))
                        p.ss(v, v.tensor_scalar(mrow[:], qmx[:], kmx[:, 0:1], -scale,
                                                ALU.mult, ALU.mult))
                        v.tensor_copy(mrow_bf[:], mrow[:]).then_inc(p.sem(tg + "hmn"), 1)
                        p.inc(tg + "hmn")

                    @blk.tensor
                    def _(t):
                        t.wait_ge(p.sem(tg + "hmn"), p.val(tg + "hmn"))
                        t.matmul(pr[:, 0:4], io["ones2"][0:1, :], mrow_bf[:],
                                 start=True, stop=True)
                        pe_fence(p, t, io, trash, tg + "hmb")

                    @blk.vector
                    def _(v):
                        v.wait_ge(p.sem(tg + "hmb"), p.val(tg + "hmb"))
                        p.ss(v, v.tensor_copy(mcols[:], pr[:, 0:4]))
                        for qs in range(4):
                            v.tensor_scalar(biasm[:, qs * 32:(qs + 1) * 32],
                                            ac_h[:], mcols[:, qs:qs + 1], None,
                                            ALU.add)


            nkt_tab = [min(32, (2048 + 512 * q + 512) // 128) for q in range(nqs)]
            with nc.Block() as blk:
                for qs in range(nqs):
                    nkt = nkt_tab[qs]
                    nkt_prev = nkt_tab[qs - 1] if qs > 0 else 32
                    b_msc = p.val(tg + "msc")
                    b_mmk = p.val(tg + "mmk")
                    b_mpe = p.val(tg + "mpe")
                    b_mev = p.val(tg + "mev")
                    b_sst = p.val(tg + "sst")
                    b_xm1 = p.val(tg + "xm1")
                    b_xm2 = p.val(tg + "xm2")
                    b_xm3 = p.val(tg + "xm3")
                    b_xm4 = p.val(tg + "xm4")
                    b_xm5 = p.val(tg + "xm5")
                    b_ld = p.val(tg + "ld")

                    @blk.tensor
                    def _(t, qs=qs, nkt=nkt, b_mmk=b_mmk, b_mpe=b_mpe,
                          b_mev=b_mev, b_xm1=b_xm1, b_xm3=b_xm3, b_ld=b_ld):
                        if qs == 0 and not cfg["alibi"]:
                            t.wait_ge(p.sem(tg + "ld"), b_ld)
                        for kt in range(nkt):
                            if kt >= 2:
                                t.wait_ge(p.sem(tg + "mmk"), b_mmk + kt - 1)
                            if cfg["alibi"]:
                                t.matmul(psc[kt % 2][:], io["ones2"][:],
                                         sq_h[:, qs * 512:(qs + 1) * 512],
                                         start=True, stop=False)
                            t.matmul(psc[kt % 2][:],
                                     kt_sb[:, kt * 128:(kt + 1) * 128],
                                     qt_h[:, qs * 512:(qs + 1) * 512],
                                     start=not cfg["alibi"],
                                     stop=True).then_inc(p.sem(tg + "msc"), 1)
                            p.inc(tg + "msc")
                        pe_fence(p, t, io, trash, tg + "msc")
                        if not cfg["alibi"]:
                            t.wait_ge(p.sem(tg + "xm1"), b_xm1 + 1)
                            t.transpose(ptm[:], pm1b[:], io["ident"][:])
                            pe_fence(p, t, io, trash, tg + "xm2")
                            t.wait_ge(p.sem(tg + "xm3"), b_xm3 + 1)
                            t.matmul(pr[:, 0:1], io["ones2"][0:1, :], mnegb[:],
                                     start=True, stop=True)
                            pe_fence(p, t, io, trash, tg + "xm4")
                        if qs > 0:
                            t.wait_ge(p.sem(tg + "mev"), b_mev)
                        for kt in range(nkt):
                            t.wait_ge(p.sem(tg + "mpe"), b_mpe + kt + 1)
                            t.matmul(pav[:], v_sb[:, kt, :],
                                     pT[:, kt * 512:(kt + 1) * 512],
                                     start=(kt == 0), stop=(kt == nkt - 1))
                            t.matmul(pl[:], io["onescol"][:],
                                     pT[:, kt * 512:(kt + 1) * 512],
                                     start=(kt == 0), stop=(kt == nkt - 1))
                        pe_fence(p, t, io, trash, tg + "mavd")
                        t.wait_ge(p.sem(tg + "mlv"), p.val(tg + "mlv") + 1)
                        t.matmul(pr[:], io["ones2"][0:1, :], linv_bf[:],
                                 start=True, stop=True)
                        pe_fence(p, t, io, trash, tg + "mr")
                        if sgoT is not None:
                            t.wait_ge(p.sem(tg + "mev"), p.val(tg + "mev") + 1)
                            t.matmul(psg[:], io["onescol"][:], aev[:],
                                     start=True, stop=True)
                            pe_fence(p, t, io, trash, tg + "sgm")

                    @blk.vector
                    def _(v, qs=qs, nkt=nkt, nkt_prev=nkt_prev, b_msc=b_msc,
                          b_mpe=b_mpe, b_sst=b_sst,
                          b_xm2=b_xm2, b_xm4=b_xm4,
                          b_sgev=p.val(tg + "sgev")):
                        vv_base = 512 * qs + MST_OFF
                        for kt in range(nkt):
                            v.wait_ge(p.sem(tg + "msc"), b_msc + kt + 2)
                            if cfg["alibi"]:
                                if kt >= 2:
                                    v.wait_ge(p.sem(tg + "mpe"), b_mpe + kt - 1)
                                mdst = msk[kt % 2][:]
                            else:
                                if qs > 0:
                                    v.wait_ge(p.sem(tg + "mpe"),
                                              b_mpe - nkt_prev
                                              + min(kt + 1, nkt_prev))
                                mdst = mskbig[:, kt, :]
                            vv0 = vv_base - 128 * kt
                            v.tensor_add(mdst, psc[kt % 2][:],
                                         masterT_sb[:, vv0:vv0 + 512]
                                         ).then_inc(p.sem(tg + "mmk"), 1)
                            p.inc(tg + "mmk")
                        if not cfg["alibi"]:
                            p.ss(v, v.tensor_copy(io["fcell"][0:1, 0:1],
                                                  io["eps"][0:1, :]))
                            p.ss(v, v.reduce_max(permax[:, 0:nkt],
                                                 mskbig[:, 0:nkt, :],
                                                 axis=AX.X))
                            p.ss(v, v.reduce_max(pm1[:], permax[:, 0:nkt],
                                                 axis=AX.X))
                            v.tensor_copy(pm1b[:], pm1[:]).then_inc(
                                p.sem(tg + "xm1"), 1)
                            p.inc(tg + "xm1")
                            v.wait_ge(p.sem(tg + "xm2"), b_xm2 + 1)
                            p.ss(v, v.tensor_copy(prow128[:], ptm[0:1, :]))
                            p.ss(v, v.reduce_max(prow[:], prow128[:], axis=AX.X))
                            p.ss(v, v.tensor_scalar(mneg[:], prow[:],
                                                    -1.0 * scale, 70.0,
                                                    ALU.mult, ALU.add))
                            v.tensor_copy(mnegb[:], mneg[:]).then_inc(
                                p.sem(tg + "xm3"), 1)
                            p.inc(tg + "xm3")
                            v.wait_ge(p.sem(tg + "xm4"), b_xm4 + 1)
                            v.tensor_copy(mcol1[:], pr[:, 0:1]).then_inc(
                                p.sem(tg + "xm5"), 1)
                            p.inc(tg + "xm5")
                        v.wait_ge(p.sem(tg + "mavd"), p.val(tg + "mavd"))
                        p.ss(v, v.reciprocal(linv[:], pl[:]))
                        v.tensor_copy(linv_bf[:], linv[:]).then_inc(p.sem(tg + "mlv"), 1)
                        p.inc(tg + "mlv")
                        v.wait_ge(p.sem(tg + "mr"), p.val(tg + "mr"))
                        p.ss(v, v.tensor_copy(r_sb[:], pr[:]))
                        if qs > 0:
                            v.wait_ge(p.sem(tg + "sst"), b_sst)
                            if sgoT is not None:
                                v.wait_ge(p.sem(tg + "sgev"), b_sgev)
                        v.tensor_mul(aev[:], pav[:], r_sb[:]).then_inc(p.sem(tg + "mev"), 1)
                        p.inc(tg + "mev")
                        if sgoT is not None:
                            v.wait_ge(p.sem(tg + "sgm"), p.val(tg + "sgm"))
                            v.tensor_copy(sgrow[:, qs * 512:(qs + 1) * 512],
                                          psg[:]).then_inc(p.sem(tg + "sgev"), 1)
                            p.inc(tg + "sgev")

                    @blk.scalar
                    def _(sc, qs=qs, nkt=nkt, b_mmk=b_mmk, b_mpe=b_mpe,
                          b_xm5=b_xm5):
                        if cfg["alibi"]:
                            for kt in range(nkt):
                                sc.wait_ge(p.sem(tg + "mmk"), b_mmk + kt + 1)
                                bias_ap = biasm[:, qs * 32 + kt: qs * 32 + kt + 1]
                                sc.activation(pT[:, kt * 512:(kt + 1) * 512],
                                              msk[kt % 2][:], AF.Exp, bias=bias_ap,
                                              scale=scale).then_inc(p.sem(tg + "mpe"), 1)
                                p.inc(tg + "mpe")
                        else:
                            sc.wait_ge(p.sem(tg + "xm5"), b_xm5 + 1)
                            for kt in range(nkt):
                                sc.activation(pT[:, kt * 512:(kt + 1) * 512],
                                              mskbig[:, kt, :], AF.Exp,
                                              bias=mcol1[:, 0:1], scale=scale
                                              ).then_inc(p.sem(tg + "mpe"), 1)
                                p.inc(tg + "mpe")

                    @blk.sync
                    def _(sy, qs=qs):
                        sy.wait_ge(p.sem(tg + "mev"), p.val(tg + "mev"))
                        dma(sy, p, tg + "sst", attn_dst[h * 128:(h + 1) * 128,
                                                        qs * 512:(qs + 1) * 512],
                            aev[:])
                @blk.sync
                def _(sy):
                    if sgoT is not None:
                        sy.wait_ge(p.sem(tg + "sgev"), p.val(tg + "sgev"))
                        dma(sy, p, tg + "sgs", sgoT[h:h + 1, :], sgrow[:])
                        sy.wait_ge(p.sem(tg + "sgs"), p.val(tg + "sgs"))
                    sy.wait_ge(p.sem(tg + "sst"), p.val(tg + "sst"))


def emit_oproj(p, io, attn_src, dst_hbm, tag, w_load, bias_bcast=None,
               corr=None):
    nc = p.nc
    with contextlib.ExitStack() as st:
        wmov = [st.enter_context(nc.sbuf_tensor(f"{tag}w{i}", [128, DC, 512], BF16))
                for i in range(2)]
        acol = [st.enter_context(nc.sbuf_tensor(f"{tag}a{i}", [128, DC, 128], BF16))
                for i in range(2)]
        evb = [st.enter_context(nc.sbuf_tensor(f"{tag}e{i}", [128, 512], BF16))
               for i in range(2)]
        psq = [st.enter_context(nc.psum_tensor(f"{tag}p{i}", [128, 512], F32))
               for i in range(2)]
        trash = st.enter_context(nc.psum_tensor(tag + "tr", [128, 512], F32))
        b_wrdy = p.val(tag + "wrdy")
        for os_ in range(8):
            par = os_ % 2
            with nc.Block() as blk:
                if os_ == 0:
                    w_load(blk, 0, wmov[0])
                if os_ + 1 < 8:
                    w_load(blk, os_ + 1, wmov[1 - par])

                b_mm = p.val(tag + "mm")
                b_ev = p.val(tag + "ev")
                b_st = [p.val(tag + "st0"), p.val(tag + "st1")]
                b_al = [p.val(tag + "al0"), p.val(tag + "al1")]

                def load_acol(sy, tt):
                    src = attn_src[:, tt * 128:(tt + 1) * 128].rearrange(
                        "(c p) f -> p c f", p=128)
                    for r in range(4):
                        dma(sy, p, tag + "al" + str(tt % 2),
                            acol[tt % 2][:, r * 8:(r + 1) * 8, :],
                            src[:, r * 8:(r + 1) * 8, :])

                @blk.sync
                def _(sy):
                    for tt in range(NT):
                        if tt == 0:
                            load_acol(sy, 0)
                            load_acol(sy, 1)
                        if tt + 2 < NT:
                            sy.wait_ge(p.sem(tag + "mm"), b_mm + tt + 1)
                            load_acol(sy, tt + 2)
                        sy.wait_ge(p.sem(tag + "ev"), b_ev + tt + 1)
                        dma(sy, p, tag + "st" + str(tt % 2),
                            dst_hbm[tt * 128:(tt + 1) * 128,
                                    os_ * 512:(os_ + 1) * 512],
                            evb[tt % 2][:])
                    sy.wait_ge(p.sem(tag + "st0"), p.val(tag + "st0"))
                    sy.wait_ge(p.sem(tag + "st1"), p.val(tag + "st1"))

                @blk.tensor
                def _(t):
                    t.wait_ge(p.sem(tag + "wrdy"), b_wrdy + os_ + 1)
                    for tt in range(NT):
                        t.wait_ge(p.sem(tag + "al" + str(tt % 2)),
                                  b_al[tt % 2] + 64 * (tt // 2 + 1))
                        if tt >= 2:
                            t.wait_ge(p.sem(tag + "ev"), b_ev + tt - 1)
                        ps = psq[tt % 2]
                        last = corr is None
                        for c in range(DC):
                            t.matmul(ps[:], acol[tt % 2][:, c, :],
                                     wmov[par][:, c, :],
                                     start=(c == 0),
                                     stop=(c == DC - 1 and last))
                        if corr is not None:
                            sgoT, s8o = corr
                            t.matmul(ps[:], sgoT[:, tt * 128:(tt + 1) * 128],
                                     s8o[:, os_ * 512:(os_ + 1) * 512],
                                     start=False, stop=True)
                        pe_fence(p, t, io, trash, tag + "mm")

                @blk.vector
                def _(v):
                    for tt in range(NT):
                        v.wait_ge(p.sem(tag + "mm"), b_mm + tt + 1)
                        if tt >= 2:
                            v.wait_ge(p.sem(tag + "st" + str(tt % 2)),
                                      b_st[tt % 2] + 16 * (tt // 2))
                        if bias_bcast is not None:
                            ins = v.tensor_add(evb[tt % 2][:], psq[tt % 2][:],
                                               bias_bcast[:, os_ * 512:(os_ + 1) * 512])
                        else:
                            ins = v.tensor_copy(evb[tt % 2][:], psq[tt % 2][:])
                        ins.then_inc(p.sem(tag + "ev"), 1)
                        p.inc(tag + "ev")


def emit_epilogue(p, x_src, a_src, f_src, dst, tag):
    nc = p.nc
    with contextlib.ExitStack() as st:
        xst = st.enter_context(nc.sbuf_tensor(tag + "x", [128, D], F32))
        ast = st.enter_context(nc.sbuf_tensor(tag + "a", [128, D], BF16))
        fst = st.enter_context(nc.sbuf_tensor(tag + "f", [128, D], BF16))
        ost = st.enter_context(nc.sbuf_tensor(tag + "o", [128, D], F32))
        for tt in range(NT):
            with nc.Block() as blk:
                @blk.sync
                def _(sy):
                    dma(sy, p, tag + "ld", xst[:], x_src[tt * 128:(tt + 1) * 128, :])
                    dma(sy, p, tag + "ld", ast[:], a_src[tt * 128:(tt + 1) * 128, :])
                    dma(sy, p, tag + "ld", fst[:], f_src[tt * 128:(tt + 1) * 128, :])

                @blk.vector
                def _(v):
                    v.wait_ge(p.sem(tag + "ld"), p.val(tag + "ld"))
                    p.ss(v, v.tensor_add(ost[:], xst[:], ast[:]))
                    v.tensor_add(ost[:], ost[:], fst[:]).then_inc(p.sem(tag + "ev"), 1)
                    p.inc(tag + "ev")

                @blk.sync
                def _(sy):
                    sy.wait_ge(p.sem(tag + "ev"), p.val(tag + "ev"))
                    dma(sy, p, tag + "st", dst[tt * 128:(tt + 1) * 128, :], ost[:])
                    sy.wait_ge(p.sem(tag + "st"), p.val(tag + "st"))



def build_l0(debug=(), upto="full", fuse=None):
    if fuse is None:
        nc = bass.Bass("TRN2", target_bir_lowering=False)
        p = Prog(nc)
    else:
        nc = NcPfx(fuse["nc"], "zA")
        p = fuse["p"]
        p.nc = nc

    x0 = nc.dram_tensor("x0", [T, D], F32, kind="ExternalInput").ap()
    w_attn = nc.dram_tensor("w_attn", [D, 3 * D], F32, kind="ExternalInput").ap()
    b_attn = nc.dram_tensor("b_attn", [1, 3 * D], F32, kind="ExternalInput").ap()
    w_proj = nc.dram_tensor("w_proj", [D, D], F32, kind="ExternalInput").ap()
    b_proj = nc.dram_tensor("b_proj", [1, D], F32, kind="ExternalInput").ap()
    w_baseT = nc.dram_tensor("w_baseT", [D, D], F32, kind="ExternalInput").ap()
    b_base = nc.dram_tensor("b_base", [1, D], F32, kind="ExternalInput").ap()
    a4t = nc.dram_tensor("a4t", [D, 32], F32, kind="ExternalInput").ap()
    bblk = nc.dram_tensor("bblk", [32, D], F32, kind="ExternalInput").ap()
    w4t = nc.dram_tensor("w4t", [D, 4], F32, kind="ExternalInput").ap()
    rb4 = nc.dram_tensor("rb4", [1, 4], F32, kind="ExternalInput").ap()
    prior4 = nc.dram_tensor("prior4", [1, 4], F32, kind="ExternalInput").ap()
    b_qk_cols = nc.dram_tensor("b_qk_cols", [128, 64], F32, kind="ExternalInput").ap()
    masterT = nc.dram_tensor("masterT", [128, MST_W], BF16, kind="ExternalInput").ap()
    alibi_cols = nc.dram_tensor("alibi_cols", [32, 128, 32], F32, kind="ExternalInput").ap()
    sqrows = nc.dram_tensor("sqrows", [32, 2, T], BF16, kind="ExternalInput").ap()
    expmat = nc.dram_tensor("expmat", [4, 32], F32, kind="ExternalInput").ap()

    if fuse is None:
        x1 = nc.dram_tensor("x1", [T, D], F32, kind="ExternalOutput").ap()
    else:
        x1 = fuse["x1"]
    dbg = {}
    for name, shp, dt_ in (("xnT", [128, DC * T], BF16), ("qt", [D, T], BF16),
                           ("kt", [D, T], BF16), ("v", [T, D], BF16),
                           ("attnT", [D, T], BF16), ("ffn", [T, D], BF16),
                           ("atno", [T, D], BF16)):
        if name in debug:
            dbg[name] = nc.dram_tensor("dbg_" + name, shp, dt_,
                                       kind="ExternalOutput").ap()

    qt_s = nc.dram_tensor("qt_s", [D, T], BF16, kind="Internal").ap()
    ag_k_in = nc.dram_tensor("ag_k_in", [D, T], BF16, kind="Internal")
    ag_k_out = nc.dram_tensor("ag_k_out", [2 * D, T], BF16, kind="Internal")
    ag_vi = nc.dram_tensor("ag_vi", [T, D], BF16, kind="Internal")
    ag_vo = nc.dram_tensor("ag_vo", [S, D], BF16, kind="Internal")
    attn_s = nc.dram_tensor("attn_s", [D, T], BF16, kind="Internal").ap()
    ffn_s = nc.dram_tensor("ffn_s", [T, D], BF16, kind="Internal").ap()
    atno_s = nc.dram_tensor("atno_s", [T, D], BF16, kind="Internal").ap()

    gctx = contextlib.ExitStack()
    if fuse is None:
        io = _mk_io(nc, gctx)
    else:
        io = fuse["io"]

    with gctx:
        if fuse is None:
            emit_init_consts(p, io)

        with contextlib.ExitStack() as r1:
            xnT = r1.enter_context(nc.sbuf_tensor("xnT", [128, DC * T], BF16))
            emit_ln_xnt(p, io, x0, xnT, "ln")
            if "xnT" in dbg:
                with nc.Block() as blk:
                    @blk.sync
                    def _(sy):
                        dma(sy, p, "dbg", dbg["xnT"], xnT[:])
                        sy.wait_ge(p.sem("dbg"), p.val("dbg"))

            bcols = r1.enter_context(nc.sbuf_tensor("bcols", [128, 96], F32))
            with nc.Block() as blk:
                @blk.sync
                def _(sy):
                    dma(sy, p, "bcl", bcols[:, 0:64], b_qk_cols)
                    sy.wait_ge(p.sem("bcl"), p.val("bcl"))

            if upto == "ln":
                return _fin(nc)
            emit_featmaj_proj(p, io, xnT, w_attn, 0, 32, bcols, 0, qt_s, "pq")
            if upto == "q":
                if "qt" in dbg:
                    _copy_hbm(p, dbg["qt"], qt_s)
                return _fin(nc)
            emit_featmaj_proj(p, io, xnT, w_attn, D, 32, bcols, 32,
                              ag_k_in.ap(), "pk")
            with nc.Block() as blk:
                @blk.gpsimd
                def _(g):
                    g.collective_compute(
                        "AllGather", ALU.bypass, replica_groups=REPLICA_PAIRS,
                        ins=[ag_k_in[:]], outs=[ag_k_out[:]],
                    ).then_inc(p.sem("agk"), 1)
                    p.inc("agk")

            bvb = r1.enter_context(nc.sbuf_tensor("bvb", [128, D], BF16))
            bbb = r1.enter_context(nc.sbuf_tensor("bbb", [128, D], BF16))
            with nc.Block() as blk:
                @blk.gpsimd
                def _(g):
                    dma(g, p, "bcl", bvb[:],
                        b_attn[:, 2 * D:3 * D].to_broadcast((128, D)))
                    dma(g, p, "bcl", bbb[:], b_base.to_broadcast((128, D)))
                    g.wait_ge(p.sem("bcl"), p.val("bcl"))
            emit_tokmaj_proj(p, io, xnT, w_attn, D, 2 * D, 8, bvb,
                             ag_vi.ap(), "pv")
            with nc.Block() as blk:
                @blk.gpsimd
                def _(g):
                    g.collective_compute(
                        "AllGather", ALU.bypass, replica_groups=REPLICA_PAIRS,
                        ins=[ag_vi[:]], outs=[ag_vo[:]],
                    ).then_inc(p.sem("agv"), 1)
                    p.inc("agv")

            if upto == "v":
                if "qt" in dbg:
                    _copy_hbm(p, dbg["qt"], qt_s)
                with nc.Block() as blk:
                    @blk.gpsimd
                    def _(g):
                        g.wait_ge(p.sem("agk"), p.val("agk"))
                        g.wait_ge(p.sem("agv"), p.val("agv"))
                if "kt" in dbg:
                    _copy_hbm(p, dbg["kt"], ag_k_out.ap()[0:D, :])
                if "v" in dbg:
                    _copy_hbm(p, dbg["v"], ag_vo.ap()[0:T, :])
                return _fin(nc)
            zh_sb, bblk_sb = emit_lora_prep(p, io, r1, xnT, a4t, w4t, bblk,
                                            rb4, prior4, expmat)

            def lora_mm(t, tt, fs, ps):
                return t.matmul(ps[:], zh_sb[:, tt * 128:(tt + 1) * 128],
                                bblk_sb[:, fs * 512:(fs + 1) * 512],
                                start=False, stop=True)

            emit_tokmaj_proj(p, io, xnT, w_baseT, D, 0, 8, bbb,
                             ffn_s, "py", extra_mm=lora_mm)

        if "ffn" in dbg:
            _copy_hbm(p, dbg["ffn"], ffn_s)
        if "qt" in dbg:
            _copy_hbm(p, dbg["qt"], qt_s)
        if upto == "ffn":
            return _fin(nc)

        with contextlib.ExitStack() as r2:
            mst_sb = r2.enter_context(nc.sbuf_tensor("mst", [128, MST_W], BF16))
            with nc.Block() as blk:
                @blk.sync
                def _(sy):
                    dma(sy, p, "mstl", mst_sb[:], masterT)
                    sy.wait_ge(p.sem("mstl"), p.val("mstl"))

                @blk.gpsimd
                def _(g):
                    g.wait_ge(p.sem("agk"), p.val("agk"))
                    g.wait_ge(p.sem("agv"), p.val("agv"))
            if "kt" in dbg:
                _copy_hbm(p, dbg["kt"], ag_k_out.ap()[0:D, :])
            if "v" in dbg:
                _copy_hbm(p, dbg["v"], ag_vo.ap()[0:T, :])
            cfg = dict(n_qh=32, kv_per=1, dkv=D, scale=SCALE0, alibi=True,
                       tag="a0")
            emit_attention(p, io, cfg, qt_s, ag_k_out.ap(), ag_vo.ap(),
                           attn_s, mst_sb, alibi_cols, sqrows)
        if "attnT" in dbg:
            _copy_hbm(p, dbg["attnT"], attn_s)
        if upto == "attn":
            return _fin(nc)

        with contextlib.ExitStack() as r3:
            bpb = r3.enter_context(nc.sbuf_tensor("bpb", [128, D], F32))
            with nc.Block() as blk:
                @blk.sync
                def _(sy):
                    dma(sy, p, "bcl2", bpb[:], b_proj.to_broadcast((128, D)))
                    sy.wait_ge(p.sem("bcl2"), p.val("bcl2"))

            def wo_load(blk, os_, buf):
                @blk.gpsimd
                def _(g):
                    load_w_colstrip(g, p, "powl", buf, w_proj, os_ * 512, 512)
                    g.wait_ge(p.sem("powl"), p.val("powl"))
                    g.sem_inc(p.sem("powrdy"), 1)
                    p.inc("powrdy")

            emit_oproj(p, io, attn_s, atno_s, "po", wo_load, bias_bcast=bpb)
        if "atno" in dbg:
            _copy_hbm(p, dbg["atno"], atno_s)

        emit_epilogue(p, x0, atno_s, ffn_s, x1, "ep")
    return nc


def _fin(nc):
    return nc


def _copy_hbm(p, dst, src, sbuf=False):
    nc = p.nc
    if sbuf:
        with nc.Block() as blk:
            @blk.sync
            def _(sy):
                dma(sy, p, "cps", dst, src)
                sy.wait_ge(p.sem("cps"), p.val("cps"))
        return
    rows, cols = dst.shape
    with contextlib.ExitStack() as st:
        buf = st.enter_context(nc.sbuf_tensor("cph" + str(p.inc("cpn")),
                                              [128, cols], BF16))
        for r0 in range(0, rows, 128):
            with nc.Block() as blk:
                @blk.sync
                def _(sy):
                    dma(sy, p, "cpl", buf[:], src[r0:r0 + 128, :])
                    sy.wait_ge(p.sem("cpl"), p.val("cpl"))
            with nc.Block() as blk:
                @blk.sync
                def _(sy):
                    dma(sy, p, "cps", dst[r0:r0 + 128, :], buf[:])
                    sy.wait_ge(p.sem("cps"), p.val("cps"))



def _hilo_bf16(x):
    hi = x.astype(bfdt)
    lo = (x - hi.astype(np.float32)).astype(bfdt)
    return hi, lo


def host_prep_l0(inputs):
    slopes = _alibi_slopes(32)
    A4 = np.asarray(inputs["L0_ffn_lora_A"])[np.asarray(inputs["L0_ffn_active"])]
    B4 = np.asarray(inputs["L0_ffn_lora_B"])[np.asarray(inputs["L0_ffn_active"])]
    W4 = np.asarray(inputs["L0_ffn_router_W"])[np.asarray(inputs["L0_ffn_active"])]
    rb4 = np.asarray(inputs["L0_ffn_router_b"])[np.asarray(inputs["L0_ffn_active"])]
    pr4 = np.asarray(inputs["L0_ffn_domain_prior"])[np.asarray(inputs["L0_ffn_active"])]
    expmat = np.zeros((4, 32), np.float32)
    for e in range(4):
        expmat[e, e * 8:(e + 1) * 8] = 1.0

    shared = dict(
        w_attn=np.ascontiguousarray(inputs["L0_attn_w_attn"], np.float32),
        b_attn=np.ascontiguousarray(inputs["L0_attn_b_attn"], np.float32).reshape(1, -1),
        b_qk_cols=np.ascontiguousarray(
            np.asarray(inputs["L0_attn_b_attn"], np.float32)[:2 * D].reshape(64, 128).T),
        w_proj=np.ascontiguousarray(inputs["L0_attn_w_proj"], np.float32),
        b_proj=np.ascontiguousarray(inputs["L0_attn_b_proj"], np.float32).reshape(1, -1),
        w_baseT=np.ascontiguousarray(np.asarray(inputs["L0_ffn_W_base"]).T, np.float32),
        b_base=np.ascontiguousarray(inputs["L0_ffn_b_base"], np.float32).reshape(1, -1),
        a4t=np.ascontiguousarray(A4.transpose(2, 0, 1).reshape(D, 32), np.float32),
        bblk=np.ascontiguousarray(B4.transpose(0, 2, 1).reshape(32, D), np.float32),
        w4t=np.ascontiguousarray(W4.T, np.float32),
        rb4=np.ascontiguousarray(rb4, np.float32).reshape(1, 4),
        prior4=np.ascontiguousarray(pr4, np.float32).reshape(1, 4),
        expmat=expmat,
    )
    kk = (128 * np.arange(32)[None, :] + np.arange(128)[:, None]).astype(np.float32)
    ac = slopes[:, None, None] * kk[None]
    shared["alibi_cols"] = np.ascontiguousarray(ac, np.float32)

    x_full = np.asarray(inputs["L0_attn_hidden_states"])
    percore = []
    for c in range(NCORES):
        b, hh = c // 2, c % 2
        q0g = hh * T
        d = dict(x0=np.ascontiguousarray(x_full[b, q0g:q0g + T, :], np.float32))
        ii = np.arange(128)[:, None]
        vv = np.arange(MST_W)[None, :]
        mt = np.where(ii - (vv - MST_OFF) <= q0g, 0.0, NEGBIG).astype(np.float32)
        d["masterT"] = mt.astype(bfdt)
        qg = (q0g + np.arange(T)).astype(np.float32)
        v = -128.0 * slopes[:, None] * qg[None, :]
        hi, lo = _hilo_bf16(v)
        d["sqrows"] = np.stack([hi, lo], axis=1)
        percore.append(d)
    return shared, percore


def host_prep_l1(inputs, x1_shards=None):
    shared = dict(
        q_codes=np.ascontiguousarray(inputs["L1_attn_q_codes"], np.int32),
        q_scales=np.ascontiguousarray(inputs["L1_attn_q_scales"], np.float32),
        k_codes=np.ascontiguousarray(inputs["L1_attn_k_codes"], np.int32),
        k_scales=np.ascontiguousarray(inputs["L1_attn_k_scales"], np.float32),
        v_codes=np.ascontiguousarray(inputs["L1_attn_v_codes"], np.int32),
        v_scales=np.ascontiguousarray(inputs["L1_attn_v_scales"], np.float32),
        o_codes=np.ascontiguousarray(inputs["L1_attn_o_codes"], np.int32),
        o_scales=np.ascontiguousarray(inputs["L1_attn_o_scales"], np.float32),
        qw_gu=np.ascontiguousarray(inputs["L1_ffn_qweight_gu"], np.int32),
        qz_gu=np.ascontiguousarray(inputs["L1_ffn_qzeros_gu"], np.int32),
        sc_gu=np.ascontiguousarray(inputs["L1_ffn_scales_gu"], np.float32),
        qw_dn=np.ascontiguousarray(inputs["L1_ffn_qweight_dn"], np.int32),
        qz_dn=np.ascontiguousarray(inputs["L1_ffn_qzeros_dn"], np.int32),
        sc_dn=np.ascontiguousarray(inputs["L1_ffn_scales_dn"], np.float32),
        sc_dnp=np.ascontiguousarray(
            np.asarray(inputs["L1_ffn_scales_dn"], np.float32)
            .reshape(86, 8, 64, 8).transpose(0, 1, 3, 2).reshape(86, 4096)),
    )
    inv_freq = (1.0 / 10000.0 ** (np.arange(0, 128, 2, dtype=np.float32) / 128.0))
    percore = []
    ii = np.arange(128)[:, None]
    vv = np.arange(MST_W)[None, :]
    for c in range(NCORES):
        q0g = (c % 2) * T
        pos = (q0g + np.arange(T)).astype(np.float32)
        f = pos[:, None] * inv_freq[None, :]
        mt = np.where(ii - (vv - MST_OFF) <= q0g, 0.0, NEGBIG).astype(np.float32)
        pc = dict(
            masterT=mt.astype(bfdt),
            rope_cos=np.cos(f).astype(np.float32),
            rope_sin=np.sin(f).astype(np.float32),
        )
        if x1_shards is not None:
            pc["x1"] = np.ascontiguousarray(x1_shards[c], np.float32)
        percore.append(pc)
    return shared, percore


def run_l1(inputs, x1_shards, debug=(), upto="full"):
    nc = build_l1(debug=debug, upto=upto)
    shared, percore = host_prep_l1(inputs, x1_shards)
    in_maps = [dict(shared, **pc) for pc in percore]
    res = run_bass_kernel_spmd(nc, in_maps, core_ids=list(range(NCORES)))
    return res.results


def run_l0(inputs, debug=()):
    nc = build_l0(debug=debug)
    shared, percore = host_prep_l0(inputs)
    in_maps = [dict(shared, **pc) for pc in percore]
    res = run_bass_kernel_spmd(nc, in_maps, core_ids=list(range(NCORES)))
    return res.results


def kernel_unfused(**inputs):
    res0 = run_l0(inputs)
    x1_shards = [r["x1"] for r in res0]
    res1 = run_l1(inputs, x1_shards)
    out = np.empty((4, S, D), np.float32)
    for c in range(NCORES):
        b, hh = c // 2, c % 2
        out[b, hh * T:(hh + 1) * T, :] = res1[c]["out"]
    return out


def build_full():
    nc = bass.Bass("TRN2", target_bir_lowering=False)
    p = Prog(nc)
    gctx = contextlib.ExitStack()
    io = _mk_io(nc, gctx)
    x1f = nc.dram_tensor("x1f", [T, D], F32, kind="Internal").ap()
    with gctx:
        emit_init_consts(p, io)
        build_l0(fuse=dict(nc=nc, p=p, io=io, x1=x1f))
        build_l1(fuse=dict(nc=nc, p=p, io=io, x1=x1f))
    return nc


def kernel(**inputs):
    nc = build_full()
    shared0, percore0 = host_prep_l0(inputs)
    shared1, percore1 = host_prep_l1(inputs)
    in_maps = []
    for c in range(NCORES):
        m = {}
        for k, v in dict(shared0, **percore0[c]).items():
            m["zA" + k] = v
        for k, v in dict(shared1, **percore1[c]).items():
            m["zB" + k] = v
        in_maps.append(m)
    res = run_bass_kernel_spmd(nc, in_maps, core_ids=list(range(NCORES))).results
    out = np.empty((4, S, D), np.float32)
    for c in range(NCORES):
        b, hh = c // 2, c % 2
        out[b, hh * T:(hh + 1) * T, :] = res[c]["zBout"]
    return out



def emit_dq_cs_strip(p, blk, io, codes_hbm, scales_hbm, col0, ncols, wdst, tag,
                     bufs, done_sem, nch=DC):
    cod, sbc = bufs
    b_dq = p.val(tag + "dq")
    b_cl = [p.val(tag + "cl0"), p.val(tag + "cl1")]

    @blk.gpsimd
    def _(g):
        for c in range(nch):
            g.wait_ge(p.sem(tag + "dq"), max(0, b_dq + c - 1))
            dma(g, p, tag + "cl" + str(c % 2), cod[c % 2][:, 0:ncols],
                codes_hbm[c * 128:(c + 1) * 128, col0:col0 + ncols])
            dma(g, p, tag + "cl" + str(c % 2), sbc[c % 2][:, 0:ncols],
                scales_hbm[c:c + 1, col0:col0 + ncols].to_broadcast((128, ncols)))

    @blk.vector
    def _(v):
        for c in range(nch):
            v.wait_ge(p.sem(tag + "cl" + str(c % 2)),
                      b_cl[c % 2] + 32 * (c // 2 + 1))
            ins = v.tensor_mul(wdst[:, c, :], cod[c % 2][:, 0:ncols],
                               sbc[c % 2][:, 0:ncols])
            ins.then_inc(p.sem(tag + "dq"), 1)
            p.inc(tag + "dq")


def emit_l1_zs(p, io, qz_gu, sc_gu, qz_dn, sc_dn):
    nc = p.nc
    INTER = 11008
    IC = INTER // 128
    zs_gu = nc.dram_tensor("zs_gu_s", [32, 2 * INTER], BF16, kind="Internal").ap()
    zs_dn = nc.dram_tensor("zs_dn_s", [IC, D], BF16, kind="Internal").ap()
    with contextlib.ExitStack() as st:
        qzp = st.enter_context(nc.sbuf_tensor("zsqz", [IC, 512], I32))
        zint = st.enter_context(nc.sbuf_tensor("zszi", [IC, 4096], I32))
        scp = st.enter_context(nc.sbuf_tensor("zssc", [IC, 4096], F32))
        zsp = st.enter_context(nc.sbuf_tensor("zszs", [IC, 4096], BF16))
        jobs = [(qz_gu, sc_gu, zs_gu, 32, 2752, 8)]
        for pc in range(8):
            with nc.Block() as blk:
                @blk.sync
                def _(sy, pc=pc):
                    dma(sy, p, "zsl", qzp[0:32, 0:344],
                        qz_gu[:, pc * 344:(pc + 1) * 344])
                    dma(sy, p, "zsl", scp[0:32, 0:2752],
                        sc_gu[:, pc * 2752:(pc + 1) * 2752])
                    sy.wait_ge(p.sem("zsl"), p.val("zsl"))

                @blk.vector
                def _(v):
                    v.wait_ge(p.sem("zsl"), p.val("zsl"))
                    for j in range(8):
                        v.tensor_scalar(zint[0:32, 0:2752][:, j::8],
                                        qzp[0:32, 0:344], 4 * j, 15,
                                        ALU.logical_shift_right, ALU.bitwise_and)
                    p.ss(v, v.tensor_copy(io["fcell"][0:1, 0:1], io["eps"][0:1, :]))
                    v.scalar_tensor_tensor(zsp[0:32, 0:2752], zint[0:32, 0:2752],
                                           -1.0, scp[0:32, 0:2752],
                                           ALU.mult, ALU.mult).then_inc(p.sem("zse"), 1)
                    p.inc("zse")

                @blk.sync
                def _(sy, pc=pc):
                    sy.wait_ge(p.sem("zse"), p.val("zse"))
                    dma(sy, p, "zss", zs_gu[:, pc * 2752:(pc + 1) * 2752],
                        zsp[0:32, 0:2752])
                    sy.wait_ge(p.sem("zss"), p.val("zss"))
        with nc.Block() as blk:
            @blk.sync
            def _(sy):
                dma(sy, p, "zsl", qzp[:, 0:512], qz_dn)
                dma(sy, p, "zsl", scp[:, 0:4096], sc_dn)
                sy.wait_ge(p.sem("zsl"), p.val("zsl"))

            @blk.vector
            def _(v):
                v.wait_ge(p.sem("zsl"), p.val("zsl"))
                for j in range(8):
                    v.tensor_scalar(zint[:, j::8], qzp[:, 0:512], 4 * j, 15,
                                    ALU.logical_shift_right, ALU.bitwise_and)
                p.ss(v, v.tensor_copy(io["fcell"][0:1, 0:1], io["eps"][0:1, :]))
                v.scalar_tensor_tensor(zsp[:], zint[:], -1.0, scp[:, 0:4096],
                                       ALU.mult, ALU.mult).then_inc(p.sem("zse"), 1)
                p.inc("zse")

            @blk.sync
            def _(sy):
                sy.wait_ge(p.sem("zse"), p.val("zse"))
                dma(sy, p, "zss", zs_dn, zsp[:])
                sy.wait_ge(p.sem("zss"), p.val("zss"))
    return zs_gu, zs_dn


def emit_l1_gu(p, io, xnT, sigT, qw_gu, sc_gu, h_dst, sgh_hbm, zs_gu):
    nc = p.nc
    INTER = 11008
    IC = INTER // 128
    with contextlib.ExitStack() as st:
        wpk = st.enter_context(nc.sbuf_tensor("gupk", [128, DC, 16], I32))
        wint = st.enter_context(nc.sbuf_tensor("guwi", [128, DC, 128], I32))
        wbf = st.enter_context(nc.sbuf_tensor("guwb", [128, DC, 128], BF16))
        sgub = st.enter_context(nc.sbuf_tensor("gusb", [128, DC, 128], BF16))
        zsl = st.enter_context(nc.sbuf_tensor("guzsl", [32, 128], BF16))
        gT = st.enter_context(nc.sbuf_tensor("gugt", [128, T], BF16))
        hT = st.enter_context(nc.sbuf_tensor("guht", [128, 512], BF16))
        sgrow = st.enter_context(nc.sbuf_tensor("gusgr", [1, T], F32))
        psq = [st.enter_context(nc.psum_tensor(f"gup{i}", [128, 512], F32))
               for i in range(4)]
        psn = st.enter_context(nc.psum_tensor("gusn", [1, 512], F32))
        trash = st.enter_context(nc.psum_tensor("gutr", [128, 512], F32))
        for i86 in range(IC):
            for half in range(2):
                ncol = half * INTER + i86 * 128
                with nc.Block() as blk:
                    @blk.gpsimd
                    def _(g, ncol=ncol):
                        load_colblock(g, p, "guwl", wpk, qw_gu, ncol // 8, 16, D)
                        dma(g, p, "guwl", sgub[:],
                            sc_gu[:, ncol:ncol + 128].unsqueeze(0)
                            .broadcast_to((128, DC, 128)))

                    @blk.sync
                    def _(sy, ncol=ncol):
                        dma(sy, p, "guzll", zsl[:], zs_gu[:, ncol:ncol + 128])

                    @blk.vector
                    def _(v):
                        v.wait_ge(p.sem("guwl"), p.val("guwl"))
                        for j in range(8):
                            v.tensor_scalar(wint[:, :, j::8], wpk[:], 4 * j, 15,
                                            ALU.logical_shift_right, ALU.bitwise_and)
                        p.ss(v, v.tensor_copy(io["fcell"][0:1, 0:1], io["eps"][0:1, :]))
                        v.tensor_mul(wbf[:], wint[:], sgub[:]).then_inc(p.sem("gudq"), 1)
                        p.inc("gudq")

                    b_gm = p.val("gumm")
                    b_ge = p.val("guev")
                    b_hs = p.val("guhs")
                    b_sg = p.val("gusg")

                    @blk.tensor
                    def _(t, half=half):
                        t.wait_ge(p.sem("gudq"), p.val("gudq"))
                        t.wait_ge(p.sem("guzll"), p.val("guzll"))
                        for ts in range(4):
                            for c in range(DC):
                                t.matmul(psq[ts][:], wbf[:, c, :],
                                         xnT[:, c * T + ts * 512: c * T + ts * 512 + 512],
                                         start=(c == 0), stop=False)
                            t.matmul(psq[ts][:], zsl[:],
                                     sigT[:, ts * 512:(ts + 1) * 512],
                                     start=False, stop=True)
                        pe_fence(p, t, io, trash, "gumm")
                        if half == 1:
                            for ts in range(4):
                                t.wait_ge(p.sem("guev"), b_ge + ts + 1)
                                t.matmul(psn[:], io["onescol"][:], hT[:],
                                         start=True, stop=True)
                                pe_fence(p, t, io, trash, "gusg")

                    if half == 0:
                        @blk.scalar
                        def _(sc):
                            sc.wait_ge(p.sem("gumm"), b_gm + 1)
                            for ts in range(4):
                                sc.activation(gT[:, ts * 512:(ts + 1) * 512],
                                              psq[ts][:], AF.Silu)
                    else:
                        @blk.vector
                        def _(v, i86=i86):
                            for ts in range(4):
                                v.wait_ge(p.sem("gumm"), b_gm + 1)
                                if ts > 0:
                                    v.wait_ge(p.sem("guhs"), b_hs + 16 * ts)
                                    v.wait_ge(p.sem("gusg"), b_sg + ts)
                                v.tensor_mul(hT[:], psq[ts][:],
                                             gT[:, ts * 512:(ts + 1) * 512]
                                             ).then_inc(p.sem("guev"), 1)
                                p.inc("guev")
                                v.wait_ge(p.sem("gusg"), b_sg + ts + 1)
                                v.tensor_copy(sgrow[:, ts * 512:(ts + 1) * 512],
                                              psn[:]).then_inc(p.sem("gusge"), 1)
                                p.inc("gusge")

                        @blk.sync
                        def _(sy, i86=i86):
                            for ts in range(4):
                                sy.wait_ge(p.sem("guev"), b_ge + ts + 1)
                                dma(sy, p, "guhs",
                                    h_dst[i86 * 128:(i86 + 1) * 128,
                                          ts * 512:(ts + 1) * 512], hT[:])
                            sy.wait_ge(p.sem("gusge"), p.val("gusge"))
                            dma(sy, p, "gusgs", sgh_hbm[i86:i86 + 1, :], sgrow[:])
                            sy.wait_ge(p.sem("guhs"), p.val("guhs"))
                            sy.wait_ge(p.sem("gusgs"), p.val("gusgs"))


def emit_l1_down(p, io, qw_dn, sc_dn, h_src, sghT, ffn_dst, zs_dn):
    nc = p.nc
    INTER = 11008
    IC = INTER // 128
    with contextlib.ExitStack() as st:
        wmov = st.enter_context(nc.sbuf_tensor("dnw", [128, IC, 512], BF16))
        cod = [st.enter_context(nc.sbuf_tensor(f"dnc{i}", [128, 64], I32))
               for i in range(2)]
        uint = [st.enter_context(nc.sbuf_tensor(f"dnu{i}", [128, 512], I32))
                for i in range(2)]
        sbc = [st.enter_context(nc.sbuf_tensor(f"dns{i}", [128, 512], F32))
               for i in range(2)]
        zdl = st.enter_context(nc.sbuf_tensor("dnzs", [IC, 512], BF16))
        zdlp = st.enter_context(nc.sbuf_tensor("dnzp", [IC, 512], BF16))
        acol = [st.enter_context(nc.sbuf_tensor(f"dna{i}", [128, IC, 128], BF16))
                for i in range(2)]
        evb = [st.enter_context(nc.sbuf_tensor(f"dne{i}", [128, 512], BF16))
               for i in range(2)]
        psq = [st.enter_context(nc.psum_tensor(f"dnp{i}", [128, 512], F32))
               for i in range(2)]
        trash = st.enter_context(nc.psum_tensor("dntr", [128, 512], F32))
        mm_bases = []
        for os_ in range(8):
            with nc.Block() as blk:
                b_cl = [p.val("dncl0"), p.val("dncl1")]
                b_dq = p.val("dndq")
                war_mm = p.val("dnmm") if os_ >= 1 else None

                @blk.gpsimd
                def _(g, os_=os_):
                    for c in range(IC):
                        g.wait_ge(p.sem("dndq"), max(0, b_dq + c - 1))
                        dma(g, p, "dncl" + str(c % 2), cod[c % 2][:],
                            qw_dn[c * 128:(c + 1) * 128, os_ * 64:(os_ + 1) * 64])
                        dma(g, p, "dncl" + str(c % 2), sbc[c % 2][:],
                            sc_dn[c:c + 1, os_ * 512:(os_ + 1) * 512]
                            .to_broadcast((128, 512)))

                @blk.sync
                def _(sy, os_=os_):
                    dma(sy, p, "dnzl", zdl[:], zs_dn[:, os_ * 512:(os_ + 1) * 512])

                @blk.vector
                def _(v, os_=os_, war_mm=war_mm):
                    if war_mm is not None:
                        v.wait_ge(p.sem("dnmm"), war_mm)
                    v.wait_ge(p.sem("dnzl"), p.val("dnzl"))
                    v.tensor_copy(zdlp[:].rearrange("p (j w) -> p j w", w=64),
                                  zdl[:].rearrange("p (w j) -> p j w", j=8)
                                  ).then_inc(p.sem("dnzp"), 1)
                    p.inc("dnzp")
                    for c in range(IC):
                        v.wait_ge(p.sem("dncl" + str(c % 2)),
                                  b_cl[c % 2] + 32 * (c // 2 + 1))
                        for j in range(8):
                            v.tensor_scalar(uint[c % 2][:, j * 64:(j + 1) * 64],
                                            cod[c % 2][:], 4 * j, 15,
                                            ALU.logical_shift_right, ALU.bitwise_and)
                        p.ss(v, v.tensor_copy(io["fcell"][0:1, 0:1], io["eps"][0:1, :]))
                        v.tensor_mul(wmov[:, c, :], uint[c % 2][:],
                                     sbc[c % 2][:]).then_inc(p.sem("dndq"), 1)
                        p.inc("dndq")

                b_mm = p.val("dnmm")
                b_ev = p.val("dnev")
                b_al = [p.val("dnal0"), p.val("dnal1")]
                b_sp = [p.val("dnsp0"), p.val("dnsp1")]
                dq_target = p.val("dndq")

                @blk.sync
                def _(sy, os_=os_):
                    for tt in range(NT):
                        if tt == 0:
                            load_colblock(sy, p, "dnal0", acol[0],
                                          h_src, 0, 128, INTER)
                            load_colblock(sy, p, "dnal1", acol[1],
                                          h_src, 128, 128, INTER)
                        if tt + 2 < NT:
                            sy.wait_ge(p.sem("dnmm"), b_mm + tt + 1)
                            load_colblock(sy, p, "dnal" + str(tt % 2), acol[tt % 2],
                                          h_src, (tt + 2) * 128, 128, INTER)
                        sy.wait_ge(p.sem("dnev"), b_ev + tt + 1)
                        dma(sy, p, "dnsp" + str(tt % 2),
                            ffn_dst[tt * 128:(tt + 1) * 128,
                                    os_ * 512:(os_ + 1) * 512],
                            evb[tt % 2][:])
                    sy.wait_ge(p.sem("dnsp0"), p.val("dnsp0"))
                    sy.wait_ge(p.sem("dnsp1"), p.val("dnsp1"))

                nload = 11 * 16

                @blk.tensor
                def _(t, os_=os_):
                    t.wait_ge(p.sem("dndq"), dq_target)
                    t.wait_ge(p.sem("dnzp"), p.val("dnzp"))
                    for tt in range(NT):
                        t.wait_ge(p.sem("dnal" + str(tt % 2)),
                                  b_al[tt % 2] + nload * (tt // 2 + 1))
                        if tt >= 2 or os_ > 0:
                            t.wait_ge(p.sem("dnev"), b_ev + tt - 1)
                        ps = psq[tt % 2]
                        for c in range(IC):
                            t.matmul(ps[:], acol[tt % 2][:, c, :],
                                     wmov[:, c, :], start=(c == 0), stop=False)
                        t.matmul(ps[:], sghT[:, tt * 128:(tt + 1) * 128],
                                 zdlp[:], start=False, stop=True)
                        pe_fence(p, t, io, trash, "dnmm")

                @blk.scalar
                def _(sc, os_=os_):
                    for tt in range(NT):
                        sc.wait_ge(p.sem("dnmm"), b_mm + tt + 1)
                        if tt >= 2 or os_ > 0:
                            sc.wait_ge(p.sem("dnsp" + str(tt % 2)),
                                       b_sp[tt % 2] + 16 * (tt // 2))
                        sc.activation(evb[tt % 2][:].rearrange(
                                          "p (w j) -> p w j", j=8),
                                      psq[tt % 2][:].rearrange(
                                          "p (j w) -> p w j", w=64),
                                      AF.Copy).then_inc(p.sem("dnev"), 1)
                        p.inc("dnev")


def build_l1(debug=(), upto="full", fuse=None):
    if fuse is None:
        nc = bass.Bass("TRN2", target_bir_lowering=False)
        p = Prog(nc)
    else:
        nc = NcPfx(fuse["nc"], "zB")
        p = fuse["p"]
        p.nc = nc

    NKV = 8
    DKV = 1024
    INTER = 11008
    IC = INTER // 128

    if fuse is None:
        x1 = nc.dram_tensor("x1", [T, D], F32, kind="ExternalInput").ap()
    else:
        x1 = fuse["x1"]
    q_codes = nc.dram_tensor("q_codes", [D, D], I32, kind="ExternalInput").ap()
    q_scales = nc.dram_tensor("q_scales", [32, D], F32, kind="ExternalInput").ap()
    k_codes = nc.dram_tensor("k_codes", [D, DKV], I32, kind="ExternalInput").ap()
    k_scales = nc.dram_tensor("k_scales", [32, DKV], F32, kind="ExternalInput").ap()
    v_codes = nc.dram_tensor("v_codes", [D, DKV], I32, kind="ExternalInput").ap()
    v_scales = nc.dram_tensor("v_scales", [32, DKV], F32, kind="ExternalInput").ap()
    o_codes = nc.dram_tensor("o_codes", [D, D], I32, kind="ExternalInput").ap()
    o_scales = nc.dram_tensor("o_scales", [32, D], F32, kind="ExternalInput").ap()
    qw_gu = nc.dram_tensor("qw_gu", [D, 2 * INTER // 8], I32, kind="ExternalInput").ap()
    qz_gu = nc.dram_tensor("qz_gu", [32, 2 * INTER // 8], I32, kind="ExternalInput").ap()
    sc_gu = nc.dram_tensor("sc_gu", [32, 2 * INTER], F32, kind="ExternalInput").ap()
    qw_dn = nc.dram_tensor("qw_dn", [INTER, D // 8], I32, kind="ExternalInput").ap()
    qz_dn = nc.dram_tensor("qz_dn", [IC, D // 8], I32, kind="ExternalInput").ap()
    sc_dn = nc.dram_tensor("sc_dn", [IC, D], F32, kind="ExternalInput").ap()
    sc_dnp = nc.dram_tensor("sc_dnp", [IC, D], F32, kind="ExternalInput").ap()
    masterT = nc.dram_tensor("masterT", [128, MST_W], BF16, kind="ExternalInput").ap()
    rope_cos = nc.dram_tensor("rope_cos", [T, 64], F32, kind="ExternalInput").ap()
    rope_sin = nc.dram_tensor("rope_sin", [T, 64], F32, kind="ExternalInput").ap()

    out = nc.dram_tensor("out", [T, D], F32, kind="ExternalOutput").ap()
    dbg = {}
    for name, shp, dt_ in (("xnT", [128, DC * T], BF16), ("qt", [D, T], BF16),
                           ("kt", [DKV, T], BF16), ("v", [T, DKV], BF16),
                           ("attnT", [D, T], BF16), ("ffn", [T, D], BF16),
                           ("atno", [T, D], BF16), ("h", [INTER, T], BF16),
                           ("sgo", [32, T], BF16)):
        if name in debug:
            dbg[name] = nc.dram_tensor("dbg_" + name, shp, dt_,
                                       kind="ExternalOutput").ap()

    qt_s = nc.dram_tensor("qt_s", [D, T], BF16, kind="Internal").ap()
    ag_k_in = nc.dram_tensor("ag_k_in", [DKV, T], BF16, kind="Internal")
    ag_k_out = nc.dram_tensor("ag_k_out", [2 * DKV, T], BF16, kind="Internal")
    ag_vi = nc.dram_tensor("ag_vi", [T, DKV], BF16, kind="Internal")
    ag_vo = nc.dram_tensor("ag_vo", [S, DKV], BF16, kind="Internal")
    attn_s = nc.dram_tensor("attn_s", [D, T], BF16, kind="Internal").ap()
    ffn_s = nc.dram_tensor("ffn_s", [T, D], BF16, kind="Internal").ap()
    atno_s = nc.dram_tensor("atno_s", [T, D], BF16, kind="Internal").ap()
    h_s = nc.dram_tensor("h_s", [INTER, T], BF16, kind="Internal").ap()

    gctx = contextlib.ExitStack()
    if fuse is None:
        io = _mk_io(nc, gctx)
    else:
        io = fuse["io"]
    sigT = gctx.enter_context(nc.sbuf_tensor("sigT", [32, T], BF16))
    sgo_hbm = nc.dram_tensor("sgo_hbm", [32, T], F32, kind="Internal").ap()
    sgh_hbm = nc.dram_tensor("sgh_hbm", [IC, T], F32, kind="Internal").ap()

    with gctx:
        if fuse is None:
            emit_init_consts(p, io)
        zs_gu_hbm, zs_dn_hbm = emit_l1_zs(p, io, qz_gu, sc_gu, qz_dn, sc_dn)

        def make_s8(st_, name, src_sc, width):
            dst = st_.enter_context(nc.sbuf_tensor(name, [32, width], BF16))
            with contextlib.ExitStack() as st2:
                tmp = st2.enter_context(nc.sbuf_tensor(name + "t", [32, width], F32))
                with nc.Block() as blk:
                    @blk.sync
                    def _(sy):
                        dma(sy, p, "s8l", tmp[:], src_sc)
                        sy.wait_ge(p.sem("s8l"), p.val("s8l"))
                with nc.Block() as blk:
                    @blk.vector
                    def _(v):
                        v.tensor_scalar(dst[:], tmp[:], -8.0, None, ALU.mult)
            return dst

        r1 = contextlib.ExitStack()
        xnT = r1.enter_context(nc.sbuf_tensor("xnT", [128, DC * T], BF16))
        sgtok = r1.enter_context(nc.sbuf_tensor("sgtok", [128, NT * 32], F32))
        with r1:
            def sig_cb(v, tt, xnb):
                v.tensor_reduce(sgtok[:, tt * 32:(tt + 1) * 32],
                                xnb[:].rearrange("p (c f) -> p c f", f=128),
                                axis=AX.X, op=ALU.add)

            emit_ln_xnt(p, io, x1, xnT, "ln", sig_cb=sig_cb)
            with contextlib.ExitStack() as st:
                sgbf = st.enter_context(nc.sbuf_tensor("sgbf", [128, NT * 32], BF16))
                pstr = st.enter_context(nc.psum_tensor("sgps", [32, 128], BF16))
                trs = st.enter_context(nc.psum_tensor("sgtr", [128, 512], F32))
                with nc.Block() as blk:
                    @blk.vector
                    def _(v):
                        v.tensor_copy(sgbf[:], sgtok[:])
                for tt in range(NT):
                    with nc.Block() as blk:
                        @blk.tensor
                        def _(t):
                            t.transpose(pstr[:], sgbf[:, tt * 32:(tt + 1) * 32],
                                        io["ident"][:])
                            pe_fence(p, t, io, trs, "sgtr")

                        @blk.vector
                        def _(v):
                            v.wait_ge(p.sem("sgtr"), p.val("sgtr"))
                            v.tensor_copy(sigT[:, tt * 128:(tt + 1) * 128], pstr[:])
            if "xnT" in dbg:
                _copy_hbm(p, dbg["xnT"], xnT[:], sbuf=True)
            if upto == "ln":
                return nc

            s8q = make_s8(r1, "s8q", q_scales, D)
            s8k = make_s8(r1, "s8k", k_scales, DKV)
            s8v = make_s8(r1, "s8v", v_scales, DKV)
            which = "qkv"
            if upto.startswith("m_"):
                which = upto[2:]
            emit_l1_qkv(p, io, xnT, sigT, q_codes, q_scales, k_codes, k_scales,
                        v_codes, v_scales, s8q, s8k, s8v, rope_cos, rope_sin,
                        qt_s, ag_k_in.ap(), ag_vi.ap(), which=which)
            if upto.startswith("m_"):
                if "qt" in dbg and "q" in which:
                    _copy_hbm(p, dbg["qt"], qt_s)
                if "kt" in dbg and "k" in which:
                    _copy_hbm(p, dbg["kt"], ag_k_in.ap())
                if "v" in dbg and "v" in which:
                    _copy_hbm(p, dbg["v"], ag_vi.ap())
                return nc
            if upto == "qonly":
                if "qt" in dbg:
                    _copy_hbm(p, dbg["qt"], qt_s)
                if "kt" in dbg:
                    _copy_hbm(p, dbg["kt"], ag_k_in.ap())
                if "v" in dbg:
                    _copy_hbm(p, dbg["v"], ag_vi.ap())
                return nc
            with nc.Block() as blk:
                @blk.gpsimd
                def _(g):
                    g.collective_compute(
                        "AllGather", ALU.bypass, replica_groups=REPLICA_PAIRS,
                        ins=[ag_k_in[:]], outs=[ag_k_out[:]],
                    ).then_inc(p.sem("agk"), 1)
                    p.inc("agk")
                    g.collective_compute(
                        "AllGather", ALU.bypass, replica_groups=REPLICA_PAIRS,
                        ins=[ag_vi[:]], outs=[ag_vo[:]],
                    ).then_inc(p.sem("agv"), 1)
                    p.inc("agv")
            if "qt" in dbg:
                _copy_hbm(p, dbg["qt"], qt_s)
            if upto == "qkv":
                with nc.Block() as blk:
                    @blk.gpsimd
                    def _(g):
                        g.wait_ge(p.sem("agk"), p.val("agk"))
                        g.wait_ge(p.sem("agv"), p.val("agv"))
                if "kt" in dbg:
                    _copy_hbm(p, dbg["kt"], ag_k_out.ap()[0:DKV, :])
                if "v" in dbg:
                    _copy_hbm(p, dbg["v"], ag_vo.ap()[0:T, :])
                return nc

            if upto != "attn1":
                emit_l1_gu(p, io, xnT, sigT, qw_gu, sc_gu, h_s, sgh_hbm, zs_gu_hbm)
        if "h" in dbg:
            _copy_hbm(p, dbg["h"], h_s)
        if upto == "gu":
            return nc

        if upto == "attn1":
            class _Skip: pass
        with contextlib.ExitStack() as rdn:
            sghT = rdn.enter_context(nc.sbuf_tensor("sghT", [IC, T], BF16))
            if upto != "attn1":
                with nc.Block() as blk:
                    @blk.gpsimd
                    def _(g):
                        dma(g, p, "sghl", sghT[:], sgh_hbm)
                        g.wait_ge(p.sem("sghl"), p.val("sghl"))
                emit_l1_down(p, io, qw_dn, sc_dnp, h_s, sghT, ffn_s, zs_dn_hbm)
        if "ffn" in dbg:
            _copy_hbm(p, dbg["ffn"], ffn_s)
        if upto == "ffn":
            return nc

        with contextlib.ExitStack() as r2:
            mst_sb = r2.enter_context(nc.sbuf_tensor("mst", [128, MST_W], BF16))
            with nc.Block() as blk:
                @blk.sync
                def _(sy):
                    dma(sy, p, "mstl", mst_sb[:], masterT)
                    sy.wait_ge(p.sem("mstl"), p.val("mstl"))

                @blk.gpsimd
                def _(g):
                    g.wait_ge(p.sem("agk"), p.val("agk"))
                    g.wait_ge(p.sem("agv"), p.val("agv"))
            if "kt" in dbg:
                _copy_hbm(p, dbg["kt"], ag_k_out.ap()[0:DKV, :])
            if "v" in dbg:
                _copy_hbm(p, dbg["v"], ag_vo.ap()[0:T, :])
            cfg = dict(n_qh=32, kv_per=4, dkv=DKV, scale=SCALE1, alibi=False,
                       tag="a1")
            emit_attention(p, io, cfg, qt_s, ag_k_out.ap(), ag_vo.ap(),
                           attn_s, mst_sb, sgoT=sgo_hbm)
        if "attnT" in dbg:
            _copy_hbm(p, dbg["attnT"], attn_s)
        if upto in ("attn", "attn1"):
            return nc

        with contextlib.ExitStack() as r3:
            s8o = make_s8(r3, "s8o", o_scales, D)
            sgoT = r3.enter_context(nc.sbuf_tensor("sgoT", [32, T], BF16))
            with nc.Block() as blk:
                @blk.gpsimd
                def _(g):
                    dma(g, p, "sgol", sgoT[:], sgo_hbm)
                    g.wait_ge(p.sem("sgol"), p.val("sgol"))
            codq = [r3.enter_context(nc.sbuf_tensor(f"oqc{i}", [128, 512], I32))
                    for i in range(2)]
            sbcq = [r3.enter_context(nc.sbuf_tensor(f"oqs{i}", [128, 512], F32))
                    for i in range(2)]

            def wo_load(blk, os_, buf):
                emit_dq_cs_strip(p, blk, io, o_codes, o_scales, os_ * 512, 512,
                                 buf, "podq", (codq, sbcq), None)
                tgt = p.val("podqdq")

                @blk.gpsimd
                def _(g):
                    g.wait_ge(p.sem("podqdq"), tgt)
                    g.sem_inc(p.sem("powrdy"), 1)
                    p.inc("powrdy")

            emit_oproj(p, io, attn_s, atno_s, "po", wo_load,
                       corr=(sgoT, s8o))
        if "atno" in dbg:
            _copy_hbm(p, dbg["atno"], atno_s)

        emit_epilogue(p, x1, atno_s, ffn_s, out, "ep")
    return nc


def load_colblock(eng, p, sname, dst3, src_hbm, col0, ncols, nrows, max_c=8):
    nch = nrows // 128
    src = src_hbm[:, col0:col0 + ncols].rearrange("(c p) f -> p c f", p=128)
    c0 = 0
    while c0 < nch:
        c1 = min(c0 + max_c, nch)
        dma(eng, p, sname, dst3[:, c0:c1, :], src[:, c0:c1, :])
        c0 = c1


def emit_l1_qkv(p, io, xnT, sigT, q_codes, q_scales, k_codes, k_scales,
                v_codes, v_scales, s8q, s8k, s8v, rope_cos, rope_sin,
                qt_dst, agk_dst, agv_dst, which="qkv"):
    nc = p.nc
    with contextlib.ExitStack() as st:
        wmov = st.enter_context(nc.sbuf_tensor("qkw", [128, DC, 512], BF16))
        cod = [st.enter_context(nc.sbuf_tensor(f"qkc{i}", [128, 512], I32))
               for i in range(2)]
        sbc = [st.enter_context(nc.sbuf_tensor(f"qks{i}", [128, 512], F32))
               for i in range(2)]
        cosb = [st.enter_context(nc.sbuf_tensor(f"qkcos{i}", [128, 256], F32))
                for i in range(2)]
        sinb = [st.enter_context(nc.sbuf_tensor(f"qksin{i}", [128, 256], F32))
                for i in range(2)]
        rt1 = st.enter_context(nc.sbuf_tensor("qkrt1", [128, 256], F32))
        rt2 = st.enter_context(nc.sbuf_tensor("qkrt2", [128, 256], F32))
        evb = [st.enter_context(nc.sbuf_tensor(f"qke{i}", [128, 512], BF16))
               for i in range(2)]
        stg = [st.enter_context(nc.sbuf_tensor(f"qkg{i}", [128, 512], BF16))
               for i in range(2)]
        psq = [st.enter_context(nc.psum_tensor(f"qkp{i}", [128, 512], F32))
               for i in range(2)]
        tps = [st.enter_context(nc.psum_tensor(f"qkt{i}", [128, 128], BF16))
               for i in range(2)]
        trash = st.enter_context(nc.psum_tensor("qktr", [128, 512], F32))

        mats = [("q", q_codes, q_scales, s8q, 8, "rope_t"),
                ("k", k_codes, k_scales, s8k, 2, "rope_t"),
                ("v", v_codes, v_scales, s8v, 2, "plain")]
        if which == "K":
            mats = [("v", k_codes, k_scales, s8k, 2, "plain")]
        elif which == "R":
            mats = [("v", k_codes, k_scales, s8k, 2, "rope_p")]
        else:
            mats = [m for m in mats if m[0] in which]
        for mname, codes, scales, s8x, nfs, mode in mats:
            for fs in range(nfs):
                with nc.Block() as blk:
                    emit_dq_cs_strip(p, blk, io, codes, scales, fs * 512, 512,
                                     wmov, "xdq", (cod, sbc), None)
                    dq_tgt = p.val("xdqdq")
                    b_mm = p.val("qmm")
                    b_rp = p.val("qrp")
                    b_tn = p.val("qtn")
                    b_cp = p.val("qcp")
                    b_cs = [p.val("qcs0"), p.val("qcs1")]
                    b_sp = [p.val("qsp0"), p.val("qsp1")]

                    @blk.gpsimd
                    def _(g, mode=mode):
                        if mode.startswith("rope"):
                            for tt in range(NT):
                                if tt >= 2:
                                    g.wait_ge(p.sem("qrp"), b_rp + tt - 1)
                                csrc = rope_cos[tt * 128:(tt + 1) * 128, :]
                                ssrc = rope_sin[tt * 128:(tt + 1) * 128, :]
                                dma(g, p, "qcs" + str(tt % 2),
                                    cosb[tt % 2][:].rearrange("p (r f) -> p r f", f=64),
                                    csrc.unsqueeze(1).broadcast_to((128, 4, 64)))
                                dma(g, p, "qcs" + str(tt % 2),
                                    sinb[tt % 2][:].rearrange("p (r f) -> p r f", f=64),
                                    ssrc.unsqueeze(1).broadcast_to((128, 4, 64)))

                    def emit_trans(t, tt):
                        for j in range(4):
                            n = 4 * tt + j
                            if n >= 2:
                                t.wait_ge(p.sem("qcp"), b_cp + n - 1)
                            t.transpose(tps[j % 2][:],
                                        evb[tt % 2][:, j * 128:(j + 1) * 128],
                                        io["ident"][:]).then_inc(p.sem("qtn"), 1)
                            p.inc("qtn")
                        pe_fence(p, t, io, trash, "qtn")

                    @blk.tensor
                    def _(t, s8x=s8x, fs=fs, mode=mode):
                        t.wait_ge(p.sem("xdqdq"), dq_tgt)
                        for tt in range(NT):
                            if tt >= 2:
                                t.wait_ge(p.sem("qrp"), b_rp + tt - 1)
                            ps = psq[tt % 2]
                            for c in range(DC):
                                t.matmul(ps[:], xnT[:, c * T + tt * 128: c * T + tt * 128 + 128],
                                         wmov[:, c, :], start=(c == 0), stop=False)
                            t.matmul(ps[:], sigT[:, tt * 128:(tt + 1) * 128],
                                     s8x[:, fs * 512:(fs + 1) * 512],
                                     start=False, stop=True)
                            pe_fence(p, t, io, trash, "qmm")
                            if mode == "rope_t" and tt >= 1:
                                t.wait_ge(p.sem("qrp"), b_rp + tt)
                                emit_trans(t, tt - 1)
                        if mode == "rope_t":
                            t.wait_ge(p.sem("qrp"), b_rp + NT)
                            emit_trans(t, NT - 1)

                    def emit_copies(v, tt):
                        for j in range(4):
                            v.wait_ge(p.sem("qtn"), b_tn + 5 * tt + j + 2)
                            sl = (tt * 4 + j) % 2
                            if tt * 4 + j >= 2:
                                v.wait_ge(p.sem("qsp" + str(sl)),
                                          b_sp[sl] + 16 * ((tt * 4 + j) // 2))
                            v.tensor_copy(stg[sl][:, 0:128],
                                          tps[j % 2][:]).then_inc(p.sem("qcp"), 1)
                            p.inc("qcp")

                    @blk.vector
                    def _(v, mode=mode):
                        for tt in range(NT):
                            v.wait_ge(p.sem("qmm"), b_mm + tt + 1)
                            ps = psq[tt % 2]
                            eb = evb[tt % 2]
                            if mode.startswith("rope"):
                                v.wait_ge(p.sem("qcs" + str(tt % 2)),
                                          b_cs[tt % 2] + 32 * (tt // 2 + 1))
                                if mode == "rope_t" and tt >= 2:
                                    v.wait_ge(p.sem("qtn"), b_tn + 5 * (tt - 1))
                                if mode == "rope_p" and tt >= 2:
                                    v.wait_ge(p.sem("qsp" + str(tt % 2)),
                                              b_sp[tt % 2] + 16 * (tt // 2))
                                v.wait_ge(p.sem("qrp"), p.val("qrp"))
                                pse = ps[:].rearrange("p (a two) -> p two a", two=2)[:, 0, :]
                                pso = ps[:].rearrange("p (a two) -> p two a", two=2)[:, 1, :]
                                p.ss(v, v.tensor_mul(rt1[:], pse, cosb[tt % 2][:]))
                                p.ss(v, v.tensor_mul(rt2[:], pso, sinb[tt % 2][:]))
                                ebe = eb[:].rearrange("p (a two) -> p two a", two=2)[:, 0, :]
                                ebo = eb[:].rearrange("p (a two) -> p two a", two=2)[:, 1, :]
                                p.ss(v, v.tensor_sub(ebe, rt1[:], rt2[:]))
                                p.ss(v, v.tensor_mul(rt1[:], pso, cosb[tt % 2][:]))
                                p.ss(v, v.tensor_mul(rt2[:], pse, sinb[tt % 2][:]))
                                ins = v.tensor_add(ebo, rt1[:], rt2[:])
                                ins.then_inc(p.sem("qrp"), 1)
                                p.inc("qrp")
                                if mode == "rope_t" and tt >= 1:
                                    emit_copies(v, tt - 1)
                            else:
                                if tt >= 2:
                                    v.wait_ge(p.sem("qsp" + str(tt % 2)),
                                              b_sp[tt % 2] + 16 * (tt // 2))
                                v.tensor_copy(eb[:], ps[:]).then_inc(p.sem("qrp"), 1)
                                p.inc("qrp")
                        if mode == "rope_t":
                            emit_copies(v, NT - 1)

                    if mode == "rope_t":
                        @blk.sync
                        def _(sy, mname=mname, fs=fs):
                            dst = qt_dst if mname == "q" else agk_dst
                            for tt in range(NT):
                                for j in range(4):
                                    sy.wait_ge(p.sem("qcp"), b_cp + 4 * tt + j + 1)
                                    sl = (tt * 4 + j) % 2
                                    r0 = fs * 512 + j * 128
                                    dma(sy, p, "qsp" + str(sl),
                                        dst[r0:r0 + 128, tt * 128:(tt + 1) * 128],
                                        stg[sl][:, 0:128])
                            sy.wait_ge(p.sem("qsp0"), p.val("qsp0"))
                            sy.wait_ge(p.sem("qsp1"), p.val("qsp1"))
                    if mode != "rope_t":
                        @blk.sync
                        def _(sy, fs=fs):
                            for tt in range(NT):
                                sy.wait_ge(p.sem("qrp"), b_rp + tt + 1)
                                dma(sy, p, "qsp" + str(tt % 2),
                                    agv_dst[tt * 128:(tt + 1) * 128,
                                            fs * 512:(fs + 1) * 512],
                                    evb[tt % 2][:])
                            sy.wait_ge(p.sem("qsp0"), p.val("qsp0"))
                            sy.wait_ge(p.sem("qsp1"), p.val("qsp1"))

